# version 14
# baseline (speedup 1.0000x reference)
import math

import numpy as np
import ml_dtypes
import concourse.bass as bass
import concourse.mybir as mybir
from concourse.bass_utils import run_bass_kernel_spmd

F32 = mybir.dt.float32
BF16 = mybir.dt.bfloat16
I8 = mybir.dt.int8
U8 = mybir.dt.uint8
AF = mybir.ActivationFunctionType
ALU = mybir.AluOpType
AX = mybir.AxisListType.X

T = 2048
D = 1024
H = 16
HD = 64
NT = 16
NCH = 8
NQC = 4
MAGIC = 12582912.0
LN_EPS = 1e-5
RMS_EPS = 1e-6
QEPS = 1e-5
LN_0125 = math.log(0.125)

ENGS = ("sync", "scalar", "vector", "tensor", "gpsimd")

_SIM_LOCAL_CC = False
_TRUNC = 0


class _Sem:
    __slots__ = ("h", "n")

    def __init__(self, h):
        self.h = h
        self.n = 0


def _build_program():
    nc = bass.Bass("TRN2", target_bir_lowering=False)

    d_xin = nc.dram_tensor("xin", [T, D], F32, kind="ExternalInput").ap()
    d_xt = nc.dram_tensor("xtin", [D, T], F32, kind="ExternalInput").ap()
    d_wq = nc.dram_tensor("wq", [D, D], F32, kind="ExternalInput").ap()
    d_wk = nc.dram_tensor("wk", [D, D], F32, kind="ExternalInput").ap()
    d_wv = nc.dram_tensor("wv", [D, D], F32, kind="ExternalInput").ap()
    d_wo = nc.dram_tensor("wo", [D, D], F32, kind="ExternalInput").ap()
    d_wfft = nc.dram_tensor("wfft", [D, D], I8, kind="ExternalInput").ap()
    d_wsrow = nc.dram_tensor("wsrow", [1, D], F32, kind="ExternalInput").ap()
    d_brow = nc.dram_tensor("brow", [1, D], F32, kind="ExternalInput").ap()
    d_wqrow = nc.dram_tensor("wqrow", [1, HD], F32, kind="ExternalInput").ap()
    d_wkrow = nc.dram_tensor("wkrow", [1, HD], F32, kind="ExternalInput").ap()
    d_cosq = nc.dram_tensor("cosq", [T, 32], F32, kind="ExternalInput").ap()
    d_sinq = nc.dram_tensor("sinq", [T, 32], F32, kind="ExternalInput").ap()
    d_cosk = nc.dram_tensor("cosk", [T, 32], F32, kind="ExternalInput").ap()
    d_sink = nc.dram_tensor("sink", [T, 32], F32, kind="ExternalInput").ap()
    d_trimask = nc.dram_tensor("trimask", [128, 128], BF16, kind="ExternalInput").ap()
    d_ident = nc.dram_tensor("identb", [128, 128], BF16, kind="ExternalInput").ap()
    d_y = nc.dram_tensor("y", [T, D], F32, kind="ExternalOutput").ap()

    d_ccin = nc.dram_tensor("ccin", [1, 1], F32)
    d_ccout = nc.dram_tensor("ccout", [1, 1], F32, addr_space="Shared")

    from contextlib import ExitStack
    es = ExitStack()
    S = es.enter_context

    arena = S(nc.sbuf_tensor("arena", [128, 65536], U8))
    QT0 = 32768
    s_kt = S(nc.sbuf_tensor("s_kt", [128, 16 * 1024], BF16))
    s_v = S(nc.sbuf_tensor("s_v", [128, NT * (H * 65)], BF16))
    s_w0 = S(nc.sbuf_tensor("s_w0", [128, NCH * 1024], BF16))
    s_w1 = S(nc.sbuf_tensor("s_w1", [128, NCH * 1024], BF16))
    s_pt = S(nc.sbuf_tensor("s_pt", [128, 2048], BF16))
    s_an = S(nc.sbuf_tensor("s_an", [128, 4096], BF16))
    s_at = S(nc.sbuf_tensor("s_at", [128, 4096], BF16))
    s_xti = S(nc.sbuf_tensor("s_xti", [128, 1024], F32))
    s_xqt = S(nc.sbuf_tensor("s_xqt", [128, 1024], BF16))
    s_rq = S(nc.sbuf_tensor("s_rq", [128, 1024], BF16))
    s_scr = S(nc.sbuf_tensor("s_scr", [128, 1024], F32))
    s_tmp = S(nc.sbuf_tensor("s_tmp", [128, 2048], BF16))
    s_tbl = S(nc.sbuf_tensor("s_tbl", [128, 4096], BF16))
    s_tri = S(nc.sbuf_tensor("s_tri", [128, 128], BF16))
    s_id = S(nc.sbuf_tensor("s_id", [128, 128], BF16))
    s_ms = S(nc.sbuf_tensor("s_ms", [128, 16], F32))
    s_rstd = S(nc.sbuf_tensor("s_rstd", [128, 16], F32))
    s_rstd2 = S(nc.sbuf_tensor("s_rstd2", [128, 16], F32))
    s_wqb = S(nc.sbuf_tensor("s_wqb", [128, 64], F32))
    s_wkb = S(nc.sbuf_tensor("s_wkb", [128, 64], F32))
    s_den = S(nc.sbuf_tensor("s_den", [128, 4], F32))
    s_rec = S(nc.sbuf_tensor("s_rec", [128, 4], F32))
    s_amax = S(nc.sbuf_tensor("s_amax", [128, 16], F32))
    s_am128 = S(nc.sbuf_tensor("s_am128", [1, 128], F32))
    s_sml = S(nc.sbuf_tensor("s_sml", [128, 16], F32))
    s_eps1 = S(nc.sbuf_tensor("s_eps1", [128, 1], F32))
    s_eps2 = S(nc.sbuf_tensor("s_eps2", [128, 1], F32))
    s_ones = S(nc.sbuf_tensor("s_ones", [1, 128], F32))
    s_ln8 = S(nc.sbuf_tensor("s_ln8", [1, 1], F32))
    s_mh = S(nc.sbuf_tensor("s_mh", [1, 4], F32))
    s_nmh = S(nc.sbuf_tensor("s_nmh", [128, 1], F32))
    s_xs = S(nc.sbuf_tensor("s_xs", [1, 2], F32))
    s_xsb = S(nc.sbuf_tensor("s_xsb", [128, 2], F32))

    p_a = S(nc.psum_tensor("p_a", [128, 1024], F32))
    p_b = S(nc.psum_tensor("p_b", [128, 1024], F32))
    p_c = S(nc.psum_tensor("p_c", [128, 1024], F32))
    p_t1 = S(nc.psum_tensor("p_t1", [128, 1024], BF16))
    p_t2 = S(nc.psum_tensor("p_t2", [128, 1024], BF16))

    sems = {}
    for name in ("lw0", "lw1", "lwx", "ldx", "ldg", "ldg2", "pe", "act",
                 "dve", "pool", "stE", "stO", "scc", "cc", "xld"):
        sems[name] = _Sem(S(nc.semaphore(name)))
    SW0, SW1, SWX, SX, SG, SG2, SP, SA, SV, SL, SSE, SSO, SCC, SC, SXL = (
        sems[n] for n in ("lw0", "lw1", "lwx", "ldx", "ldg", "ldg2", "pe",
                          "act", "dve", "pool", "stE", "stO", "scc", "cc",
                          "xld"))

    Q = {e: [] for e in ENGS}

    def on(eng, fn, waits=(), inc=None, n=1):
        tok = None
        rec_inc = None
        if inc is not None:
            inc.n += n
            tok = (inc, inc.n)
            rec_inc = (inc.h, n)
        Q[eng].append((tuple(w for w in waits if w is not None), fn, rec_inc))
        return tok

    def drain(eng):
        on(eng, lambda e: e.drain())

    def xt_blk(ch, t):
        base = ch * 4096 + t * 256
        return arena[:, base:base + 256].bitcast(BF16)

    def xt_full():
        return arena[:, 0:32768].bitcast(BF16)

    def qt_win(qc, ch):
        off = QT0 + qc * 8192 + ch * 1024
        return arena[:, off:off + 1024].bitcast(BF16)

    def qt_blk(qc, ch, tl):
        off = QT0 + qc * 8192 + ch * 1024 + tl * 256
        return arena[:, off:off + 256].bitcast(BF16)

    def kt_blk(j, ch):
        return s_kt[:, j * 1024 + ch * 128: j * 1024 + (ch + 1) * 128]

    def x2_tile(t):
        if t < 8:
            return arena[:, t * 4096:(t + 1) * 4096].bitcast(F32)
        return arena[:, QT0 + (t - 8) * 4096: QT0 + (t - 7) * 4096].bitcast(F32)

    def tmp_f32():
        return s_tmp[:, :].bitcast(F32)

    sml = lambda i: s_sml[:, i:i + 1]

    on("vector", lambda e: e.memset(s_eps1[:], RMS_EPS))
    on("vector", lambda e: e.memset(s_eps2[:], LN_EPS))
    on("vector", lambda e: e.memset(s_ones[:], 1.0))
    on("vector", lambda e: e.memset(s_ln8[:], LN_0125))
    v3 = s_v[:, :].rearrange("p (g j) -> p g j", j=65)
    tok_vones = on("vector", lambda e: e.memset(v3[:, :, 64:65], 1.0), inc=SV)

    on("sync", lambda e: e.dma_start(out=s_tri[:], in_=d_trimask), inc=SX, n=16)
    on("sync", lambda e: e.dma_start(out=s_id[:], in_=d_ident), inc=SX, n=16)
    stg = s_an[:, :].bitcast(F32)
    stg_v = [stg[:, i * 512:(i + 1) * 512] for i in range(4)]
    for i, dt_ in enumerate((d_cosq, d_sinq, d_cosk, d_sink)):
        src = dt_.rearrange("(n p) j -> p n j", p=128)
        dst = stg_v[i].rearrange("p (n j) -> p n j", j=32)
        on("sync", lambda e, s=src, d=dst: e.dma_start(out=d, in_=s),
           inc=SX, n=16)
    tok_tri = tok_id = (SX, SX.n)
    cs_toks = [(SX, SX.n)] * 4
    on("gpsimd", lambda e: e.dma_start(out=s_wqb[:], in_=d_wqrow.broadcast_to([128, HD])), inc=SG, n=16)
    on("gpsimd", lambda e: e.dma_start(out=s_wkb[:], in_=d_wkrow.broadcast_to([128, HD])), inc=SG, n=16)
    tok_wqb = tok_wkb = (SG, SG.n)
    def wload(dst_sb, src, sem, waits=()):
        src_r = src.rearrange("(c p) n -> p c n", p=128)
        dst_r = dst_sb[:, :].rearrange("p (c n) -> p c n", n=1024)
        return on("gpsimd", lambda e: e.dma_start(out=dst_r, in_=src_r),
                  waits=waits, inc=sem, n=16)
    tok_wq = wload(s_w0, d_wq, SW0)
    tok_wk = wload(s_w1, d_wk, SW1)
    xt_src = d_xt.rearrange("(c p) t -> p c t", p=128)
    xt_dst = xt_full().rearrange("p (c t) -> p c t", t=2048)
    tok_xt = on("gpsimd", lambda e: e.dma_start(out=xt_dst, in_=xt_src), inc=SWX, n=16)

    tbl = [s_tbl[:, i * 1024:(i + 1) * 1024] for i in range(4)]
    tok_tbl = None
    for m, (ci, si_, wb, wtok) in enumerate(((0, 1, s_wqb, tok_wqb), (2, 3, s_wkb, tok_wkb))):
        for tix, fi in ((2 * m, ci), (2 * m + 1, si_)):
            dst3 = tbl[tix].rearrange("p (t j) -> p t j", j=64)
            src3 = stg_v[fi].rearrange("p (t j) -> p t j", j=32)
            for half in range(2):
                wv_ = wb[:, half * 32:(half + 1) * 32].unsqueeze(1).broadcast_to([128, 16, 32])
                dsth = dst3[:, :, half * 32:(half + 1) * 32]
                tok_tbl = on("vector",
                             lambda e, d_=dsth, s_=src3, w_=wv_: e.tensor_tensor(d_, s_, w_, ALU.mult),
                             waits=(cs_toks[fi], wtok), inc=SV)

    tok_sq1 = on("scalar", lambda e: e.activation(s_scr[0:1, 0:64], s_wqb[0:1, :], AF.Square,
                                                  accum_out=s_mh[:, 0:1]),
                 waits=(tok_wqb,), inc=SA)
    tok_sq2 = on("scalar", lambda e: e.activation(s_scr[0:1, 64:128], s_wkb[0:1, :], AF.Square,
                                                  accum_out=s_mh[:, 1:2]),
                 waits=(tok_wkb,), inc=SA)
    tok_t1 = on("vector", lambda e: e.tensor_tensor(s_mh[:, 2:3], s_mh[:, 0:1], s_mh[:, 1:2], ALU.mult),
                waits=(tok_sq1, tok_sq2), inc=SV)
    tok_lnm = on("scalar", lambda e: e.activation(s_mh[:, 3:4], s_mh[:, 2:3], AF.Ln),
                 waits=(tok_t1,), inc=SA)
    drain("scalar")
    tok_mh = on("scalar", lambda e: e.activation(s_mh[:, 2:3], s_mh[:, 3:4], AF.Exp,
                                                 scale=0.5, bias=s_ln8[:]),
                waits=(tok_lnm,), inc=SA)
    drain("scalar")
    tok_nmh0 = on("scalar", lambda e: e.mul(s_mh[:, 3:4], s_mh[:, 2:3], -1.0),
                  waits=(tok_mh,), inc=SA)
    tok_nmhmm = on("tensor", lambda e: e.matmul(p_c[:, 0:1], s_ones[:], s_mh[0:1, 3:4],
                                                start=True, stop=True),
                   waits=(tok_nmh0,), inc=SP)
    tok_nmh = on("scalar", lambda e: e.activation(s_nmh[:], p_c[:, 0:1], AF.Copy),
                 waits=(tok_nmhmm,), inc=SA)

    psum_slots = (p_a, p_b, p_c)
    slot_free = [None, None, tok_nmh]
    tp_banks = (p_t1, p_t2)
    bank_free = [None, None]
    bank_idx = [0]

    def tp_group(pairs, ev_eng, first_waits=(), ev_first_waits=()):
        b = bank_idx[0] % 2
        bank_idx[0] += 1
        pt = tp_banks[b]
        ev_sem = SA if ev_eng == "scalar" else SV
        tps = []
        for i, (src, dst) in enumerate(pairs):
            w = [bank_free[b]] + list(first_waits) if i == 0 else []
            tp = on("tensor",
                    lambda e, o_=pt[:, i * 128:(i + 1) * 128], s_=src:
                    e.transpose(o_, s_, s_id[:]),
                    waits=w, inc=SP if i == len(pairs) - 1 else None)
            tps.append((tp, pt[:, i * 128:(i + 1) * 128], dst))
        last_tp = tps[-1][0]
        last_ev = None
        for i, (_, slot, dst) in enumerate(tps):
            w = [last_tp] + list(ev_first_waits) if i == 0 else []
            if ev_eng == "scalar":
                last_ev = on("scalar",
                             lambda e, d_=dst, s_=slot: e.activation(d_, s_, AF.Copy),
                             waits=w, inc=ev_sem)
            else:
                last_ev = on("vector",
                             lambda e, d_=dst, s_=slot: e.tensor_copy(d_, s_),
                             waits=w, inc=ev_sem)
        bank_free[b] = last_ev
        return last_tp, last_ev

    tok_vevac_last = None
    tok_rq_free = None
    tok_qh_free = None
    tok_rstd_free = None
    tok_scr_free = None
    tok_tmp_free = None
    last_proj_tok = {}
    pending = [None]

    def emit_qk_tp():
        nonlocal tok_rq_free
        if pending[0] is None:
            return
        tok_rq, mat, t = pending[0]
        pending[0] = None
        pairs = []
        for ch in range(NCH):
            if mat == "q":
                dst = qt_blk(t // 4, ch, t % 4)
            else:
                dst = kt_blk(t, ch)
            pairs.append((s_rq[:, ch * 128:(ch + 1) * 128], dst))
        last_tp, _ = tp_group(pairs, "scalar", first_waits=(tok_rq, tok_id))
        tok_rq_free = last_tp

    si = 0
    for m_i, mat in enumerate(("q", "k", "v")):
        wbuf = s_w0 if mat in ("q", "v") else s_w1
        wtok = {"q": tok_wq, "k": tok_wk}.get(mat)
        if mat == "v":
            wtok = wload(s_w0, d_wv, SW0, waits=(last_proj_tok["q"],))
        tbl_c = tbl[0] if mat == "q" else tbl[2]
        tbl_s = tbl[1] if mat == "q" else tbl[3]
        for t in range(NT):
            slot = si % 3
            si += 1
            pp = psum_slots[slot]
            mm_tok = None
            for ch in range(NCH):
                for half in range(2):
                    w_v = wbuf[:, ch * 1024 + half * 512: ch * 1024 + (half + 1) * 512]
                    waits = []
                    if ch == 0 and half == 0:
                        waits = [slot_free[slot], wtok, tok_xt]
                    mm_tok = on("tensor",
                                lambda e, p_=pp[:, half * 512:(half + 1) * 512],
                                       l_=xt_blk(ch, t), r_=w_v, s0=(ch == 0), s1=(ch == NCH - 1):
                                e.matmul(p_, l_, r_, start=s0, stop=s1),
                                waits=waits,
                                inc=SP if (ch == NCH - 1 and half == 1) else None)
            last_proj_tok[mat] = mm_tok
            emit_qk_tp()
            pview = pp[:, :].rearrange("p (h j) -> p h j", j=64)

            if mat == "v":
                dst = s_v[:, t * (H * 65):(t + 1) * (H * 65)].rearrange(
                    "p (h j) -> p h j", j=65)[:, :, 0:64]
                tok_vevac_last = on("vector",
                                    lambda e, d_=dst, s_=pview: e.tensor_copy(d_, s_),
                                    waits=(mm_tok, tok_vones), inc=SV)
                slot_free[slot] = tok_vevac_last
                continue

            sqv = tmp_f32()
            tok_sq = on("scalar", lambda e, s_=pp[:, :], d_=sqv: e.activation(d_, s_, AF.Square),
                        waits=(mm_tok, tok_tmp_free), inc=SA)
            tok_ms = on("vector",
                        lambda e: e.tensor_reduce(
                            s_ms[:, :].rearrange("p (h o) -> p h o", o=1),
                            tmp_f32().rearrange("p (h j) -> p h j", j=64), AX, ALU.add),
                        waits=(tok_sq, tok_rstd_free), inc=SV)
            tok_ln = on("scalar", lambda e: e.activation(s_rstd[:], s_ms[:], AF.Ln,
                                                         scale=1.0 / 64.0, bias=s_eps1[:]),
                        waits=(tok_ms,), inc=SA)
            drain("scalar")
            tok_rstd = on("scalar", lambda e: e.activation(s_rstd2[:], s_rstd[:], AF.Exp, scale=-0.5),
                          waits=(tok_ln,), inc=SA)
            rb = s_rstd2[:, :].rearrange("p (h o) -> p h o", o=1).broadcast_to([128, 16, 64])
            qh3 = s_xqt[:, :].rearrange("p (h j) -> p h j", j=64)
            tok_qh = on("vector",
                        lambda e, o_=qh3, i_=pview, b_=rb: e.tensor_tensor(o_, i_, b_, ALU.mult),
                        waits=(tok_rstd, tok_qh_free), inc=SV)
            tok_rstd_free = tok_qh
            slot_free[slot] = tok_qh
            qq = s_xqt[:, :].rearrange("p (h two j) -> p h two j", two=2, j=32)
            q1v, q2v = qq[:, :, 0, :], qq[:, :, 1, :]
            ct = tbl_c[:, t * 64:(t + 1) * 64]
            st_ = tbl_s[:, t * 64:(t + 1) * 64]
            c1 = ct[:, 0:32].unsqueeze(1).broadcast_to([128, 16, 32])
            c2 = ct[:, 32:64].unsqueeze(1).broadcast_to([128, 16, 32])
            s1v = st_[:, 0:32].unsqueeze(1).broadcast_to([128, 16, 32])
            s2v = st_[:, 32:64].unsqueeze(1).broadcast_to([128, 16, 32])
            tmp3 = s_tmp[:, :].rearrange("p (k j) -> p k j", j=512)
            ta = tmp3[:, 0, :].rearrange("p (h j) -> p h j", j=32)
            tb = tmp3[:, 1, :].rearrange("p (h j) -> p h j", j=32)
            tc = tmp3[:, 2, :].rearrange("p (h j) -> p h j", j=32)
            td = tmp3[:, 3, :].rearrange("p (h j) -> p h j", j=32)
            drain("vector")
            on("vector", lambda e, o_=ta, a_=q1v, b_=c1: e.tensor_tensor(o_, a_, b_, ALU.mult),
               waits=(tok_tbl,))
            on("vector", lambda e, o_=tb, a_=q2v, b_=s2v: e.tensor_tensor(o_, a_, b_, ALU.mult))
            on("vector", lambda e, o_=tc, a_=q2v, b_=c2: e.tensor_tensor(o_, a_, b_, ALU.mult))
            on("vector", lambda e, o_=td, a_=q1v, b_=s1v: e.tensor_tensor(o_, a_, b_, ALU.mult))
            drain("vector")
            rq3 = s_rq[:, :].rearrange("p (h two j) -> p h two j", two=2, j=32)
            on("vector", lambda e, o_=rq3[:, :, 0, :], a_=ta, b_=tb: e.tensor_tensor(o_, a_, b_, ALU.subtract),
               waits=(tok_rq_free,))
            tok_rq = on("vector", lambda e, o_=rq3[:, :, 1, :], a_=tc, b_=td: e.tensor_tensor(o_, a_, b_, ALU.add),
                        inc=SV)
            tok_qh_free = tok_rq
            tok_tmp_free = tok_rq
            pending[0] = (tok_rq, mat, t)

    tok_qk_ready = (SA, SA.n)
    marks = {1: ({e: len(Q[e]) for e in ENGS}, (SA.n, SV.n, SP.n))}

    tok_wo = wload(s_w1, d_wo, SW1, waits=(last_proj_tok["k"],))
    tok_wff = wload(s_w0, d_wfft, SW0, waits=(last_proj_tok["v"],))

    pt_half_free = [None, None]
    pc_free = [slot_free[2], slot_free[2]]
    psum_free = {"a": slot_free[0], "b": slot_free[1]}
    tmp_free_att = tok_tmp_free
    an_free = None
    tok_r_evac = {}
    tok_x2 = {}
    tok_amax = {}
    tok_xdma = None
    tok_xti_free = None
    att_gate = [tok_qk_ready, tok_vevac_last, tok_nmh, tok_tri]
    first_s_done = False
    wo_tok_prev = None
    tok_sqr1_last = None

    for c in range(NQC):
        an_toks = {}
        at_last_tp = [None]
        at_first = [True]

        def emit_at_tp(m, c=c, an_toks=an_toks, at_last_tp=at_last_tp,
                       at_first=at_first):
            pairs = []
            for tl in range(4):
                src = s_an[:, tl * 1024 + m * 128: tl * 1024 + (m + 1) * 128]
                dst = s_at[:, m * 512 + tl * 128: m * 512 + (tl + 1) * 128]
                pairs.append((src, dst))
            evw = ()
            if at_first[0]:
                evw = (wo_tok_prev,)
                at_first[0] = False
            tp, _ = tp_group(pairs, "vector",
                             first_waits=(an_toks[2 * m + 1],),
                             ev_first_waits=evw)
            at_last_tp[0] = tp

        for h in range(H):
            if h % 2 == 1 and h >= 3:
                emit_at_tp((h - 3) // 2)
            ch = h // 2
            hb = (h % 2) * 64
            pcb = (h % 2) * 512
            jmax = 4 * (c + 1)
            tok_avlast = None

            def emit_av(g, etok, mtoks, pth, h=h, c=c, pcb=pcb, jmax=jmax):
                nonlocal tok_avlast
                par = g % 2
                for jj in range(2):
                    j = 2 * g + jj
                    rhs_v = s_v[:, j * (H * 65) + h * 65: j * (H * 65) + (h + 1) * 65]
                    for tl in range(max(0, j - 4 * c), 4):
                        diag = (j - 4 * c == tl)
                        if diag:
                            lhs = s_tmp[:, tl * 128:(tl + 1) * 128]
                        else:
                            lhs = pth[:, jj * 512 + tl * 128: jj * 512 + (tl + 1) * 128]
                        waits = [etok]
                        if diag:
                            waits.append(mtoks[j])
                        s0 = (g == 0 and jj == 0 and tl == 0)
                        if s0:
                            waits.append(pc_free[h % 2])
                        tok_avlast = on("tensor",
                                        lambda e, p_=p_c[:, pcb + tl * 65: pcb + (tl + 1) * 65],
                                               l_=lhs, r_=rhs_v, s0=s0, s1=(j == 4 * c + tl):
                                        e.matmul(p_, l_, r_, start=s0, stop=s1,
                                                 skip_group_check=True),
                                        waits=waits,
                                        inc=SP if (j == jmax - 1 and tl == 3) else None)
                pt_half_free[par] = tok_avlast

            pend_av = None
            for g in range(jmax // 2):
                par = g % 2
                ps = p_a if par == 0 else p_b
                pth = s_pt[:, par * 1024:(par + 1) * 1024]
                qs = [max(0, 2 * g + jj - 4 * c) * 128 for jj in range(2)]
                stok = None
                for jj in range(2):
                    j = 2 * g + jj
                    lhs = kt_blk(j, ch)[hb:hb + 64, :]
                    rhs_q = qt_win(c, ch)[hb:hb + 64, qs[jj]:512]
                    waits = []
                    if jj == 0:
                        waits = [psum_free["a" if par == 0 else "b"]]
                        if not first_s_done:
                            waits += att_gate
                            first_s_done = True
                    stok = on("tensor",
                              lambda e, p_=ps[:, jj * 512 + qs[jj]:(jj + 1) * 512],
                                     l_=lhs, r_=rhs_q:
                              e.matmul(p_, l_, r_, start=True, stop=True),
                              waits=waits, inc=SP if jj == 1 else None)
                qs0 = qs[0]
                pv = ps[:, :].rearrange("p (two q) -> p two q", q=512)[:, :, qs0:512]
                ev_ = pth[:, :].rearrange("p (two q) -> p two q", q=512)[:, :, qs0:512]
                etok = on("scalar",
                          lambda e, o_=ev_, i_=pv: e.activation(o_, i_, AF.Exp, bias=s_nmh[:]),
                          waits=(stok, pt_half_free[par]), inc=SA)
                psum_free["a" if par == 0 else "b"] = etok
                mtoks = {}
                for jj in range(2):
                    j = 2 * g + jj
                    if j >= 4 * c:
                        tld = j - 4 * c
                        blk = pth[:, jj * 512 + tld * 128: jj * 512 + (tld + 1) * 128]
                        mdst = s_tmp[:, tld * 128:(tld + 1) * 128]
                        mtoks[j] = on("gpsimd",
                                      lambda e, o_=mdst, a_=blk, b_=s_tri[:]:
                                      e.tensor_tensor(o_, a_, b_, ALU.mult),
                                      waits=(etok, tmp_free_att), inc=SL)
                if pend_av is not None:
                    emit_av(*pend_av)
                pend_av = (g, etok, mtoks, pth)
            emit_av(*pend_av)
            tmp_free_att = tok_avlast
            pcv = p_c[:, pcb:pcb + 260].rearrange("p (t j) -> p t j", j=65)
            tok_den = on("vector",
                         lambda e, v_=pcv: e.tensor_copy(
                             s_den[:, :].rearrange("p (t o) -> p t o", o=1), v_[:, :, 64:65]),
                         waits=(tok_avlast, an_free), inc=SV)
            drain("vector")
            tok_recip = on("vector", lambda e: e.reciprocal(s_rec[:], s_den[:]), inc=SV)
            drain("vector")
            an4 = s_an[:, :].rearrange("p (t hh j) -> p t hh j", hh=16, j=64)
            recb = s_rec[:, :].rearrange("p (t o) -> p t o", o=1).broadcast_to([128, 4, 64])
            tok_an = on("vector",
                        lambda e, o_=an4[:, :, h, :], i_=pcv, b_=recb:
                        e.tensor_tensor(o_, i_[:, :, 0:64], b_, ALU.mult),
                        inc=SV)
            pc_free[h % 2] = tok_an
            an_toks[h] = tok_an

        emit_at_tp(7)
        an_free = at_last_tp[0]
        tok_at_ready = (SV, SV.n)

        for tl in range(4):
            t = c * 4 + tl
            par = "a" if tl % 2 == 0 else "b"
            pp = p_a if par == "a" else p_b
            tok_xdma = on("sync",
                          lambda e, t_=t: e.dma_start(out=s_xti[:], in_=d_xin[t_ * 128:(t_ + 1) * 128, :]),
                          waits=(tok_xti_free,), inc=SXL, n=16)
            wo_tok = None
            for chh in range(NCH):
                for half in range(2):
                    lhs = s_at[:, chh * 512 + tl * 128: chh * 512 + (tl + 1) * 128]
                    rhs = s_w1[:, chh * 1024 + half * 512: chh * 1024 + (half + 1) * 512]
                    waits = []
                    if chh == 0 and half == 0:
                        waits = [tok_at_ready, psum_free[par]]
                        if c == 0 and tl == 0:
                            waits.append(tok_wo)
                    wo_tok = on("tensor",
                                lambda e, p_=pp[:, half * 512:(half + 1) * 512], l_=lhs, r_=rhs,
                                       s0=(chh == 0), s1=(chh == NCH - 1):
                                e.matmul(p_, l_, r_, start=s0, stop=s1),
                                waits=waits,
                                inc=SP if (chh == NCH - 1 and half == 1) else None)
            tok_r = on("vector",
                       lambda e, p_=pp: e.scalar_tensor_tensor(
                           s_scr[:], s_xti[:], 1.0, p_[:, :], ALU.mult, ALU.add,
                           accum_out=sml(0)),
                       waits=(wo_tok, tok_xdma, tok_scr_free), inc=SV)
            psum_free[par] = tok_r
            tok_r_evac[t] = tok_r
            drain("vector")
            tok_sqr = on("scalar",
                         lambda e: e.activation(s_xti[:], s_scr[:], AF.Square, accum_out=sml(1)),
                         waits=(tok_r,), inc=SA)
            tok_xti_free = tok_sqr
            tok_sqr1_last = tok_sqr
            tok_sq1t = on("scalar",
                          lambda e: e.activation(sml(2), sml(0), AF.Square, scale=1.0 / 32.0),
                          waits=(tok_r,), inc=SA)
            tok_t2 = on("vector",
                        lambda e: e.tensor_tensor(sml(3), sml(1), sml(2), ALU.subtract),
                        waits=(tok_sqr, tok_sq1t), inc=SV)
            tok_lnv = on("scalar",
                         lambda e: e.activation(sml(6), sml(3), AF.Ln, scale=1.0 / 1024.0,
                                                bias=s_eps2[:]),
                         waits=(tok_t2,), inc=SA)
            drain("scalar")
            tok_rstd2 = on("scalar",
                           lambda e: e.activation(sml(4), sml(6), AF.Exp, scale=-0.5),
                           waits=(tok_lnv,), inc=SA)
            tok_nmr = on("vector",
                         lambda e: e.scalar_tensor_tensor(sml(5), sml(0), -1.0 / 1024.0, sml(4),
                                                          ALU.mult, ALU.mult),
                         waits=(tok_rstd2,), inc=SV)
            drain("vector")
            tok_x2[t] = on("vector",
                           lambda e, o_=x2_tile(t): e.tensor_scalar(
                               o_, s_scr[:], sml(4), sml(5), ALU.mult, ALU.add),
                           inc=SV)
            tok_scr_free = tok_x2[t]
            tok_amax[t] = on("vector",
                             lambda e, o_=s_amax[:, t:t + 1], i_=x2_tile(t):
                             e.tensor_reduce(o_, i_, AX, ALU.max,
                                             apply_absolute_value=True),
                             waits=(tok_x2[t],), inc=SV)
        wo_tok_prev = wo_tok

    marks[2] = ({e: len(Q[e]) for e in ENGS}, (SA.n, SV.n, SP.n))

    ws_b = s_at[:, 0:2048].bitcast(F32)
    b_b = s_at[:, 2048:4096].bitcast(F32)
    on("gpsimd", lambda e: e.dma_start(out=ws_b, in_=d_wsrow.broadcast_to([128, D])),
       waits=(wo_tok,), inc=SG2, n=16)
    on("gpsimd", lambda e: e.dma_start(out=b_b, in_=d_brow.broadcast_to([128, D])),
       waits=(wo_tok,), inc=SG2, n=16)
    tok_wsb = tok_bb = (SG2, SG2.n)

    tok_amc = on("vector", lambda e: e.tensor_reduce(sml(7), s_amax[:], AX, ALU.max),
                 waits=(tok_amax[NT - 1],), inc=SV)
    tok_gl = on("gpsimd",
                lambda e: e.tensor_reduce(s_xs[0:1, 0:1], sml(7),
                                          mybir.AxisListType.XYZWC, ALU.max),
                waits=(tok_amc,), inc=SL)
    if _SIM_LOCAL_CC:
        tok_cdn = on("gpsimd", lambda e: e.tensor_copy(s_am128[0:1, 0:1], s_xs[0:1, 0:1]),
                     waits=(tok_gl,), inc=SL)
    else:
        tok_cup = on("gpsimd", lambda e: e.dma_start(out=d_ccin.ap(), in_=s_xs[0:1, 0:1]),
                     waits=(tok_gl,), inc=SCC, n=16)
        tok_cc = on("gpsimd", lambda e: e.collective_compute(
            "AllReduce", ALU.max, replica_groups=[list(range(8))],
            ins=[d_ccin.ap()], outs=[d_ccout.ap()]),
            waits=(tok_cup,), inc=SC, n=1)
        tok_cdn = on("gpsimd", lambda e: e.dma_start(out=s_am128[0:1, 0:1], in_=d_ccout.ap()),
                     waits=(tok_cc,), inc=SCC, n=16)
    tok_xsv = on("vector", lambda e: e.tensor_scalar(s_xs[:, 0:1], s_am128[0:1, 0:1],
                                                     float(np.float32(1.0) / np.float32(127.0)),
                                                     QEPS, ALU.mult, ALU.max),
                 waits=(tok_cdn,), inc=SV)
    drain("vector")
    tok_rxs = on("vector", lambda e: e.reciprocal(s_xs[:, 1:2], s_xs[:, 0:1]),
                 waits=(tok_xsv,), inc=SV)
    tok_bc1 = on("tensor", lambda e: e.matmul(p_c[:, 2:3], s_ones[:], s_xs[0:1, 0:1],
                                              start=True, stop=True),
                 waits=(tok_rxs, pc_free[0]), inc=SP)
    tok_bc2 = on("tensor", lambda e: e.matmul(p_c[:, 3:4], s_ones[:], s_xs[0:1, 1:2],
                                              start=True, stop=True),
                 waits=(tok_rxs, pc_free[1]), inc=SP)
    tok_xsb = on("scalar", lambda e: e.activation(s_xsb[:, 0:1], p_c[:, 2:3], AF.Copy),
                 waits=(tok_bc1,), inc=SA)
    tok_rxsb = on("scalar", lambda e: e.activation(s_xsb[:, 1:2], p_c[:, 3:4], AF.Copy),
                  waits=(tok_bc2,), inc=SA)
    ts_b = s_an[:, 0:2048].bitcast(F32)
    tok_tsb = on("vector", lambda e: e.tensor_scalar(ts_b, ws_b, s_xsb[:, 0:1], None, ALU.mult),
                 waits=(tok_wsb, tok_xsb, an_free), inc=SV)

    xq_all = s_kt[:, :]
    vf32 = s_v[:, :].bitcast(F32)
    scrF = (vf32[:, 0:1024], vf32[:, 1024:2048])
    tmpF = (vf32[:, 2048:3072], vf32[:, 3072:4096])
    x2bF = (vf32[:, 4096:5120], vf32[:, 5120:6144])
    xtiF = (vf32[:, 6144:7168], vf32[:, 7168:8192])
    smlF = lambda t, i: s_sml[:, (t % 2) * 8 + i:(t % 2) * 8 + i + 1]

    tok_q2 = {}
    tok_q3 = {}
    tok_r2d = {}
    tok_sqr2 = {}

    def emit_quant(t):
        p = t % 2
        tok_q1 = on("vector",
                    lambda e, i_=x2_tile(t), o_=scrF[p]: e.tensor_scalar(
                        o_, i_, s_xsb[:, 1:2], MAGIC, ALU.mult, ALU.add),
                    waits=(tok_rxsb, tok_x2[t],
                           tmp_free_att if t < 2 else tok_r2d[t - 2]), inc=SV)
        tok_q2[t] = on("scalar",
                       lambda e, i_=scrF[p], o_=tmpF[p]: e.activation(
                           o_, i_, AF.Copy, bias=-MAGIC),
                       waits=(tok_q1, tok_q3.get(t - 2)), inc=SA)
        tok_q3[t] = on("vector",
                       lambda e, i_=tmpF[p], o_=xq_all[:, t * 1024:(t + 1) * 1024]:
                       e.tensor_scalar(o_, i_, 127.0, -128.0, ALU.min, ALU.max),
                       waits=(tok_q2[t],), inc=SV)

    emit_quant(0)
    emit_quant(1)

    marks[3] = ({e: len(Q[e]) for e in ENGS}, (SA.n, SV.n, SP.n))

    xqt_bufs = (s_xqt, s_rq)
    tok_xqt_free = [None, None]
    tok_store = {}
    tok_out_free = [None, None]
    for t in range(NT):
        p = t % 2
        par = "a" if p == 0 else "b"
        pp = p_a if par == "a" else p_b
        xb = xqt_bufs[p]
        pairs = []
        for chh in range(NCH):
            src = xq_all[:, t * 1024 + chh * 128: t * 1024 + (chh + 1) * 128]
            dst = xb[:, chh * 128:(chh + 1) * 128]
            pairs.append((src, dst))
        tp_group(pairs, "scalar", first_waits=(tok_q3[t],),
                 ev_first_waits=(tok_xqt_free[p],))
        tok_xqt = (SA, SA.n)
        tok_x2b = on("gpsimd",
                     lambda e, i_=x2_tile(t), o_=x2bF[p]: e.tensor_tensor(
                         o_, i_, b_b, ALU.add),
                     waits=(tok_bb, tok_x2[t],
                            tmp_free_att if t < 2 else tok_sqr2[t - 2]), inc=SL)
        mm_tok = None
        for chh in range(NCH):
            for half in range(2):
                rhs = s_w0[:, chh * 1024 + half * 512: chh * 1024 + (half + 1) * 512]
                waits = []
                if chh == 0 and half == 0:
                    waits = [tok_xqt, psum_free[par]]
                    if t == 0:
                        waits.append(tok_wff)
                mm_tok = on("tensor",
                            lambda e, p_=pp[:, half * 512:(half + 1) * 512],
                                   l_=xb[:, chh * 128:(chh + 1) * 128], r_=rhs,
                                   s0=(chh == 0), s1=(chh == NCH - 1):
                            e.matmul(p_, l_, r_, start=s0, stop=s1),
                            waits=waits,
                            inc=SP if (chh == NCH - 1 and half == 1) else None)
        tok_xqt_free[p] = mm_tok
        tok_o1 = on("vector",
                    lambda e, p_=pp, o_=scrF[p]: e.tensor_tensor(o_, p_[:, :], ts_b, ALU.mult),
                    waits=(mm_tok, tok_tsb), inc=SV)
        psum_free[par] = tok_o1
        drain("vector")
        tok_r2d[t] = on("vector",
                        lambda e, a_=scrF[p], b_=x2bF[p], o_=xtiF[p]:
                        e.scalar_tensor_tensor(o_, a_, 1.0, b_, ALU.mult, ALU.add,
                                               accum_out=smlF(t, 0)),
                        waits=(tok_x2b, tok_sqr2.get(t - 2)), inc=SV)
        tok_r2 = tok_r2d[t]
        drain("vector")
        tok_sqr2[t] = on("scalar",
                         lambda e, i_=xtiF[p], o_=x2bF[p]: e.activation(
                             o_, i_, AF.Square, accum_out=smlF(t, 1)),
                         waits=(tok_r2,), inc=SA)
        tok_sq1t = on("scalar",
                      lambda e: e.activation(smlF(t, 2), smlF(t, 0), AF.Square, scale=1.0 / 32.0),
                      waits=(tok_r2,), inc=SA)
        tok_t2 = on("vector",
                    lambda e: e.tensor_tensor(smlF(t, 3), smlF(t, 1), smlF(t, 2), ALU.subtract),
                    waits=(tok_sqr2[t], tok_sq1t), inc=SV)
        tok_lnv = on("scalar",
                     lambda e: e.activation(smlF(t, 6), smlF(t, 3), AF.Ln, scale=1.0 / 1024.0,
                                            bias=s_eps2[:]),
                     waits=(tok_t2,), inc=SA)
        drain("scalar")
        tok_rstd2 = on("scalar",
                       lambda e: e.activation(smlF(t, 4), smlF(t, 6), AF.Exp, scale=-0.5),
                       waits=(tok_lnv,), inc=SA)
        tok_nmr = on("vector",
                     lambda e: e.scalar_tensor_tensor(smlF(t, 5), smlF(t, 0), -1.0 / 1024.0,
                                                      smlF(t, 4), ALU.mult, ALU.mult),
                     waits=(tok_rstd2,), inc=SV)
        drain("vector")
        outv = s_tbl[:, p * 2048:p * 2048 + 2048].bitcast(F32)
        tok_out = on("vector",
                     lambda e, o_=outv, i_=xtiF[p]: e.tensor_scalar(
                         o_, i_, smlF(t, 4), smlF(t, 5), ALU.mult, ALU.add),
                     waits=(tok_out_free[p],), inc=SV)
        tok_store[t] = on("gpsimd",
                          lambda e, t_=t, o_=outv: e.dma_start(out=d_y[t_ * 128:(t_ + 1) * 128, :], in_=o_),
                          waits=(tok_out,), inc=SSE if p == 0 else SSO, n=16)
        tok_out_free[p] = tok_store[t]
        if t + 2 < NT:
            emit_quant(t + 2)

    on("gpsimd", lambda e: e.nop(), waits=((SSE, SSE.n), (SSO, SSO.n)))

    if _TRUNC:
        cut, (na, nv, np_) = marks[_TRUNC]
        for e in ENGS:
            Q[e] = Q[e][:cut[e]]
        Q["gpsimd"].append((((SA, na), (SV, nv), (SP, np_)),
                            lambda e: e.nop(), None))

    with nc.Block() as block:
        def run(eng_name):
            waited = {}

            def body(eng):
                for waits, fn, inc in Q[eng_name]:
                    for (sm, v) in waits:
                        if v > waited.get(id(sm), 0):
                            eng.wait_ge(sm.h, v)
                            waited[id(sm)] = v
                    ins = fn(eng)
                    if inc is not None:
                        ins.then_inc(inc[0], inc[1])
            return body

        block.sync(run("sync"))
        block.scalar(run("scalar"))
        block.vector(run("vector"))
        block.tensor(run("tensor"))
        block.gpsimd(run("gpsimd"))

    es.close()
    return nc


_NC_CACHE = None


def _get_nc():
    global _NC_CACHE
    if _NC_CACHE is None:
        _NC_CACHE = _build_program()
    return _NC_CACHE


def _rope_tables():
    t = np.arange(T, dtype=np.float32)
    invf = (1.0 / (10000.0 ** (np.arange(0, HD, 2, dtype=np.float64) / HD))).astype(np.float32)
    freqs = t[:, None] * invf[None, :]
    cos = np.cos(freqs.astype(np.float64)).astype(np.float32)
    sin = np.sin(freqs.astype(np.float64)).astype(np.float32)
    cosq = cos * np.float32(0.125)
    sinq = sin * np.float32(0.125)
    return cosq, sinq, cos, sin


def kernel(L0_attn_x, L0_attn_Wq, L0_attn_Wk, L0_attn_Wv, L0_attn_Wo,
           L0_attn_q_norm_w, L0_attn_k_norm_w,
           L0_ffn_weight_int8, L0_ffn_weight_scale, L0_ffn_bias):
    x = np.ascontiguousarray(np.asarray(L0_attn_x, dtype=np.float32))
    wq = np.ascontiguousarray(np.asarray(L0_attn_Wq, dtype=np.float32))
    wk = np.ascontiguousarray(np.asarray(L0_attn_Wk, dtype=np.float32))
    wv = np.ascontiguousarray(np.asarray(L0_attn_Wv, dtype=np.float32))
    wo = np.ascontiguousarray(np.asarray(L0_attn_Wo, dtype=np.float32))
    wqn = np.asarray(L0_attn_q_norm_w, dtype=np.float32).reshape(1, HD)
    wkn = np.asarray(L0_attn_k_norm_w, dtype=np.float32).reshape(1, HD)
    w8 = np.ascontiguousarray(np.asarray(L0_ffn_weight_int8).T)
    wsr = np.ascontiguousarray(np.asarray(L0_ffn_weight_scale, dtype=np.float32).reshape(1, D))
    br = np.ascontiguousarray(np.asarray(L0_ffn_bias, dtype=np.float32).reshape(1, D))
    cosq, sinq, cosk, sink = _rope_tables()
    tri = np.triu(np.ones((128, 128), dtype=np.float32)).astype(ml_dtypes.bfloat16)
    ident = np.eye(128, dtype=np.float32).astype(ml_dtypes.bfloat16)

    nc = _get_nc()
    in_maps = []
    for c in range(8):
        xc = np.ascontiguousarray(x[c])
        in_maps.append({
            "xin": xc,
            "xtin": np.ascontiguousarray(xc.T),
            "wq": wq, "wk": wk, "wv": wv, "wo": wo,
            "wfft": w8, "wsrow": wsr, "brow": br,
            "wqrow": wqn, "wkrow": wkn,
            "cosq": cosq, "sinq": sinq, "cosk": cosk, "sink": sink,
            "trimask": tri, "identb": ident,
        })
    res = run_bass_kernel_spmd(nc, in_maps, core_ids=list(range(8)))
    out = np.stack([np.asarray(res.results[c]["y"]) for c in range(8)], axis=0)
    return out


# version 15
# speedup vs baseline: 1.0617x; 1.0617x over previous
import math

import numpy as np
import ml_dtypes
import concourse.bass as bass
import concourse.mybir as mybir
from concourse.bass_utils import run_bass_kernel_spmd

F32 = mybir.dt.float32
BF16 = mybir.dt.bfloat16
I8 = mybir.dt.int8
U8 = mybir.dt.uint8
AF = mybir.ActivationFunctionType
ALU = mybir.AluOpType
AX = mybir.AxisListType.X

T = 2048
D = 1024
H = 16
HD = 64
NT = 16
NCH = 8
NQC = 4
MAGIC = 12582912.0
LN_EPS = 1e-5
RMS_EPS = 1e-6
QEPS = 1e-5
LN_0125 = math.log(0.125)

ENGS = ("sync", "scalar", "vector", "tensor", "gpsimd")

_SIM_LOCAL_CC = False
_TRUNC = 0


class _Sem:
    __slots__ = ("h", "n")

    def __init__(self, h):
        self.h = h
        self.n = 0


def _build_program():
    nc = bass.Bass("TRN2", target_bir_lowering=False)

    d_xin = nc.dram_tensor("xin", [T, D], F32, kind="ExternalInput").ap()
    d_xt = nc.dram_tensor("xtin", [D, T], F32, kind="ExternalInput").ap()
    d_wq = nc.dram_tensor("wq", [D, D], F32, kind="ExternalInput").ap()
    d_wk = nc.dram_tensor("wk", [D, D], F32, kind="ExternalInput").ap()
    d_wv = nc.dram_tensor("wv", [D, D], F32, kind="ExternalInput").ap()
    d_wo = nc.dram_tensor("wo", [D, D], F32, kind="ExternalInput").ap()
    d_wfft = nc.dram_tensor("wfft", [D, D], I8, kind="ExternalInput").ap()
    d_wsrow = nc.dram_tensor("wsrow", [1, D], F32, kind="ExternalInput").ap()
    d_brow = nc.dram_tensor("brow", [1, D], F32, kind="ExternalInput").ap()
    d_wqrow = nc.dram_tensor("wqrow", [1, HD], F32, kind="ExternalInput").ap()
    d_wkrow = nc.dram_tensor("wkrow", [1, HD], F32, kind="ExternalInput").ap()
    d_cosq = nc.dram_tensor("cosq", [T, 32], F32, kind="ExternalInput").ap()
    d_sinq = nc.dram_tensor("sinq", [T, 32], F32, kind="ExternalInput").ap()
    d_cosk = nc.dram_tensor("cosk", [T, 32], F32, kind="ExternalInput").ap()
    d_sink = nc.dram_tensor("sink", [T, 32], F32, kind="ExternalInput").ap()
    d_trimask = nc.dram_tensor("trimask", [128, 128], BF16, kind="ExternalInput").ap()
    d_ident = nc.dram_tensor("identb", [128, 128], BF16, kind="ExternalInput").ap()
    d_y = nc.dram_tensor("y", [T, D], F32, kind="ExternalOutput").ap()

    d_ccin = nc.dram_tensor("ccin", [1, 1], F32)
    d_ccout = nc.dram_tensor("ccout", [1, 1], F32, addr_space="Shared")

    from contextlib import ExitStack
    es = ExitStack()
    S = es.enter_context

    arena = S(nc.sbuf_tensor("arena", [128, 65536], U8))
    QT0 = 32768
    s_kt = S(nc.sbuf_tensor("s_kt", [128, 16 * 1024], BF16))
    s_v = S(nc.sbuf_tensor("s_v", [128, NT * (H * 65)], BF16))
    s_w0 = S(nc.sbuf_tensor("s_w0", [128, NCH * 1024], BF16))
    s_w1 = S(nc.sbuf_tensor("s_w1", [128, NCH * 1024], BF16))
    s_pt = S(nc.sbuf_tensor("s_pt", [128, 2048], BF16))
    s_an = S(nc.sbuf_tensor("s_an", [128, 4096], BF16))
    s_at = S(nc.sbuf_tensor("s_at", [128, 4096], BF16))
    s_xti = S(nc.sbuf_tensor("s_xti", [128, 1024], F32))
    s_xqt = S(nc.sbuf_tensor("s_xqt", [128, 1024], BF16))
    s_rq = S(nc.sbuf_tensor("s_rq", [128, 1024], BF16))
    s_scr = S(nc.sbuf_tensor("s_scr", [128, 1024], F32))
    s_tmp = S(nc.sbuf_tensor("s_tmp", [128, 2048], BF16))
    s_tbl = S(nc.sbuf_tensor("s_tbl", [128, 4096], BF16))
    s_tri = S(nc.sbuf_tensor("s_tri", [128, 128], BF16))
    s_id = S(nc.sbuf_tensor("s_id", [128, 128], BF16))
    s_ms = S(nc.sbuf_tensor("s_ms", [128, 16], F32))
    s_rstd = S(nc.sbuf_tensor("s_rstd", [128, 16], F32))
    s_rstd2 = S(nc.sbuf_tensor("s_rstd2", [128, 16], F32))
    s_wqb = S(nc.sbuf_tensor("s_wqb", [128, 64], F32))
    s_wkb = S(nc.sbuf_tensor("s_wkb", [128, 64], F32))
    s_den = S(nc.sbuf_tensor("s_den", [128, 4], F32))
    s_rec = S(nc.sbuf_tensor("s_rec", [128, 4], F32))
    s_amax = S(nc.sbuf_tensor("s_amax", [128, 16], F32))
    s_am128 = S(nc.sbuf_tensor("s_am128", [1, 128], F32))
    s_sml = S(nc.sbuf_tensor("s_sml", [128, 16], F32))
    s_eps1 = S(nc.sbuf_tensor("s_eps1", [128, 1], F32))
    s_eps2 = S(nc.sbuf_tensor("s_eps2", [128, 1], F32))
    s_ones = S(nc.sbuf_tensor("s_ones", [1, 128], F32))
    s_ln8 = S(nc.sbuf_tensor("s_ln8", [1, 1], F32))
    s_mh = S(nc.sbuf_tensor("s_mh", [1, 4], F32))
    s_nmh = S(nc.sbuf_tensor("s_nmh", [128, 1], F32))
    s_xs = S(nc.sbuf_tensor("s_xs", [1, 2], F32))
    s_xsb = S(nc.sbuf_tensor("s_xsb", [128, 2], F32))

    p_a = S(nc.psum_tensor("p_a", [128, 1024], F32))
    p_b = S(nc.psum_tensor("p_b", [128, 1024], F32))
    p_c = S(nc.psum_tensor("p_c", [128, 1024], F32))
    p_t1 = S(nc.psum_tensor("p_t1", [128, 1024], BF16))
    p_t2 = S(nc.psum_tensor("p_t2", [128, 1024], BF16))

    sems = {}
    for name in ("lw0", "lw1", "lwx", "ldx", "ldg", "ldg2", "pe", "act",
                 "dve", "pool", "stE", "stO", "scc", "cc", "xld"):
        sems[name] = _Sem(S(nc.semaphore(name)))
    SW0, SW1, SWX, SX, SG, SG2, SP, SA, SV, SL, SSE, SSO, SCC, SC, SXL = (
        sems[n] for n in ("lw0", "lw1", "lwx", "ldx", "ldg", "ldg2", "pe",
                          "act", "dve", "pool", "stE", "stO", "scc", "cc",
                          "xld"))

    Q = {e: [] for e in ENGS}

    def on(eng, fn, waits=(), inc=None, n=1):
        tok = None
        rec_inc = None
        if inc is not None:
            inc.n += n
            tok = (inc, inc.n)
            rec_inc = (inc.h, n)
        Q[eng].append((tuple(w for w in waits if w is not None), fn, rec_inc))
        return tok

    def drain(eng):
        on(eng, lambda e: e.drain())

    def xt_blk(ch, t):
        base = ch * 4096 + t * 256
        return arena[:, base:base + 256].bitcast(BF16)

    def xt_full():
        return arena[:, 0:32768].bitcast(BF16)

    def qt_win(qc, ch):
        off = QT0 + qc * 8192 + ch * 1024
        return arena[:, off:off + 1024].bitcast(BF16)

    def qt_blk(qc, ch, tl):
        off = QT0 + qc * 8192 + ch * 1024 + tl * 256
        return arena[:, off:off + 256].bitcast(BF16)

    def kt_blk(j, ch):
        return s_kt[:, j * 1024 + ch * 128: j * 1024 + (ch + 1) * 128]

    def x2_tile(t):
        if t < 8:
            return arena[:, t * 4096:(t + 1) * 4096].bitcast(F32)
        return arena[:, QT0 + (t - 8) * 4096: QT0 + (t - 7) * 4096].bitcast(F32)

    def tmp_f32():
        return s_tmp[:, :].bitcast(F32)

    sml = lambda i: s_sml[:, i:i + 1]

    on("vector", lambda e: e.memset(s_eps1[:], RMS_EPS))
    on("vector", lambda e: e.memset(s_eps2[:], LN_EPS))
    on("vector", lambda e: e.memset(s_ones[:], 1.0))
    on("vector", lambda e: e.memset(s_ln8[:], LN_0125))
    v3 = s_v[:, :].rearrange("p (g j) -> p g j", j=65)
    tok_vones = on("vector", lambda e: e.memset(v3[:, :, 64:65], 1.0), inc=SV)

    on("sync", lambda e: e.dma_start(out=s_tri[:], in_=d_trimask), inc=SX, n=16)
    on("sync", lambda e: e.dma_start(out=s_id[:], in_=d_ident), inc=SX, n=16)
    stg = s_an[:, :].bitcast(F32)
    stg_v = [stg[:, i * 512:(i + 1) * 512] for i in range(4)]
    for i, dt_ in enumerate((d_cosq, d_sinq, d_cosk, d_sink)):
        src = dt_.rearrange("(n p) j -> p n j", p=128)
        dst = stg_v[i].rearrange("p (n j) -> p n j", j=32)
        on("sync", lambda e, s=src, d=dst: e.dma_start(out=d, in_=s),
           inc=SX, n=16)
    tok_tri = tok_id = (SX, SX.n)
    cs_toks = [(SX, SX.n)] * 4
    on("gpsimd", lambda e: e.dma_start(out=s_wqb[:], in_=d_wqrow.broadcast_to([128, HD])), inc=SG, n=16)
    on("gpsimd", lambda e: e.dma_start(out=s_wkb[:], in_=d_wkrow.broadcast_to([128, HD])), inc=SG, n=16)
    tok_wqb = tok_wkb = (SG, SG.n)
    def wload(dst_sb, src, sem, waits=()):
        src_r = src.rearrange("(c p) n -> p c n", p=128)
        dst_r = dst_sb[:, :].rearrange("p (c n) -> p c n", n=1024)
        return on("gpsimd", lambda e: e.dma_start(out=dst_r, in_=src_r),
                  waits=waits, inc=sem, n=16)
    tok_wq = wload(s_w0, d_wq, SW0)
    tok_wk = wload(s_w1, d_wk, SW1)
    xt_src = d_xt.rearrange("(c p) t -> p c t", p=128)
    xt_dst = xt_full().rearrange("p (c t) -> p c t", t=2048)
    tok_xt = on("gpsimd", lambda e: e.dma_start(out=xt_dst, in_=xt_src), inc=SWX, n=16)

    tbl = [s_tbl[:, i * 1024:(i + 1) * 1024] for i in range(4)]
    tok_tbl = None
    for m, (ci, si_, wb, wtok) in enumerate(((0, 1, s_wqb, tok_wqb), (2, 3, s_wkb, tok_wkb))):
        for tix, fi in ((2 * m, ci), (2 * m + 1, si_)):
            dst3 = tbl[tix].rearrange("p (t j) -> p t j", j=64)
            src3 = stg_v[fi].rearrange("p (t j) -> p t j", j=32)
            for half in range(2):
                wv_ = wb[:, half * 32:(half + 1) * 32].unsqueeze(1).broadcast_to([128, 16, 32])
                dsth = dst3[:, :, half * 32:(half + 1) * 32]
                tok_tbl = on("vector",
                             lambda e, d_=dsth, s_=src3, w_=wv_: e.tensor_tensor(d_, s_, w_, ALU.mult),
                             waits=(cs_toks[fi], wtok), inc=SV)

    tok_sq1 = on("scalar", lambda e: e.activation(s_scr[0:1, 0:64], s_wqb[0:1, :], AF.Square,
                                                  accum_out=s_mh[:, 0:1]),
                 waits=(tok_wqb,), inc=SA)
    tok_sq2 = on("scalar", lambda e: e.activation(s_scr[0:1, 64:128], s_wkb[0:1, :], AF.Square,
                                                  accum_out=s_mh[:, 1:2]),
                 waits=(tok_wkb,), inc=SA)
    tok_t1 = on("vector", lambda e: e.tensor_tensor(s_mh[:, 2:3], s_mh[:, 0:1], s_mh[:, 1:2], ALU.mult),
                waits=(tok_sq1, tok_sq2), inc=SV)
    tok_lnm = on("scalar", lambda e: e.activation(s_mh[:, 3:4], s_mh[:, 2:3], AF.Ln),
                 waits=(tok_t1,), inc=SA)
    drain("scalar")
    tok_mh = on("scalar", lambda e: e.activation(s_mh[:, 2:3], s_mh[:, 3:4], AF.Exp,
                                                 scale=0.5, bias=s_ln8[:]),
                waits=(tok_lnm,), inc=SA)
    drain("scalar")
    tok_nmh0 = on("scalar", lambda e: e.mul(s_mh[:, 3:4], s_mh[:, 2:3], -1.0),
                  waits=(tok_mh,), inc=SA)
    tok_nmhmm = on("tensor", lambda e: e.matmul(p_c[:, 0:1], s_ones[:], s_mh[0:1, 3:4],
                                                start=True, stop=True),
                   waits=(tok_nmh0,), inc=SP)
    tok_nmh = on("scalar", lambda e: e.activation(s_nmh[:], p_c[:, 0:1], AF.Copy),
                 waits=(tok_nmhmm,), inc=SA)

    psum_slots = (p_a, p_b, p_c)
    slot_free = [None, None, tok_nmh]
    tp_banks = (p_t1, p_t2)
    bank_free = [None, None]
    bank_idx = [0]

    def tp_group(pairs, ev_eng, first_waits=(), ev_first_waits=()):
        b = bank_idx[0] % 2
        bank_idx[0] += 1
        pt = tp_banks[b]
        ev_sem = SA if ev_eng == "scalar" else SV
        tps = []
        for i, (src, dst) in enumerate(pairs):
            w = [bank_free[b]] + list(first_waits) if i == 0 else []
            tp = on("tensor",
                    lambda e, o_=pt[:, i * 128:(i + 1) * 128], s_=src:
                    e.transpose(o_, s_, s_id[:]),
                    waits=w, inc=SP if i == len(pairs) - 1 else None)
            tps.append((tp, pt[:, i * 128:(i + 1) * 128], dst))
        last_tp = tps[-1][0]
        last_ev = None
        for i, (_, slot, dst) in enumerate(tps):
            w = [last_tp] + list(ev_first_waits) if i == 0 else []
            if ev_eng == "scalar":
                last_ev = on("scalar",
                             lambda e, d_=dst, s_=slot: e.activation(d_, s_, AF.Copy),
                             waits=w, inc=ev_sem)
            else:
                last_ev = on("vector",
                             lambda e, d_=dst, s_=slot: e.tensor_copy(d_, s_),
                             waits=w, inc=ev_sem)
        bank_free[b] = last_ev
        return last_tp, last_ev

    tok_vevac_last = None
    tok_rq_free = None
    tok_qh_free = None
    tok_rstd_free = None
    tok_scr_free = None
    tok_tmp_free = None
    last_proj_tok = {}
    pending = [None]

    def emit_qk_tp():
        nonlocal tok_rq_free
        if pending[0] is None:
            return
        tok_rq, mat, t = pending[0]
        pending[0] = None
        pairs = []
        for ch in range(NCH):
            if mat == "q":
                dst = qt_blk(t // 4, ch, t % 4)
            else:
                dst = kt_blk(t, ch)
            pairs.append((s_rq[:, ch * 128:(ch + 1) * 128], dst))
        last_tp, _ = tp_group(pairs, "scalar", first_waits=(tok_rq, tok_id))
        tok_rq_free = last_tp

    si = 0
    for m_i, mat in enumerate(("q", "k", "v")):
        wbuf = s_w0 if mat in ("q", "v") else s_w1
        wtok = {"q": tok_wq, "k": tok_wk}.get(mat)
        if mat == "v":
            wtok = wload(s_w0, d_wv, SW0, waits=(last_proj_tok["q"],))
        tbl_c = tbl[0] if mat == "q" else tbl[2]
        tbl_s = tbl[1] if mat == "q" else tbl[3]
        for t in range(NT):
            slot = si % 3
            si += 1
            pp = psum_slots[slot]
            mm_tok = None
            for ch in range(NCH):
                for half in range(2):
                    w_v = wbuf[:, ch * 1024 + half * 512: ch * 1024 + (half + 1) * 512]
                    waits = []
                    if ch == 0 and half == 0:
                        waits = [slot_free[slot], wtok, tok_xt]
                    mm_tok = on("tensor",
                                lambda e, p_=pp[:, half * 512:(half + 1) * 512],
                                       l_=xt_blk(ch, t), r_=w_v, s0=(ch == 0), s1=(ch == NCH - 1):
                                e.matmul(p_, l_, r_, start=s0, stop=s1),
                                waits=waits,
                                inc=SP if (ch == NCH - 1 and half == 1) else None)
            last_proj_tok[mat] = mm_tok
            emit_qk_tp()
            pview = pp[:, :].rearrange("p (h j) -> p h j", j=64)

            if mat == "v":
                dst = s_v[:, t * (H * 65):(t + 1) * (H * 65)].rearrange(
                    "p (h j) -> p h j", j=65)[:, :, 0:64]
                tok_vevac_last = on("vector",
                                    lambda e, d_=dst, s_=pview: e.tensor_copy(d_, s_),
                                    waits=(mm_tok, tok_vones), inc=SV)
                slot_free[slot] = tok_vevac_last
                continue

            sqv = tmp_f32()
            tok_sq = on("scalar", lambda e, s_=pp[:, :], d_=sqv: e.activation(d_, s_, AF.Square),
                        waits=(mm_tok, tok_tmp_free), inc=SA)
            tok_ms = on("vector",
                        lambda e: e.tensor_reduce(
                            s_ms[:, :].rearrange("p (h o) -> p h o", o=1),
                            tmp_f32().rearrange("p (h j) -> p h j", j=64), AX, ALU.add),
                        waits=(tok_sq, tok_rstd_free), inc=SV)
            tok_ln = on("scalar", lambda e: e.activation(s_rstd[:], s_ms[:], AF.Ln,
                                                         scale=1.0 / 64.0, bias=s_eps1[:]),
                        waits=(tok_ms,), inc=SA)
            drain("scalar")
            tok_rstd = on("scalar", lambda e: e.activation(s_rstd2[:], s_rstd[:], AF.Exp, scale=-0.5),
                          waits=(tok_ln,), inc=SA)
            rb = s_rstd2[:, :].rearrange("p (h o) -> p h o", o=1).broadcast_to([128, 16, 64])
            qh3 = s_xqt[:, :].rearrange("p (h j) -> p h j", j=64)
            tok_qh = on("vector",
                        lambda e, o_=qh3, i_=pview, b_=rb: e.tensor_tensor(o_, i_, b_, ALU.mult),
                        waits=(tok_rstd, tok_qh_free), inc=SV)
            tok_rstd_free = tok_qh
            slot_free[slot] = tok_qh
            qq = s_xqt[:, :].rearrange("p (h two j) -> p h two j", two=2, j=32)
            q1v, q2v = qq[:, :, 0, :], qq[:, :, 1, :]
            ct = tbl_c[:, t * 64:(t + 1) * 64]
            st_ = tbl_s[:, t * 64:(t + 1) * 64]
            c1 = ct[:, 0:32].unsqueeze(1).broadcast_to([128, 16, 32])
            c2 = ct[:, 32:64].unsqueeze(1).broadcast_to([128, 16, 32])
            s1v = st_[:, 0:32].unsqueeze(1).broadcast_to([128, 16, 32])
            s2v = st_[:, 32:64].unsqueeze(1).broadcast_to([128, 16, 32])
            tmp3 = s_tmp[:, :].rearrange("p (k j) -> p k j", j=512)
            ta = tmp3[:, 0, :].rearrange("p (h j) -> p h j", j=32)
            tb = tmp3[:, 1, :].rearrange("p (h j) -> p h j", j=32)
            tc = tmp3[:, 2, :].rearrange("p (h j) -> p h j", j=32)
            td = tmp3[:, 3, :].rearrange("p (h j) -> p h j", j=32)
            drain("vector")
            on("vector", lambda e, o_=ta, a_=q1v, b_=c1: e.tensor_tensor(o_, a_, b_, ALU.mult),
               waits=(tok_tbl,))
            on("vector", lambda e, o_=tb, a_=q2v, b_=s2v: e.tensor_tensor(o_, a_, b_, ALU.mult))
            on("vector", lambda e, o_=tc, a_=q2v, b_=c2: e.tensor_tensor(o_, a_, b_, ALU.mult))
            on("vector", lambda e, o_=td, a_=q1v, b_=s1v: e.tensor_tensor(o_, a_, b_, ALU.mult))
            drain("vector")
            rq3 = s_rq[:, :].rearrange("p (h two j) -> p h two j", two=2, j=32)
            on("vector", lambda e, o_=rq3[:, :, 0, :], a_=ta, b_=tb: e.tensor_tensor(o_, a_, b_, ALU.subtract),
               waits=(tok_rq_free,))
            tok_rq = on("vector", lambda e, o_=rq3[:, :, 1, :], a_=tc, b_=td: e.tensor_tensor(o_, a_, b_, ALU.add),
                        inc=SV)
            tok_qh_free = tok_rq
            tok_tmp_free = tok_rq
            pending[0] = (tok_rq, mat, t)

    tok_qk_ready = (SA, SA.n)
    marks = {1: ({e: len(Q[e]) for e in ENGS}, (SA.n, SV.n, SP.n))}

    tok_wo = wload(s_w1, d_wo, SW1, waits=(last_proj_tok["k"],))
    tok_wff = wload(s_w0, d_wfft, SW0, waits=(last_proj_tok["v"],))

    pt_half_free = [None, None]
    pc_free = [slot_free[2], slot_free[2]]
    psum_free = {"a": slot_free[0], "b": slot_free[1]}
    tmp_free_att = tok_tmp_free
    an_free = None
    tok_r_evac = {}
    tok_x2 = {}
    tok_amax = {}
    tok_xdma = None
    tok_xti_free = None
    att_gate = [tok_qk_ready, tok_vevac_last, tok_nmh, tok_tri]
    first_s_done = False
    wo_tok_prev = None
    tok_sqr1_last = None

    for c in range(NQC):
        an_toks = {}
        at_last_tp = [None]
        at_first = [True]

        def emit_at_tp(m, c=c, an_toks=an_toks, at_last_tp=at_last_tp,
                       at_first=at_first):
            pairs = []
            for tl in range(4):
                src = s_an[:, tl * 1024 + m * 128: tl * 1024 + (m + 1) * 128]
                dst = s_at[:, m * 512 + tl * 128: m * 512 + (tl + 1) * 128]
                pairs.append((src, dst))
            evw = ()
            if at_first[0]:
                evw = (wo_tok_prev,)
                at_first[0] = False
            tp, _ = tp_group(pairs, "vector",
                             first_waits=(an_toks[2 * m + 1],),
                             ev_first_waits=evw)
            at_last_tp[0] = tp

        for h in range(H):
            if h % 2 == 1 and h >= 3:
                emit_at_tp((h - 3) // 2)
            ch = h // 2
            hb = (h % 2) * 64
            pcb = (h % 2) * 512
            jmax = 4 * (c + 1)
            tok_avlast = None

            def emit_av(g, etok, mtoks, pth, h=h, c=c, pcb=pcb, jmax=jmax):
                nonlocal tok_avlast
                par = g % 2
                for jj in range(2):
                    j = 2 * g + jj
                    rhs_v = s_v[:, j * (H * 65) + h * 65: j * (H * 65) + (h + 1) * 65]
                    for tl in range(max(0, j - 4 * c), 4):
                        diag = (j - 4 * c == tl)
                        if diag:
                            lhs = s_tmp[:, tl * 128:(tl + 1) * 128]
                        else:
                            lhs = pth[:, jj * 512 + tl * 128: jj * 512 + (tl + 1) * 128]
                        waits = [etok]
                        if diag:
                            waits.append(mtoks[j])
                        s0 = (g == 0 and jj == 0 and tl == 0)
                        if s0:
                            waits.append(pc_free[h % 2])
                        tok_avlast = on("tensor",
                                        lambda e, p_=p_c[:, pcb + tl * 65: pcb + (tl + 1) * 65],
                                               l_=lhs, r_=rhs_v, s0=s0, s1=(j == 4 * c + tl):
                                        e.matmul(p_, l_, r_, start=s0, stop=s1,
                                                 skip_group_check=True),
                                        waits=waits,
                                        inc=SP if (j == jmax - 1 and tl == 3) else None)
                pt_half_free[par] = tok_avlast

            pend_av = None
            for g in range(jmax // 2):
                par = g % 2
                ps = p_a if par == 0 else p_b
                pth = s_pt[:, par * 1024:(par + 1) * 1024]
                qs = [max(0, 2 * g + jj - 4 * c) * 128 for jj in range(2)]
                stok = None
                for jj in range(2):
                    j = 2 * g + jj
                    lhs = kt_blk(j, ch)[hb:hb + 64, :]
                    rhs_q = qt_win(c, ch)[hb:hb + 64, qs[jj]:512]
                    waits = []
                    if jj == 0:
                        waits = [psum_free["a" if par == 0 else "b"]]
                        if not first_s_done:
                            waits += att_gate
                            first_s_done = True
                    stok = on("tensor",
                              lambda e, p_=ps[:, jj * 512 + qs[jj]:(jj + 1) * 512],
                                     l_=lhs, r_=rhs_q:
                              e.matmul(p_, l_, r_, start=True, stop=True),
                              waits=waits, inc=SP if jj == 1 else None)
                qs0 = qs[0]
                pv = ps[:, :].rearrange("p (two q) -> p two q", q=512)[:, :, qs0:512]
                ev_ = pth[:, :].rearrange("p (two q) -> p two q", q=512)[:, :, qs0:512]
                etok = on("scalar",
                          lambda e, o_=ev_, i_=pv: e.activation(o_, i_, AF.Exp, bias=s_nmh[:]),
                          waits=(stok, pt_half_free[par]), inc=SA)
                psum_free["a" if par == 0 else "b"] = etok
                mtoks = {}
                for jj in range(2):
                    j = 2 * g + jj
                    if j >= 4 * c:
                        tld = j - 4 * c
                        blk = pth[:, jj * 512 + tld * 128: jj * 512 + (tld + 1) * 128]
                        mdst = s_tmp[:, tld * 128:(tld + 1) * 128]
                        mtoks[j] = on("gpsimd",
                                      lambda e, o_=mdst, a_=blk, b_=s_tri[:]:
                                      e.tensor_tensor(o_, a_, b_, ALU.mult),
                                      waits=(etok, tmp_free_att), inc=SL)
                if pend_av is not None:
                    emit_av(*pend_av)
                pend_av = (g, etok, mtoks, pth)
            emit_av(*pend_av)
            tmp_free_att = tok_avlast
            pcv = p_c[:, pcb:pcb + 260].rearrange("p (t j) -> p t j", j=65)
            tok_den = on("vector",
                         lambda e, v_=pcv: e.tensor_copy(
                             s_den[:, :].rearrange("p (t o) -> p t o", o=1), v_[:, :, 64:65]),
                         waits=(tok_avlast, an_free), inc=SV)
            drain("vector")
            tok_recip = on("vector", lambda e: e.reciprocal(s_rec[:], s_den[:]), inc=SV)
            drain("vector")
            an4 = s_an[:, :].rearrange("p (t hh j) -> p t hh j", hh=16, j=64)
            recb = s_rec[:, :].rearrange("p (t o) -> p t o", o=1).broadcast_to([128, 4, 64])
            tok_an = on("vector",
                        lambda e, o_=an4[:, :, h, :], i_=pcv, b_=recb:
                        e.tensor_tensor(o_, i_[:, :, 0:64], b_, ALU.mult),
                        inc=SV)
            pc_free[h % 2] = tok_an
            an_toks[h] = tok_an

        emit_at_tp(7)
        an_free = at_last_tp[0]
        tok_at_ready = (SV, SV.n)

        for tl in range(4):
            t = c * 4 + tl
            par = "a" if tl % 2 == 0 else "b"
            pp = p_a if par == "a" else p_b
            tok_xdma = on("sync",
                          lambda e, t_=t: e.dma_start(out=s_xti[:], in_=d_xin[t_ * 128:(t_ + 1) * 128, :]),
                          waits=(tok_xti_free,), inc=SXL, n=16)
            wo_tok = None
            for chh in range(NCH):
                for half in range(2):
                    lhs = s_at[:, chh * 512 + tl * 128: chh * 512 + (tl + 1) * 128]
                    rhs = s_w1[:, chh * 1024 + half * 512: chh * 1024 + (half + 1) * 512]
                    waits = []
                    if chh == 0 and half == 0:
                        waits = [tok_at_ready, psum_free[par]]
                        if c == 0 and tl == 0:
                            waits.append(tok_wo)
                    wo_tok = on("tensor",
                                lambda e, p_=pp[:, half * 512:(half + 1) * 512], l_=lhs, r_=rhs,
                                       s0=(chh == 0), s1=(chh == NCH - 1):
                                e.matmul(p_, l_, r_, start=s0, stop=s1),
                                waits=waits,
                                inc=SP if (chh == NCH - 1 and half == 1) else None)
            tok_r = on("vector",
                       lambda e, p_=pp: e.scalar_tensor_tensor(
                           s_scr[:], s_xti[:], 1.0, p_[:, :], ALU.mult, ALU.add,
                           accum_out=sml(0)),
                       waits=(wo_tok, tok_xdma, tok_scr_free), inc=SV)
            psum_free[par] = tok_r
            tok_r_evac[t] = tok_r
            drain("vector")
            tok_sqr = on("scalar",
                         lambda e: e.activation(s_xti[:], s_scr[:], AF.Square, accum_out=sml(1)),
                         waits=(tok_r,), inc=SA)
            tok_xti_free = tok_sqr
            tok_sqr1_last = tok_sqr
            tok_sq1t = on("scalar",
                          lambda e: e.activation(sml(2), sml(0), AF.Square, scale=1.0 / 32.0),
                          waits=(tok_r,), inc=SA)
            tok_t2 = on("vector",
                        lambda e: e.tensor_tensor(sml(3), sml(1), sml(2), ALU.subtract),
                        waits=(tok_sqr, tok_sq1t), inc=SV)
            tok_lnv = on("scalar",
                         lambda e: e.activation(sml(6), sml(3), AF.Ln, scale=1.0 / 1024.0,
                                                bias=s_eps2[:]),
                         waits=(tok_t2,), inc=SA)
            drain("scalar")
            tok_rstd2 = on("scalar",
                           lambda e: e.activation(sml(4), sml(6), AF.Exp, scale=-0.5),
                           waits=(tok_lnv,), inc=SA)
            tok_nmr = on("vector",
                         lambda e: e.scalar_tensor_tensor(sml(5), sml(0), -1.0 / 1024.0, sml(4),
                                                          ALU.mult, ALU.mult),
                         waits=(tok_rstd2,), inc=SV)
            drain("vector")
            tok_x2[t] = on("vector",
                           lambda e, o_=x2_tile(t): e.tensor_scalar(
                               o_, s_scr[:], sml(4), sml(5), ALU.mult, ALU.add),
                           inc=SV)
            tok_scr_free = tok_x2[t]
            tok_amax[t] = on("vector",
                             lambda e, o_=s_amax[:, t:t + 1], i_=x2_tile(t):
                             e.tensor_reduce(o_, i_, AX, ALU.max,
                                             apply_absolute_value=True),
                             waits=(tok_x2[t],), inc=SV)
        wo_tok_prev = wo_tok

    marks[2] = ({e: len(Q[e]) for e in ENGS}, (SA.n, SV.n, SP.n))

    ws_b = s_at[:, 0:2048].bitcast(F32)
    b_b = s_at[:, 2048:4096].bitcast(F32)
    on("gpsimd", lambda e: e.dma_start(out=ws_b, in_=d_wsrow.broadcast_to([128, D])),
       waits=(wo_tok,), inc=SG2, n=16)
    on("gpsimd", lambda e: e.dma_start(out=b_b, in_=d_brow.broadcast_to([128, D])),
       waits=(wo_tok,), inc=SG2, n=16)
    tok_wsb = tok_bb = (SG2, SG2.n)

    tok_amc = on("vector", lambda e: e.tensor_reduce(sml(7), s_amax[:], AX, ALU.max),
                 waits=(tok_amax[NT - 1],), inc=SV)
    tok_gl = on("gpsimd",
                lambda e: e.tensor_reduce(s_xs[0:1, 0:1], sml(7),
                                          mybir.AxisListType.XYZWC, ALU.max),
                waits=(tok_amc,), inc=SL)
    if _SIM_LOCAL_CC:
        tok_cdn = on("gpsimd", lambda e: e.tensor_copy(s_am128[0:1, 0:1], s_xs[0:1, 0:1]),
                     waits=(tok_gl,), inc=SL)
    else:
        tok_cup = on("gpsimd", lambda e: e.dma_start(out=d_ccin.ap(), in_=s_xs[0:1, 0:1]),
                     waits=(tok_gl,), inc=SCC, n=16)
        tok_cc = on("gpsimd", lambda e: e.collective_compute(
            "AllReduce", ALU.max, replica_groups=[list(range(8))],
            ins=[d_ccin.ap()], outs=[d_ccout.ap()]),
            waits=(tok_cup,), inc=SC, n=1)
        tok_cdn = on("gpsimd", lambda e: e.dma_start(out=s_am128[0:1, 0:1], in_=d_ccout.ap()),
                     waits=(tok_cc,), inc=SCC, n=16)
    tok_xsv = on("vector", lambda e: e.tensor_scalar(s_xs[:, 0:1], s_am128[0:1, 0:1],
                                                     float(np.float32(1.0) / np.float32(127.0)),
                                                     QEPS, ALU.mult, ALU.max),
                 waits=(tok_cdn,), inc=SV)
    drain("vector")
    tok_rxs = on("vector", lambda e: e.reciprocal(s_xs[:, 1:2], s_xs[:, 0:1]),
                 waits=(tok_xsv,), inc=SV)
    tok_bc1 = on("tensor", lambda e: e.matmul(p_c[:, 2:3], s_ones[:], s_xs[0:1, 0:1],
                                              start=True, stop=True),
                 waits=(tok_rxs, pc_free[0]), inc=SP)
    tok_bc2 = on("tensor", lambda e: e.matmul(p_c[:, 3:4], s_ones[:], s_xs[0:1, 1:2],
                                              start=True, stop=True),
                 waits=(tok_rxs, pc_free[1]), inc=SP)
    tok_xsb = on("scalar", lambda e: e.activation(s_xsb[:, 0:1], p_c[:, 2:3], AF.Copy),
                 waits=(tok_bc1,), inc=SA)
    tok_rxsb = on("scalar", lambda e: e.activation(s_xsb[:, 1:2], p_c[:, 3:4], AF.Copy),
                  waits=(tok_bc2,), inc=SA)
    ts_b = s_an[:, 0:2048].bitcast(F32)
    tok_tsb = on("vector", lambda e: e.tensor_scalar(ts_b, ws_b, s_xsb[:, 0:1], None, ALU.mult),
                 waits=(tok_wsb, tok_xsb, an_free), inc=SV)

    xq_all = s_kt[:, :]
    vf32 = s_v[:, :].bitcast(F32)
    scrF = (vf32[:, 0:1024], vf32[:, 1024:2048])
    tmpF = (vf32[:, 2048:3072], vf32[:, 3072:4096])
    x2bF = (vf32[:, 4096:5120], vf32[:, 5120:6144])
    xtiF = (vf32[:, 6144:7168], vf32[:, 7168:8192])
    smlF = lambda t, i: s_sml[:, (t % 2) * 8 + i:(t % 2) * 8 + i + 1]

    tok_q2 = {}
    tok_q3 = {}
    tok_r2d = {}
    tok_sqr2 = {}

    def emit_quant(t):
        p = t % 2
        tok_q1 = on("vector",
                    lambda e, i_=x2_tile(t), o_=scrF[p]: e.tensor_scalar(
                        o_, i_, s_xsb[:, 1:2], MAGIC, ALU.mult, ALU.add),
                    waits=(tok_rxsb, tok_x2[t],
                           tmp_free_att if t < 2 else tok_r2d[t - 2]), inc=SV)
        tok_q2[t] = on("scalar",
                       lambda e, i_=scrF[p], o_=tmpF[p]: e.activation(
                           o_, i_, AF.Copy, bias=-MAGIC),
                       waits=(tok_q1, tok_q3.get(t - 2)), inc=SA)
        tok_q3[t] = on("vector",
                       lambda e, i_=tmpF[p], o_=xq_all[:, t * 1024:(t + 1) * 1024]:
                       e.tensor_scalar(o_, i_, 127.0, -128.0, ALU.min, ALU.max),
                       waits=(tok_q2[t],), inc=SV)

    emit_quant(0)
    emit_quant(1)

    marks[3] = ({e: len(Q[e]) for e in ENGS}, (SA.n, SV.n, SP.n))

    xqt_bufs = (s_xqt, s_rq)
    tok_xqt_free = [None, None]
    tok_store = {}
    tok_out_free = [None, None]
    tok_xqt_t = {}

    def emit_xq_tp(t):
        xb = xqt_bufs[t % 2]
        pairs = []
        for chh in range(NCH):
            src = xq_all[:, t * 1024 + chh * 128: t * 1024 + (chh + 1) * 128]
            dst = xb[:, chh * 128:(chh + 1) * 128]
            pairs.append((src, dst))
        tp_group(pairs, "scalar", first_waits=(tok_q3[t],),
                 ev_first_waits=(tok_xqt_free[t % 2],))
        tok_xqt_t[t] = (SA, SA.n)

    emit_xq_tp(0)
    for t in range(NT):
        p = t % 2
        par = "a" if p == 0 else "b"
        pp = p_a if par == "a" else p_b
        xb = xqt_bufs[p]
        tok_xqt = tok_xqt_t[t]
        tok_x2b = on("gpsimd",
                     lambda e, i_=x2_tile(t), o_=x2bF[p]: e.tensor_tensor(
                         o_, i_, b_b, ALU.add),
                     waits=(tok_bb, tok_x2[t],
                            tmp_free_att if t < 2 else tok_sqr2[t - 2]), inc=SL)
        mm_tok = None
        for chh in range(NCH):
            for half in range(2):
                rhs = s_w0[:, chh * 1024 + half * 512: chh * 1024 + (half + 1) * 512]
                waits = []
                if chh == 0 and half == 0:
                    waits = [tok_xqt, psum_free[par]]
                    if t == 0:
                        waits.append(tok_wff)
                mm_tok = on("tensor",
                            lambda e, p_=pp[:, half * 512:(half + 1) * 512],
                                   l_=xb[:, chh * 128:(chh + 1) * 128], r_=rhs,
                                   s0=(chh == 0), s1=(chh == NCH - 1):
                            e.matmul(p_, l_, r_, start=s0, stop=s1),
                            waits=waits,
                            inc=SP if (chh == NCH - 1 and half == 1) else None)
        tok_xqt_free[p] = mm_tok
        if t + 1 < NT:
            emit_xq_tp(t + 1)
        tok_o1 = on("vector",
                    lambda e, p_=pp, o_=scrF[p]: e.tensor_tensor(o_, p_[:, :], ts_b, ALU.mult),
                    waits=(mm_tok, tok_tsb), inc=SV)
        psum_free[par] = tok_o1
        drain("vector")
        tok_r2d[t] = on("vector",
                        lambda e, a_=scrF[p], b_=x2bF[p], o_=xtiF[p]:
                        e.scalar_tensor_tensor(o_, a_, 1.0, b_, ALU.mult, ALU.add,
                                               accum_out=smlF(t, 0)),
                        waits=(tok_x2b, tok_sqr2.get(t - 2)), inc=SV)
        tok_r2 = tok_r2d[t]
        drain("vector")
        tok_sqr2[t] = on("scalar",
                         lambda e, i_=xtiF[p], o_=x2bF[p]: e.activation(
                             o_, i_, AF.Square, accum_out=smlF(t, 1)),
                         waits=(tok_r2,), inc=SA)
        tok_sq1t = on("scalar",
                      lambda e: e.activation(smlF(t, 2), smlF(t, 0), AF.Square, scale=1.0 / 32.0),
                      waits=(tok_r2,), inc=SA)
        tok_t2 = on("vector",
                    lambda e: e.tensor_tensor(smlF(t, 3), smlF(t, 1), smlF(t, 2), ALU.subtract),
                    waits=(tok_sqr2[t], tok_sq1t), inc=SV)
        tok_lnv = on("scalar",
                     lambda e: e.activation(smlF(t, 6), smlF(t, 3), AF.Ln, scale=1.0 / 1024.0,
                                            bias=s_eps2[:]),
                     waits=(tok_t2,), inc=SA)
        drain("scalar")
        tok_rstd2 = on("scalar",
                       lambda e: e.activation(smlF(t, 4), smlF(t, 6), AF.Exp, scale=-0.5),
                       waits=(tok_lnv,), inc=SA)
        tok_nmr = on("vector",
                     lambda e: e.scalar_tensor_tensor(smlF(t, 5), smlF(t, 0), -1.0 / 1024.0,
                                                      smlF(t, 4), ALU.mult, ALU.mult),
                     waits=(tok_rstd2,), inc=SV)
        drain("vector")
        outv = s_tbl[:, p * 2048:p * 2048 + 2048].bitcast(F32)
        tok_out = on("vector",
                     lambda e, o_=outv, i_=xtiF[p]: e.tensor_scalar(
                         o_, i_, smlF(t, 4), smlF(t, 5), ALU.mult, ALU.add),
                     waits=(tok_out_free[p],), inc=SV)
        tok_store[t] = on("gpsimd",
                          lambda e, t_=t, o_=outv: e.dma_start(out=d_y[t_ * 128:(t_ + 1) * 128, :], in_=o_),
                          waits=(tok_out,), inc=SSE if p == 0 else SSO, n=16)
        tok_out_free[p] = tok_store[t]
        if t + 2 < NT:
            emit_quant(t + 2)

    on("gpsimd", lambda e: e.nop(), waits=((SSE, SSE.n), (SSO, SSO.n)))

    if _TRUNC:
        cut, (na, nv, np_) = marks[_TRUNC]
        for e in ENGS:
            Q[e] = Q[e][:cut[e]]
        Q["gpsimd"].append((((SA, na), (SV, nv), (SP, np_)),
                            lambda e: e.nop(), None))

    with nc.Block() as block:
        def run(eng_name):
            waited = {}

            def body(eng):
                for waits, fn, inc in Q[eng_name]:
                    for (sm, v) in waits:
                        if v > waited.get(id(sm), 0):
                            eng.wait_ge(sm.h, v)
                            waited[id(sm)] = v
                    ins = fn(eng)
                    if inc is not None:
                        ins.then_inc(inc[0], inc[1])
            return body

        block.sync(run("sync"))
        block.scalar(run("scalar"))
        block.vector(run("vector"))
        block.tensor(run("tensor"))
        block.gpsimd(run("gpsimd"))

    es.close()
    return nc


_NC_CACHE = None


def _get_nc():
    global _NC_CACHE
    if _NC_CACHE is None:
        _NC_CACHE = _build_program()
    return _NC_CACHE


def _rope_tables():
    t = np.arange(T, dtype=np.float32)
    invf = (1.0 / (10000.0 ** (np.arange(0, HD, 2, dtype=np.float64) / HD))).astype(np.float32)
    freqs = t[:, None] * invf[None, :]
    cos = np.cos(freqs.astype(np.float64)).astype(np.float32)
    sin = np.sin(freqs.astype(np.float64)).astype(np.float32)
    cosq = cos * np.float32(0.125)
    sinq = sin * np.float32(0.125)
    return cosq, sinq, cos, sin


def kernel(L0_attn_x, L0_attn_Wq, L0_attn_Wk, L0_attn_Wv, L0_attn_Wo,
           L0_attn_q_norm_w, L0_attn_k_norm_w,
           L0_ffn_weight_int8, L0_ffn_weight_scale, L0_ffn_bias):
    x = np.ascontiguousarray(np.asarray(L0_attn_x, dtype=np.float32))
    wq = np.ascontiguousarray(np.asarray(L0_attn_Wq, dtype=np.float32))
    wk = np.ascontiguousarray(np.asarray(L0_attn_Wk, dtype=np.float32))
    wv = np.ascontiguousarray(np.asarray(L0_attn_Wv, dtype=np.float32))
    wo = np.ascontiguousarray(np.asarray(L0_attn_Wo, dtype=np.float32))
    wqn = np.asarray(L0_attn_q_norm_w, dtype=np.float32).reshape(1, HD)
    wkn = np.asarray(L0_attn_k_norm_w, dtype=np.float32).reshape(1, HD)
    w8 = np.ascontiguousarray(np.asarray(L0_ffn_weight_int8).T)
    wsr = np.ascontiguousarray(np.asarray(L0_ffn_weight_scale, dtype=np.float32).reshape(1, D))
    br = np.ascontiguousarray(np.asarray(L0_ffn_bias, dtype=np.float32).reshape(1, D))
    cosq, sinq, cosk, sink = _rope_tables()
    tri = np.triu(np.ones((128, 128), dtype=np.float32)).astype(ml_dtypes.bfloat16)
    ident = np.eye(128, dtype=np.float32).astype(ml_dtypes.bfloat16)

    nc = _get_nc()
    in_maps = []
    for c in range(8):
        xc = np.ascontiguousarray(x[c])
        in_maps.append({
            "xin": xc,
            "xtin": np.ascontiguousarray(xc.T),
            "wq": wq, "wk": wk, "wv": wv, "wo": wo,
            "wfft": w8, "wsrow": wsr, "brow": br,
            "wqrow": wqn, "wkrow": wkn,
            "cosq": cosq, "sinq": sinq, "cosk": cosk, "sink": sink,
            "trimask": tri, "identb": ident,
        })
    res = run_bass_kernel_spmd(nc, in_maps, core_ids=list(range(8)))
    out = np.stack([np.asarray(res.results[c]["y"]) for c in range(8)], axis=0)
    return out


# version 16
# speedup vs baseline: 1.0862x; 1.0231x over previous
import math

import numpy as np
import ml_dtypes
import concourse.bass as bass
import concourse.mybir as mybir
from concourse.bass_utils import run_bass_kernel_spmd

F32 = mybir.dt.float32
BF16 = mybir.dt.bfloat16
I8 = mybir.dt.int8
U8 = mybir.dt.uint8
AF = mybir.ActivationFunctionType
ALU = mybir.AluOpType
AX = mybir.AxisListType.X

T = 2048
D = 1024
H = 16
HD = 64
NT = 16
NCH = 8
NQC = 4
MAGIC = 12582912.0
LN_EPS = 1e-5
RMS_EPS = 1e-6
QEPS = 1e-5
LN_0125 = math.log(0.125)

ENGS = ("sync", "scalar", "vector", "tensor", "gpsimd")

_SIM_LOCAL_CC = False
_TRUNC = 0


class _Sem:
    __slots__ = ("h", "n")

    def __init__(self, h):
        self.h = h
        self.n = 0


def _build_program():
    nc = bass.Bass("TRN2", target_bir_lowering=False)

    d_xin = nc.dram_tensor("xin", [T, D], F32, kind="ExternalInput").ap()
    d_xt = nc.dram_tensor("xtin", [D, T], F32, kind="ExternalInput").ap()
    d_wq = nc.dram_tensor("wq", [D, D], F32, kind="ExternalInput").ap()
    d_wk = nc.dram_tensor("wk", [D, D], F32, kind="ExternalInput").ap()
    d_wv = nc.dram_tensor("wv", [D, D], F32, kind="ExternalInput").ap()
    d_wo = nc.dram_tensor("wo", [D, D], F32, kind="ExternalInput").ap()
    d_wfft = nc.dram_tensor("wfft", [D, D], I8, kind="ExternalInput").ap()
    d_wsrow = nc.dram_tensor("wsrow", [1, D], F32, kind="ExternalInput").ap()
    d_brow = nc.dram_tensor("brow", [1, D], F32, kind="ExternalInput").ap()
    d_wqrow = nc.dram_tensor("wqrow", [1, HD], F32, kind="ExternalInput").ap()
    d_wkrow = nc.dram_tensor("wkrow", [1, HD], F32, kind="ExternalInput").ap()
    d_cosq = nc.dram_tensor("cosq", [T, 32], F32, kind="ExternalInput").ap()
    d_sinq = nc.dram_tensor("sinq", [T, 32], F32, kind="ExternalInput").ap()
    d_cosk = nc.dram_tensor("cosk", [T, 32], F32, kind="ExternalInput").ap()
    d_sink = nc.dram_tensor("sink", [T, 32], F32, kind="ExternalInput").ap()
    d_trimask = nc.dram_tensor("trimask", [128, 128], BF16, kind="ExternalInput").ap()
    d_ident = nc.dram_tensor("identb", [128, 128], BF16, kind="ExternalInput").ap()
    d_y = nc.dram_tensor("y", [T, D], F32, kind="ExternalOutput").ap()

    d_ccin = nc.dram_tensor("ccin", [1, 1], F32)
    d_ccout = nc.dram_tensor("ccout", [1, 1], F32, addr_space="Shared")

    from contextlib import ExitStack
    es = ExitStack()
    S = es.enter_context

    arena = S(nc.sbuf_tensor("arena", [128, 65536], U8))
    QT0 = 32768
    s_kt = S(nc.sbuf_tensor("s_kt", [128, 16 * 1024], BF16))
    s_v = S(nc.sbuf_tensor("s_v", [128, NT * (H * 65)], BF16))
    s_w0 = S(nc.sbuf_tensor("s_w0", [128, NCH * 1024], BF16))
    s_w1 = S(nc.sbuf_tensor("s_w1", [128, NCH * 1024], BF16))
    s_pt = S(nc.sbuf_tensor("s_pt", [128, 2048], BF16))
    s_an = S(nc.sbuf_tensor("s_an", [128, 4096], BF16))
    s_at = S(nc.sbuf_tensor("s_at", [128, 4096], BF16))
    s_xti = S(nc.sbuf_tensor("s_xti", [128, 1024], F32))
    s_xqt = S(nc.sbuf_tensor("s_xqt", [128, 1024], BF16))
    s_rq = S(nc.sbuf_tensor("s_rq", [128, 1024], BF16))
    s_scr = S(nc.sbuf_tensor("s_scr", [128, 1024], F32))
    s_tmp = S(nc.sbuf_tensor("s_tmp", [128, 2048], BF16))
    s_tbl = S(nc.sbuf_tensor("s_tbl", [128, 4096], BF16))
    s_tri = S(nc.sbuf_tensor("s_tri", [128, 128], BF16))
    s_id = S(nc.sbuf_tensor("s_id", [128, 128], BF16))
    s_ms = S(nc.sbuf_tensor("s_ms", [128, 16], F32))
    s_rstd = S(nc.sbuf_tensor("s_rstd", [128, 16], F32))
    s_rstd2 = S(nc.sbuf_tensor("s_rstd2", [128, 16], F32))
    s_wqb = S(nc.sbuf_tensor("s_wqb", [128, 64], F32))
    s_wkb = S(nc.sbuf_tensor("s_wkb", [128, 64], F32))
    s_den = S(nc.sbuf_tensor("s_den", [128, 4], F32))
    s_rec = S(nc.sbuf_tensor("s_rec", [128, 4], F32))
    s_amax = S(nc.sbuf_tensor("s_amax", [128, 16], F32))
    s_am128 = S(nc.sbuf_tensor("s_am128", [1, 128], F32))
    s_sml = S(nc.sbuf_tensor("s_sml", [128, 16], F32))
    s_eps1 = S(nc.sbuf_tensor("s_eps1", [128, 1], F32))
    s_eps2 = S(nc.sbuf_tensor("s_eps2", [128, 1], F32))
    s_ones = S(nc.sbuf_tensor("s_ones", [1, 128], F32))
    s_ln8 = S(nc.sbuf_tensor("s_ln8", [1, 1], F32))
    s_mh = S(nc.sbuf_tensor("s_mh", [1, 4], F32))
    s_nmh = S(nc.sbuf_tensor("s_nmh", [128, 1], F32))
    s_xs = S(nc.sbuf_tensor("s_xs", [1, 2], F32))
    s_xsb = S(nc.sbuf_tensor("s_xsb", [128, 2], F32))

    p_a = S(nc.psum_tensor("p_a", [128, 1024], F32))
    p_b = S(nc.psum_tensor("p_b", [128, 1024], F32))
    p_c = S(nc.psum_tensor("p_c", [128, 1024], F32))
    p_t1 = S(nc.psum_tensor("p_t1", [128, 1024], BF16))
    p_t2 = S(nc.psum_tensor("p_t2", [128, 1024], BF16))

    sems = {}
    for name in ("lw0", "lw1", "lwx", "ldx", "ldg", "ldg2", "pe", "act",
                 "dve", "pool", "stE", "stO", "scc", "cc", "xld"):
        sems[name] = _Sem(S(nc.semaphore(name)))
    SW0, SW1, SWX, SX, SG, SG2, SP, SA, SV, SL, SSE, SSO, SCC, SC, SXL = (
        sems[n] for n in ("lw0", "lw1", "lwx", "ldx", "ldg", "ldg2", "pe",
                          "act", "dve", "pool", "stE", "stO", "scc", "cc",
                          "xld"))

    Q = {e: [] for e in ENGS}

    def on(eng, fn, waits=(), inc=None, n=1):
        tok = None
        rec_inc = None
        if inc is not None:
            inc.n += n
            tok = (inc, inc.n)
            rec_inc = (inc.h, n)
        Q[eng].append((tuple(w for w in waits if w is not None), fn, rec_inc))
        return tok

    def drain(eng):
        on(eng, lambda e: e.drain())

    def xt_blk(ch, t):
        base = ch * 4096 + t * 256
        return arena[:, base:base + 256].bitcast(BF16)

    def xt_full():
        return arena[:, 0:32768].bitcast(BF16)

    def qt_win(qc, ch):
        off = QT0 + qc * 8192 + ch * 1024
        return arena[:, off:off + 1024].bitcast(BF16)

    def qt_blk(qc, ch, tl):
        off = QT0 + qc * 8192 + ch * 1024 + tl * 256
        return arena[:, off:off + 256].bitcast(BF16)

    def kt_blk(j, ch):
        return s_kt[:, j * 1024 + ch * 128: j * 1024 + (ch + 1) * 128]

    def x2_tile(t):
        if t < 8:
            return arena[:, t * 4096:(t + 1) * 4096].bitcast(F32)
        return arena[:, QT0 + (t - 8) * 4096: QT0 + (t - 7) * 4096].bitcast(F32)

    def tmp_f32():
        return s_tmp[:, :].bitcast(F32)

    sml = lambda i: s_sml[:, i:i + 1]

    on("vector", lambda e: e.memset(s_eps1[:], RMS_EPS))
    on("vector", lambda e: e.memset(s_eps2[:], LN_EPS))
    on("vector", lambda e: e.memset(s_ones[:], 1.0))
    on("vector", lambda e: e.memset(s_ln8[:], LN_0125))
    v3 = s_v[:, :].rearrange("p (g j) -> p g j", j=65)
    tok_vones = on("vector", lambda e: e.memset(v3[:, :, 64:65], 1.0), inc=SV)

    on("sync", lambda e: e.dma_start(out=s_tri[:], in_=d_trimask), inc=SX, n=16)
    on("sync", lambda e: e.dma_start(out=s_id[:], in_=d_ident), inc=SX, n=16)
    stg = s_an[:, :].bitcast(F32)
    stg_v = [stg[:, i * 512:(i + 1) * 512] for i in range(4)]
    for i, dt_ in enumerate((d_cosq, d_sinq, d_cosk, d_sink)):
        src = dt_.rearrange("(n p) j -> p n j", p=128)
        dst = stg_v[i].rearrange("p (n j) -> p n j", j=32)
        on("sync", lambda e, s=src, d=dst: e.dma_start(out=d, in_=s),
           inc=SX, n=16)
    tok_tri = tok_id = (SX, SX.n)
    cs_toks = [(SX, SX.n)] * 4
    on("gpsimd", lambda e: e.dma_start(out=s_wqb[:], in_=d_wqrow.broadcast_to([128, HD])), inc=SG, n=16)
    on("gpsimd", lambda e: e.dma_start(out=s_wkb[:], in_=d_wkrow.broadcast_to([128, HD])), inc=SG, n=16)
    tok_wqb = tok_wkb = (SG, SG.n)
    def wload(dst_sb, src, sem, waits=()):
        src_r = src.rearrange("(c p) n -> p c n", p=128)
        dst_r = dst_sb[:, :].rearrange("p (c n) -> p c n", n=1024)
        return on("gpsimd", lambda e: e.dma_start(out=dst_r, in_=src_r),
                  waits=waits, inc=sem, n=16)
    tok_wq = wload(s_w0, d_wq, SW0)
    tok_wk = wload(s_w1, d_wk, SW1)
    xt_src = d_xt.rearrange("(c p) t -> p c t", p=128)
    xt_dst = xt_full().rearrange("p (c t) -> p c t", t=2048)
    tok_xt = on("gpsimd", lambda e: e.dma_start(out=xt_dst, in_=xt_src), inc=SWX, n=16)

    tbl = [s_tbl[:, i * 1024:(i + 1) * 1024] for i in range(4)]
    tok_tbl = None
    for m, (ci, si_, wb, wtok) in enumerate(((0, 1, s_wqb, tok_wqb), (2, 3, s_wkb, tok_wkb))):
        for tix, fi in ((2 * m, ci), (2 * m + 1, si_)):
            dst3 = tbl[tix].rearrange("p (t j) -> p t j", j=64)
            src3 = stg_v[fi].rearrange("p (t j) -> p t j", j=32)
            for half in range(2):
                wv_ = wb[:, half * 32:(half + 1) * 32].unsqueeze(1).broadcast_to([128, 16, 32])
                dsth = dst3[:, :, half * 32:(half + 1) * 32]
                tok_tbl = on("vector",
                             lambda e, d_=dsth, s_=src3, w_=wv_: e.tensor_tensor(d_, s_, w_, ALU.mult),
                             waits=(cs_toks[fi], wtok), inc=SV)

    tok_sq1 = on("scalar", lambda e: e.activation(s_scr[0:1, 0:64], s_wqb[0:1, :], AF.Square,
                                                  accum_out=s_mh[:, 0:1]),
                 waits=(tok_wqb,), inc=SA)
    tok_sq2 = on("scalar", lambda e: e.activation(s_scr[0:1, 64:128], s_wkb[0:1, :], AF.Square,
                                                  accum_out=s_mh[:, 1:2]),
                 waits=(tok_wkb,), inc=SA)
    tok_t1 = on("vector", lambda e: e.tensor_tensor(s_mh[:, 2:3], s_mh[:, 0:1], s_mh[:, 1:2], ALU.mult),
                waits=(tok_sq1, tok_sq2), inc=SV)
    tok_lnm = on("scalar", lambda e: e.activation(s_mh[:, 3:4], s_mh[:, 2:3], AF.Ln),
                 waits=(tok_t1,), inc=SA)
    drain("scalar")
    tok_mh = on("scalar", lambda e: e.activation(s_mh[:, 2:3], s_mh[:, 3:4], AF.Exp,
                                                 scale=0.5, bias=s_ln8[:]),
                waits=(tok_lnm,), inc=SA)
    drain("scalar")
    tok_nmh0 = on("scalar", lambda e: e.mul(s_mh[:, 3:4], s_mh[:, 2:3], -1.0),
                  waits=(tok_mh,), inc=SA)
    tok_nmhmm = on("tensor", lambda e: e.matmul(p_c[:, 0:1], s_ones[:], s_mh[0:1, 3:4],
                                                start=True, stop=True),
                   waits=(tok_nmh0,), inc=SP)
    tok_nmh = on("scalar", lambda e: e.activation(s_nmh[:], p_c[:, 0:1], AF.Copy),
                 waits=(tok_nmhmm,), inc=SA)

    psum_slots = (p_a, p_b, p_c)
    slot_free = [None, None, tok_nmh]
    tp_banks = (p_t1, p_t2)
    bank_free = [None, None]
    bank_idx = [0]

    def tp_group(pairs, ev_eng, first_waits=(), ev_first_waits=()):
        b = bank_idx[0] % 2
        bank_idx[0] += 1
        pt = tp_banks[b]
        ev_sem = SA if ev_eng == "scalar" else SV
        tps = []
        for i, (src, dst) in enumerate(pairs):
            w = [bank_free[b]] + list(first_waits) if i == 0 else []
            tp = on("tensor",
                    lambda e, o_=pt[:, i * 128:(i + 1) * 128], s_=src:
                    e.transpose(o_, s_, s_id[:]),
                    waits=w, inc=SP if i == len(pairs) - 1 else None)
            tps.append((tp, pt[:, i * 128:(i + 1) * 128], dst))
        last_tp = tps[-1][0]
        last_ev = None
        for i, (_, slot, dst) in enumerate(tps):
            w = [last_tp] + list(ev_first_waits) if i == 0 else []
            if ev_eng == "scalar":
                last_ev = on("scalar",
                             lambda e, d_=dst, s_=slot: e.activation(d_, s_, AF.Copy),
                             waits=w, inc=ev_sem)
            else:
                last_ev = on("vector",
                             lambda e, d_=dst, s_=slot: e.tensor_copy(d_, s_),
                             waits=w, inc=ev_sem)
        bank_free[b] = last_ev
        return last_tp, last_ev

    tok_vevac_last = None
    tok_rq_free = None
    tok_qh_free = None
    tok_rstd_free = None
    tok_scr_free = None
    tok_tmp_free = None
    last_proj_tok = {}
    pending = [None]

    def emit_qk_tp():
        nonlocal tok_rq_free
        if pending[0] is None:
            return
        tok_rq, mat, t = pending[0]
        pending[0] = None
        pairs = []
        for ch in range(NCH):
            if mat == "q":
                dst = qt_blk(t // 4, ch, t % 4)
            else:
                dst = kt_blk(t, ch)
            pairs.append((s_rq[:, ch * 128:(ch + 1) * 128], dst))
        last_tp, _ = tp_group(pairs, "scalar", first_waits=(tok_rq, tok_id))
        tok_rq_free = last_tp

    si = 0
    for m_i, mat in enumerate(("q", "k", "v")):
        wbuf = s_w0 if mat in ("q", "v") else s_w1
        wtok = {"q": tok_wq, "k": tok_wk}.get(mat)
        if mat == "v":
            wtok = wload(s_w0, d_wv, SW0, waits=(last_proj_tok["q"],))
        tbl_c = tbl[0] if mat == "q" else tbl[2]
        tbl_s = tbl[1] if mat == "q" else tbl[3]
        for t in range(NT):
            slot = si % 3
            si += 1
            pp = psum_slots[slot]
            mm_tok = None
            for ch in range(NCH):
                for half in range(2):
                    w_v = wbuf[:, ch * 1024 + half * 512: ch * 1024 + (half + 1) * 512]
                    waits = []
                    if ch == 0 and half == 0:
                        waits = [slot_free[slot], wtok, tok_xt]
                    mm_tok = on("tensor",
                                lambda e, p_=pp[:, half * 512:(half + 1) * 512],
                                       l_=xt_blk(ch, t), r_=w_v, s0=(ch == 0), s1=(ch == NCH - 1):
                                e.matmul(p_, l_, r_, start=s0, stop=s1),
                                waits=waits,
                                inc=SP if (ch == NCH - 1 and half == 1) else None)
            last_proj_tok[mat] = mm_tok
            emit_qk_tp()
            pview = pp[:, :].rearrange("p (h j) -> p h j", j=64)

            if mat == "v":
                dst = s_v[:, t * (H * 65):(t + 1) * (H * 65)].rearrange(
                    "p (h j) -> p h j", j=65)[:, :, 0:64]
                tok_vevac_last = on("vector",
                                    lambda e, d_=dst, s_=pview: e.tensor_copy(d_, s_),
                                    waits=(mm_tok, tok_vones), inc=SV)
                slot_free[slot] = tok_vevac_last
                continue

            sqv = tmp_f32()
            tok_sq = on("scalar", lambda e, s_=pp[:, :], d_=sqv: e.activation(d_, s_, AF.Square),
                        waits=(mm_tok, tok_tmp_free), inc=SA)
            tok_ms = on("vector",
                        lambda e: e.tensor_reduce(
                            s_ms[:, :].rearrange("p (h o) -> p h o", o=1),
                            tmp_f32().rearrange("p (h j) -> p h j", j=64), AX, ALU.add),
                        waits=(tok_sq, tok_rstd_free), inc=SV)
            tok_ln = on("scalar", lambda e: e.activation(s_rstd[:], s_ms[:], AF.Ln,
                                                         scale=1.0 / 64.0, bias=s_eps1[:]),
                        waits=(tok_ms,), inc=SA)
            drain("scalar")
            tok_rstd = on("scalar", lambda e: e.activation(s_rstd2[:], s_rstd[:], AF.Exp, scale=-0.5),
                          waits=(tok_ln,), inc=SA)
            rb = s_rstd2[:, :].rearrange("p (h o) -> p h o", o=1).broadcast_to([128, 16, 64])
            qh3 = s_xqt[:, :].rearrange("p (h j) -> p h j", j=64)
            tok_qh = on("vector",
                        lambda e, o_=qh3, i_=pview, b_=rb: e.tensor_tensor(o_, i_, b_, ALU.mult),
                        waits=(tok_rstd, tok_qh_free), inc=SV)
            tok_rstd_free = tok_qh
            slot_free[slot] = tok_qh
            qq = s_xqt[:, :].rearrange("p (h two j) -> p h two j", two=2, j=32)
            q1v, q2v = qq[:, :, 0, :], qq[:, :, 1, :]
            ct = tbl_c[:, t * 64:(t + 1) * 64]
            st_ = tbl_s[:, t * 64:(t + 1) * 64]
            c1 = ct[:, 0:32].unsqueeze(1).broadcast_to([128, 16, 32])
            c2 = ct[:, 32:64].unsqueeze(1).broadcast_to([128, 16, 32])
            s1v = st_[:, 0:32].unsqueeze(1).broadcast_to([128, 16, 32])
            s2v = st_[:, 32:64].unsqueeze(1).broadcast_to([128, 16, 32])
            tmp3 = s_tmp[:, :].rearrange("p (k j) -> p k j", j=512)
            ta = tmp3[:, 0, :].rearrange("p (h j) -> p h j", j=32)
            tb = tmp3[:, 1, :].rearrange("p (h j) -> p h j", j=32)
            tc = tmp3[:, 2, :].rearrange("p (h j) -> p h j", j=32)
            td = tmp3[:, 3, :].rearrange("p (h j) -> p h j", j=32)
            drain("vector")
            on("vector", lambda e, o_=ta, a_=q1v, b_=c1: e.tensor_tensor(o_, a_, b_, ALU.mult),
               waits=(tok_tbl,))
            on("vector", lambda e, o_=tb, a_=q2v, b_=s2v: e.tensor_tensor(o_, a_, b_, ALU.mult))
            on("vector", lambda e, o_=tc, a_=q2v, b_=c2: e.tensor_tensor(o_, a_, b_, ALU.mult))
            on("vector", lambda e, o_=td, a_=q1v, b_=s1v: e.tensor_tensor(o_, a_, b_, ALU.mult))
            drain("vector")
            rq3 = s_rq[:, :].rearrange("p (h two j) -> p h two j", two=2, j=32)
            on("vector", lambda e, o_=rq3[:, :, 0, :], a_=ta, b_=tb: e.tensor_tensor(o_, a_, b_, ALU.subtract),
               waits=(tok_rq_free,))
            tok_rq = on("vector", lambda e, o_=rq3[:, :, 1, :], a_=tc, b_=td: e.tensor_tensor(o_, a_, b_, ALU.add),
                        inc=SV)
            tok_qh_free = tok_rq
            tok_tmp_free = tok_rq
            pending[0] = (tok_rq, mat, t)

    tok_qk_ready = (SA, SA.n)
    marks = {1: ({e: len(Q[e]) for e in ENGS}, (SA.n, SV.n, SP.n))}

    tok_wo = wload(s_w1, d_wo, SW1, waits=(last_proj_tok["k"],))
    tok_wff = wload(s_w0, d_wfft, SW0, waits=(last_proj_tok["v"],))

    ph_bufs = (s_pt[:, 0:1024], s_pt[:, 1024:2048], s_xqt[:, :], s_rq[:, :])
    pth_free4 = [None] * 4
    pc_free = [slot_free[2], slot_free[2]]
    psum_free = {"a": slot_free[0], "b": slot_free[1]}
    tmp_free_att = tok_tmp_free
    an_free = None
    tok_r_evac = {}
    tok_x2 = {}
    tok_amax = {}
    tok_xdma = None
    tok_xti_free = None
    att_gate = [tok_qk_ready, tok_vevac_last, tok_nmh, tok_tri]
    first_s_done = False
    wo_tok_prev = None
    tok_sqr1_last = None

    for c in range(NQC):
        an_toks = {}
        at_last_tp = [None]
        at_first = [True]

        def emit_at_tp(m, c=c, an_toks=an_toks, at_last_tp=at_last_tp,
                       at_first=at_first):
            pairs = []
            for tl in range(4):
                src = s_an[:, tl * 1024 + m * 128: tl * 1024 + (m + 1) * 128]
                dst = s_at[:, m * 512 + tl * 128: m * 512 + (tl + 1) * 128]
                pairs.append((src, dst))
            evw = ()
            if at_first[0]:
                evw = (wo_tok_prev,)
                at_first[0] = False
            tp, _ = tp_group(pairs, "vector",
                             first_waits=(an_toks[2 * m + 1],),
                             ev_first_waits=evw)
            at_last_tp[0] = tp

        for h in range(H):
            if h % 2 == 1 and h >= 3:
                emit_at_tp((h - 3) // 2)
            ch = h // 2
            hb = (h % 2) * 64
            pcb = (h % 2) * 512
            jmax = 4 * (c + 1)
            tok_avlast = None

            def emit_av(g, etok, mtoks, pth, h=h, c=c, pcb=pcb, jmax=jmax):
                nonlocal tok_avlast
                par = g % 2
                for jj in range(2):
                    j = 2 * g + jj
                    rhs_v = s_v[:, j * (H * 65) + h * 65: j * (H * 65) + (h + 1) * 65]
                    for tl in range(max(0, j - 4 * c), 4):
                        diag = (j - 4 * c == tl)
                        if diag:
                            lhs = s_tmp[:, tl * 128:(tl + 1) * 128]
                        else:
                            lhs = pth[:, jj * 512 + tl * 128: jj * 512 + (tl + 1) * 128]
                        waits = [etok]
                        if diag:
                            waits.append(mtoks[j])
                        s0 = (g == 0 and jj == 0 and tl == 0)
                        if s0:
                            waits.append(pc_free[h % 2])
                        tok_avlast = on("tensor",
                                        lambda e, p_=p_c[:, pcb + tl * 65: pcb + (tl + 1) * 65],
                                               l_=lhs, r_=rhs_v, s0=s0, s1=(j == 4 * c + tl):
                                        e.matmul(p_, l_, r_, start=s0, stop=s1,
                                                 skip_group_check=True),
                                        waits=waits,
                                        inc=SP if (j == jmax - 1 and tl == 3) else None)
                pth_free4[g % 4] = tok_avlast

            pend_av = []
            for g in range(jmax // 2):
                par = g % 2
                ps = p_a if par == 0 else p_b
                pth = ph_bufs[g % 4]
                qs = [max(0, 2 * g + jj - 4 * c) * 128 for jj in range(2)]
                stok = None
                for jj in range(2):
                    j = 2 * g + jj
                    lhs = kt_blk(j, ch)[hb:hb + 64, :]
                    rhs_q = qt_win(c, ch)[hb:hb + 64, qs[jj]:512]
                    waits = []
                    if jj == 0:
                        waits = [psum_free["a" if par == 0 else "b"]]
                        if not first_s_done:
                            waits += att_gate
                            first_s_done = True
                    stok = on("tensor",
                              lambda e, p_=ps[:, jj * 512 + qs[jj]:(jj + 1) * 512],
                                     l_=lhs, r_=rhs_q:
                              e.matmul(p_, l_, r_, start=True, stop=True),
                              waits=waits, inc=SP if jj == 1 else None)
                qs0 = qs[0]
                pv = ps[:, :].rearrange("p (two q) -> p two q", q=512)[:, :, qs0:512]
                ev_ = pth[:, :].rearrange("p (two q) -> p two q", q=512)[:, :, qs0:512]
                etok = on("scalar",
                          lambda e, o_=ev_, i_=pv: e.activation(o_, i_, AF.Exp, bias=s_nmh[:]),
                          waits=(stok, pth_free4[g % 4]), inc=SA)
                psum_free["a" if par == 0 else "b"] = etok
                mtoks = {}
                for jj in range(2):
                    j = 2 * g + jj
                    if j >= 4 * c:
                        tld = j - 4 * c
                        blk = pth[:, jj * 512 + tld * 128: jj * 512 + (tld + 1) * 128]
                        mdst = s_tmp[:, tld * 128:(tld + 1) * 128]
                        mtoks[j] = on("gpsimd",
                                      lambda e, o_=mdst, a_=blk, b_=s_tri[:]:
                                      e.tensor_tensor(o_, a_, b_, ALU.mult),
                                      waits=(etok, tmp_free_att), inc=SL)
                if len(pend_av) == 2:
                    emit_av(*pend_av.pop(0))
                pend_av.append((g, etok, mtoks, pth))
            for pa in pend_av:
                emit_av(*pa)
            tmp_free_att = tok_avlast
            pcv = p_c[:, pcb:pcb + 260].rearrange("p (t j) -> p t j", j=65)
            tok_den = on("vector",
                         lambda e, v_=pcv: e.tensor_copy(
                             s_den[:, :].rearrange("p (t o) -> p t o", o=1), v_[:, :, 64:65]),
                         waits=(tok_avlast, an_free), inc=SV)
            drain("vector")
            tok_recip = on("vector", lambda e: e.reciprocal(s_rec[:], s_den[:]), inc=SV)
            drain("vector")
            an4 = s_an[:, :].rearrange("p (t hh j) -> p t hh j", hh=16, j=64)
            recb = s_rec[:, :].rearrange("p (t o) -> p t o", o=1).broadcast_to([128, 4, 64])
            tok_an = on("vector",
                        lambda e, o_=an4[:, :, h, :], i_=pcv, b_=recb:
                        e.tensor_tensor(o_, i_[:, :, 0:64], b_, ALU.mult),
                        inc=SV)
            pc_free[h % 2] = tok_an
            an_toks[h] = tok_an

        emit_at_tp(7)
        an_free = at_last_tp[0]
        tok_at_ready = (SV, SV.n)

        for tl in range(4):
            t = c * 4 + tl
            par = "a" if tl % 2 == 0 else "b"
            pp = p_a if par == "a" else p_b
            tok_xdma = on("sync",
                          lambda e, t_=t: e.dma_start(out=s_xti[:], in_=d_xin[t_ * 128:(t_ + 1) * 128, :]),
                          waits=(tok_xti_free,), inc=SXL, n=16)
            wo_tok = None
            for chh in range(NCH):
                for half in range(2):
                    lhs = s_at[:, chh * 512 + tl * 128: chh * 512 + (tl + 1) * 128]
                    rhs = s_w1[:, chh * 1024 + half * 512: chh * 1024 + (half + 1) * 512]
                    waits = []
                    if chh == 0 and half == 0:
                        waits = [tok_at_ready, psum_free[par]]
                        if c == 0 and tl == 0:
                            waits.append(tok_wo)
                    wo_tok = on("tensor",
                                lambda e, p_=pp[:, half * 512:(half + 1) * 512], l_=lhs, r_=rhs,
                                       s0=(chh == 0), s1=(chh == NCH - 1):
                                e.matmul(p_, l_, r_, start=s0, stop=s1),
                                waits=waits,
                                inc=SP if (chh == NCH - 1 and half == 1) else None)
            tok_r = on("vector",
                       lambda e, p_=pp: e.scalar_tensor_tensor(
                           s_scr[:], s_xti[:], 1.0, p_[:, :], ALU.mult, ALU.add,
                           accum_out=sml(0)),
                       waits=(wo_tok, tok_xdma, tok_scr_free), inc=SV)
            psum_free[par] = tok_r
            tok_r_evac[t] = tok_r
            drain("vector")
            tok_sqr = on("scalar",
                         lambda e: e.activation(s_xti[:], s_scr[:], AF.Square, accum_out=sml(1)),
                         waits=(tok_r,), inc=SA)
            tok_xti_free = tok_sqr
            tok_sqr1_last = tok_sqr
            tok_sq1t = on("scalar",
                          lambda e: e.activation(sml(2), sml(0), AF.Square, scale=1.0 / 32.0),
                          waits=(tok_r,), inc=SA)
            tok_t2 = on("vector",
                        lambda e: e.tensor_tensor(sml(3), sml(1), sml(2), ALU.subtract),
                        waits=(tok_sqr, tok_sq1t), inc=SV)
            tok_lnv = on("scalar",
                         lambda e: e.activation(sml(6), sml(3), AF.Ln, scale=1.0 / 1024.0,
                                                bias=s_eps2[:]),
                         waits=(tok_t2,), inc=SA)
            drain("scalar")
            tok_rstd2 = on("scalar",
                           lambda e: e.activation(sml(4), sml(6), AF.Exp, scale=-0.5),
                           waits=(tok_lnv,), inc=SA)
            tok_nmr = on("vector",
                         lambda e: e.scalar_tensor_tensor(sml(5), sml(0), -1.0 / 1024.0, sml(4),
                                                          ALU.mult, ALU.mult),
                         waits=(tok_rstd2,), inc=SV)
            drain("vector")
            tok_x2[t] = on("vector",
                           lambda e, o_=x2_tile(t): e.tensor_scalar(
                               o_, s_scr[:], sml(4), sml(5), ALU.mult, ALU.add),
                           inc=SV)
            tok_scr_free = tok_x2[t]
            tok_amax[t] = on("vector",
                             lambda e, o_=s_amax[:, t:t + 1], i_=x2_tile(t):
                             e.tensor_reduce(o_, i_, AX, ALU.max,
                                             apply_absolute_value=True),
                             waits=(tok_x2[t],), inc=SV)
        wo_tok_prev = wo_tok

    marks[2] = ({e: len(Q[e]) for e in ENGS}, (SA.n, SV.n, SP.n))

    ws_b = s_at[:, 0:2048].bitcast(F32)
    b_b = s_at[:, 2048:4096].bitcast(F32)
    on("gpsimd", lambda e: e.dma_start(out=ws_b, in_=d_wsrow.broadcast_to([128, D])),
       waits=(wo_tok,), inc=SG2, n=16)
    on("gpsimd", lambda e: e.dma_start(out=b_b, in_=d_brow.broadcast_to([128, D])),
       waits=(wo_tok,), inc=SG2, n=16)
    tok_wsb = tok_bb = (SG2, SG2.n)

    tok_amc = on("vector", lambda e: e.tensor_reduce(sml(7), s_amax[:], AX, ALU.max),
                 waits=(tok_amax[NT - 1],), inc=SV)
    tok_gl = on("gpsimd",
                lambda e: e.tensor_reduce(s_xs[0:1, 0:1], sml(7),
                                          mybir.AxisListType.XYZWC, ALU.max),
                waits=(tok_amc,), inc=SL)
    if _SIM_LOCAL_CC:
        tok_cdn = on("gpsimd", lambda e: e.tensor_copy(s_am128[0:1, 0:1], s_xs[0:1, 0:1]),
                     waits=(tok_gl,), inc=SL)
    else:
        tok_cup = on("gpsimd", lambda e: e.dma_start(out=d_ccin.ap(), in_=s_xs[0:1, 0:1]),
                     waits=(tok_gl,), inc=SCC, n=16)
        tok_cc = on("gpsimd", lambda e: e.collective_compute(
            "AllReduce", ALU.max, replica_groups=[list(range(8))],
            ins=[d_ccin.ap()], outs=[d_ccout.ap()]),
            waits=(tok_cup,), inc=SC, n=1)
        tok_cdn = on("gpsimd", lambda e: e.dma_start(out=s_am128[0:1, 0:1], in_=d_ccout.ap()),
                     waits=(tok_cc,), inc=SCC, n=16)
    tok_xsv = on("vector", lambda e: e.tensor_scalar(s_xs[:, 0:1], s_am128[0:1, 0:1],
                                                     float(np.float32(1.0) / np.float32(127.0)),
                                                     QEPS, ALU.mult, ALU.max),
                 waits=(tok_cdn,), inc=SV)
    drain("vector")
    tok_rxs = on("vector", lambda e: e.reciprocal(s_xs[:, 1:2], s_xs[:, 0:1]),
                 waits=(tok_xsv,), inc=SV)
    tok_bc1 = on("tensor", lambda e: e.matmul(p_c[:, 2:3], s_ones[:], s_xs[0:1, 0:1],
                                              start=True, stop=True),
                 waits=(tok_rxs, pc_free[0]), inc=SP)
    tok_bc2 = on("tensor", lambda e: e.matmul(p_c[:, 3:4], s_ones[:], s_xs[0:1, 1:2],
                                              start=True, stop=True),
                 waits=(tok_rxs, pc_free[1]), inc=SP)
    tok_xsb = on("scalar", lambda e: e.activation(s_xsb[:, 0:1], p_c[:, 2:3], AF.Copy),
                 waits=(tok_bc1,), inc=SA)
    tok_rxsb = on("scalar", lambda e: e.activation(s_xsb[:, 1:2], p_c[:, 3:4], AF.Copy),
                  waits=(tok_bc2,), inc=SA)
    ts_b = s_an[:, 0:2048].bitcast(F32)
    tok_tsb = on("vector", lambda e: e.tensor_scalar(ts_b, ws_b, s_xsb[:, 0:1], None, ALU.mult),
                 waits=(tok_wsb, tok_xsb, an_free), inc=SV)

    xq_all = s_kt[:, :]
    vf32 = s_v[:, :].bitcast(F32)
    scrF = (vf32[:, 0:1024], vf32[:, 1024:2048])
    tmpF = (vf32[:, 2048:3072], vf32[:, 3072:4096])
    x2bF = (vf32[:, 4096:5120], vf32[:, 5120:6144])
    xtiF = (vf32[:, 6144:7168], vf32[:, 7168:8192])
    smlF = lambda t, i: s_sml[:, (t % 2) * 8 + i:(t % 2) * 8 + i + 1]

    tok_q2 = {}
    tok_q3 = {}
    tok_r2d = {}
    tok_sqr2 = {}

    def emit_quant(t):
        p = t % 2
        tok_q1 = on("vector",
                    lambda e, i_=x2_tile(t), o_=scrF[p]: e.tensor_scalar(
                        o_, i_, s_xsb[:, 1:2], MAGIC, ALU.mult, ALU.add),
                    waits=(tok_rxsb, tok_x2[t],
                           tmp_free_att if t < 2 else tok_r2d[t - 2]), inc=SV)
        tok_q2[t] = on("scalar",
                       lambda e, i_=scrF[p], o_=tmpF[p]: e.activation(
                           o_, i_, AF.Copy, bias=-MAGIC),
                       waits=(tok_q1, tok_q3.get(t - 2)), inc=SA)
        tok_q3[t] = on("vector",
                       lambda e, i_=tmpF[p], o_=xq_all[:, t * 1024:(t + 1) * 1024]:
                       e.tensor_scalar(o_, i_, 127.0, -128.0, ALU.min, ALU.max),
                       waits=(tok_q2[t],), inc=SV)

    emit_quant(0)
    emit_quant(1)

    marks[3] = ({e: len(Q[e]) for e in ENGS}, (SA.n, SV.n, SP.n))

    xqt_bufs = (s_xqt, s_rq)
    tok_xqt_free = [None, None]
    tok_store = {}
    tok_out_free = [None, None]
    tok_xqt_t = {}

    def emit_xq_tp(t):
        xb = xqt_bufs[t % 2]
        pairs = []
        for chh in range(NCH):
            src = xq_all[:, t * 1024 + chh * 128: t * 1024 + (chh + 1) * 128]
            dst = xb[:, chh * 128:(chh + 1) * 128]
            pairs.append((src, dst))
        tp_group(pairs, "scalar", first_waits=(tok_q3[t],),
                 ev_first_waits=(tok_xqt_free[t % 2],)
                 if t >= 2 else (tmp_free_att,))
        tok_xqt_t[t] = (SA, SA.n)

    emit_xq_tp(0)
    for t in range(NT):
        p = t % 2
        par = "a" if p == 0 else "b"
        pp = p_a if par == "a" else p_b
        xb = xqt_bufs[p]
        tok_xqt = tok_xqt_t[t]
        tok_x2b = on("gpsimd",
                     lambda e, i_=x2_tile(t), o_=x2bF[p]: e.tensor_tensor(
                         o_, i_, b_b, ALU.add),
                     waits=(tok_bb, tok_x2[t],
                            tmp_free_att if t < 2 else tok_sqr2[t - 2]), inc=SL)
        mm_tok = None
        for chh in range(NCH):
            for half in range(2):
                rhs = s_w0[:, chh * 1024 + half * 512: chh * 1024 + (half + 1) * 512]
                waits = []
                if chh == 0 and half == 0:
                    waits = [tok_xqt, psum_free[par]]
                    if t == 0:
                        waits.append(tok_wff)
                mm_tok = on("tensor",
                            lambda e, p_=pp[:, half * 512:(half + 1) * 512],
                                   l_=xb[:, chh * 128:(chh + 1) * 128], r_=rhs,
                                   s0=(chh == 0), s1=(chh == NCH - 1):
                            e.matmul(p_, l_, r_, start=s0, stop=s1),
                            waits=waits,
                            inc=SP if (chh == NCH - 1 and half == 1) else None)
        tok_xqt_free[p] = mm_tok
        if t + 1 < NT:
            emit_xq_tp(t + 1)
        tok_o1 = on("vector",
                    lambda e, p_=pp, o_=scrF[p]: e.tensor_tensor(o_, p_[:, :], ts_b, ALU.mult),
                    waits=(mm_tok, tok_tsb), inc=SV)
        psum_free[par] = tok_o1
        drain("vector")
        tok_r2d[t] = on("vector",
                        lambda e, a_=scrF[p], b_=x2bF[p], o_=xtiF[p]:
                        e.scalar_tensor_tensor(o_, a_, 1.0, b_, ALU.mult, ALU.add,
                                               accum_out=smlF(t, 0)),
                        waits=(tok_x2b, tok_sqr2.get(t - 2)), inc=SV)
        tok_r2 = tok_r2d[t]
        drain("vector")
        tok_sqr2[t] = on("scalar",
                         lambda e, i_=xtiF[p], o_=x2bF[p]: e.activation(
                             o_, i_, AF.Square, accum_out=smlF(t, 1)),
                         waits=(tok_r2,), inc=SA)
        tok_sq1t = on("scalar",
                      lambda e: e.activation(smlF(t, 2), smlF(t, 0), AF.Square, scale=1.0 / 32.0),
                      waits=(tok_r2,), inc=SA)
        tok_t2 = on("vector",
                    lambda e: e.tensor_tensor(smlF(t, 3), smlF(t, 1), smlF(t, 2), ALU.subtract),
                    waits=(tok_sqr2[t], tok_sq1t), inc=SV)
        tok_lnv = on("scalar",
                     lambda e: e.activation(smlF(t, 6), smlF(t, 3), AF.Ln, scale=1.0 / 1024.0,
                                            bias=s_eps2[:]),
                     waits=(tok_t2,), inc=SA)
        drain("scalar")
        tok_rstd2 = on("scalar",
                       lambda e: e.activation(smlF(t, 4), smlF(t, 6), AF.Exp, scale=-0.5),
                       waits=(tok_lnv,), inc=SA)
        tok_nmr = on("vector",
                     lambda e: e.scalar_tensor_tensor(smlF(t, 5), smlF(t, 0), -1.0 / 1024.0,
                                                      smlF(t, 4), ALU.mult, ALU.mult),
                     waits=(tok_rstd2,), inc=SV)
        drain("vector")
        outv = s_tbl[:, p * 2048:p * 2048 + 2048].bitcast(F32)
        tok_out = on("vector",
                     lambda e, o_=outv, i_=xtiF[p]: e.tensor_scalar(
                         o_, i_, smlF(t, 4), smlF(t, 5), ALU.mult, ALU.add),
                     waits=(tok_out_free[p],), inc=SV)
        tok_store[t] = on("gpsimd",
                          lambda e, t_=t, o_=outv: e.dma_start(out=d_y[t_ * 128:(t_ + 1) * 128, :], in_=o_),
                          waits=(tok_out,), inc=SSE if p == 0 else SSO, n=16)
        tok_out_free[p] = tok_store[t]
        if t + 2 < NT:
            emit_quant(t + 2)

    on("gpsimd", lambda e: e.nop(), waits=((SSE, SSE.n), (SSO, SSO.n)))

    if _TRUNC:
        cut, (na, nv, np_) = marks[_TRUNC]
        for e in ENGS:
            Q[e] = Q[e][:cut[e]]
        Q["gpsimd"].append((((SA, na), (SV, nv), (SP, np_)),
                            lambda e: e.nop(), None))

    with nc.Block() as block:
        def run(eng_name):
            waited = {}

            def body(eng):
                for waits, fn, inc in Q[eng_name]:
                    for (sm, v) in waits:
                        if v > waited.get(id(sm), 0):
                            eng.wait_ge(sm.h, v)
                            waited[id(sm)] = v
                    ins = fn(eng)
                    if inc is not None:
                        ins.then_inc(inc[0], inc[1])
            return body

        block.sync(run("sync"))
        block.scalar(run("scalar"))
        block.vector(run("vector"))
        block.tensor(run("tensor"))
        block.gpsimd(run("gpsimd"))

    es.close()
    return nc


_NC_CACHE = None


def _get_nc():
    global _NC_CACHE
    if _NC_CACHE is None:
        _NC_CACHE = _build_program()
    return _NC_CACHE


def _rope_tables():
    t = np.arange(T, dtype=np.float32)
    invf = (1.0 / (10000.0 ** (np.arange(0, HD, 2, dtype=np.float64) / HD))).astype(np.float32)
    freqs = t[:, None] * invf[None, :]
    cos = np.cos(freqs.astype(np.float64)).astype(np.float32)
    sin = np.sin(freqs.astype(np.float64)).astype(np.float32)
    cosq = cos * np.float32(0.125)
    sinq = sin * np.float32(0.125)
    return cosq, sinq, cos, sin


def kernel(L0_attn_x, L0_attn_Wq, L0_attn_Wk, L0_attn_Wv, L0_attn_Wo,
           L0_attn_q_norm_w, L0_attn_k_norm_w,
           L0_ffn_weight_int8, L0_ffn_weight_scale, L0_ffn_bias):
    x = np.ascontiguousarray(np.asarray(L0_attn_x, dtype=np.float32))
    wq = np.ascontiguousarray(np.asarray(L0_attn_Wq, dtype=np.float32))
    wk = np.ascontiguousarray(np.asarray(L0_attn_Wk, dtype=np.float32))
    wv = np.ascontiguousarray(np.asarray(L0_attn_Wv, dtype=np.float32))
    wo = np.ascontiguousarray(np.asarray(L0_attn_Wo, dtype=np.float32))
    wqn = np.asarray(L0_attn_q_norm_w, dtype=np.float32).reshape(1, HD)
    wkn = np.asarray(L0_attn_k_norm_w, dtype=np.float32).reshape(1, HD)
    w8 = np.ascontiguousarray(np.asarray(L0_ffn_weight_int8).T)
    wsr = np.ascontiguousarray(np.asarray(L0_ffn_weight_scale, dtype=np.float32).reshape(1, D))
    br = np.ascontiguousarray(np.asarray(L0_ffn_bias, dtype=np.float32).reshape(1, D))
    cosq, sinq, cosk, sink = _rope_tables()
    tri = np.triu(np.ones((128, 128), dtype=np.float32)).astype(ml_dtypes.bfloat16)
    ident = np.eye(128, dtype=np.float32).astype(ml_dtypes.bfloat16)

    nc = _get_nc()
    in_maps = []
    for c in range(8):
        xc = np.ascontiguousarray(x[c])
        in_maps.append({
            "xin": xc,
            "xtin": np.ascontiguousarray(xc.T),
            "wq": wq, "wk": wk, "wv": wv, "wo": wo,
            "wfft": w8, "wsrow": wsr, "brow": br,
            "wqrow": wqn, "wkrow": wkn,
            "cosq": cosq, "sinq": sinq, "cosk": cosk, "sink": sink,
            "trimask": tri, "identb": ident,
        })
    res = run_bass_kernel_spmd(nc, in_maps, core_ids=list(range(8)))
    out = np.stack([np.asarray(res.results[c]["y"]) for c in range(8)], axis=0)
    return out


# version 17
# speedup vs baseline: 1.1746x; 1.0814x over previous
import math

import numpy as np
import ml_dtypes
import concourse.bass as bass
import concourse.mybir as mybir
from concourse.bass_utils import run_bass_kernel_spmd

F32 = mybir.dt.float32
BF16 = mybir.dt.bfloat16
I8 = mybir.dt.int8
U8 = mybir.dt.uint8
AF = mybir.ActivationFunctionType
ALU = mybir.AluOpType
AX = mybir.AxisListType.X

T = 2048
D = 1024
H = 16
HD = 64
NT = 16
NCH = 8
NQC = 4
MAGIC = 12582912.0
LN_EPS = 1e-5
RMS_EPS = 1e-6
QEPS = 1e-5
LN_0125 = math.log(0.125)

ENGS = ("sync", "scalar", "vector", "tensor", "gpsimd")

_SIM_LOCAL_CC = False
_TRUNC = 0


class _Sem:
    __slots__ = ("h", "n")

    def __init__(self, h):
        self.h = h
        self.n = 0


def _build_program():
    nc = bass.Bass("TRN2", target_bir_lowering=False)

    d_xin = nc.dram_tensor("xin", [T, D], F32, kind="ExternalInput").ap()
    d_xt = nc.dram_tensor("xtin", [D, T], F32, kind="ExternalInput").ap()
    d_wq = nc.dram_tensor("wq", [D, D], F32, kind="ExternalInput").ap()
    d_wk = nc.dram_tensor("wk", [D, D], F32, kind="ExternalInput").ap()
    d_wv = nc.dram_tensor("wv", [D, D], F32, kind="ExternalInput").ap()
    d_wo = nc.dram_tensor("wo", [D, D], F32, kind="ExternalInput").ap()
    d_wfft = nc.dram_tensor("wfft", [D, D], I8, kind="ExternalInput").ap()
    d_wsrow = nc.dram_tensor("wsrow", [1, D], F32, kind="ExternalInput").ap()
    d_brow = nc.dram_tensor("brow", [1, D], F32, kind="ExternalInput").ap()
    d_wqrow = nc.dram_tensor("wqrow", [1, HD], F32, kind="ExternalInput").ap()
    d_wkrow = nc.dram_tensor("wkrow", [1, HD], F32, kind="ExternalInput").ap()
    d_cosq = nc.dram_tensor("cosq", [T, 32], F32, kind="ExternalInput").ap()
    d_sinq = nc.dram_tensor("sinq", [T, 32], F32, kind="ExternalInput").ap()
    d_cosk = nc.dram_tensor("cosk", [T, 32], F32, kind="ExternalInput").ap()
    d_sink = nc.dram_tensor("sink", [T, 32], F32, kind="ExternalInput").ap()
    d_trimask = nc.dram_tensor("trimask", [128, 128], BF16, kind="ExternalInput").ap()
    d_ident = nc.dram_tensor("identb", [128, 128], BF16, kind="ExternalInput").ap()
    d_y = nc.dram_tensor("y", [T, D], F32, kind="ExternalOutput").ap()

    d_ccin = nc.dram_tensor("ccin", [1, 1], F32)
    d_ccout = nc.dram_tensor("ccout", [1, 1], F32, addr_space="Shared")

    from contextlib import ExitStack
    es = ExitStack()
    S = es.enter_context

    arena = S(nc.sbuf_tensor("arena", [128, 65536], U8))
    QT0 = 32768
    s_kt = S(nc.sbuf_tensor("s_kt", [128, 16 * 1024], BF16))
    s_v = S(nc.sbuf_tensor("s_v", [128, NT * (H * 65)], BF16))
    s_w0 = S(nc.sbuf_tensor("s_w0", [128, NCH * 1024], BF16))
    s_w1 = S(nc.sbuf_tensor("s_w1", [128, NCH * 1024], BF16))
    s_pt = S(nc.sbuf_tensor("s_pt", [128, 2048], BF16))
    s_an = S(nc.sbuf_tensor("s_an", [128, 4096], BF16))
    s_at = S(nc.sbuf_tensor("s_at", [128, 4096], BF16))
    s_xti = S(nc.sbuf_tensor("s_xti", [128, 1024], F32))
    s_xqt = S(nc.sbuf_tensor("s_xqt", [128, 1024], BF16))
    s_rq = S(nc.sbuf_tensor("s_rq", [128, 1024], BF16))
    s_scr = S(nc.sbuf_tensor("s_scr", [128, 1024], F32))
    s_tmp = S(nc.sbuf_tensor("s_tmp", [128, 2048], BF16))
    s_tbl = S(nc.sbuf_tensor("s_tbl", [128, 4096], BF16))
    s_tri = S(nc.sbuf_tensor("s_tri", [128, 128], BF16))
    s_id = S(nc.sbuf_tensor("s_id", [128, 128], BF16))
    s_ms = S(nc.sbuf_tensor("s_ms", [128, 16], F32))
    s_rstd = S(nc.sbuf_tensor("s_rstd", [128, 16], F32))
    s_rstd2 = S(nc.sbuf_tensor("s_rstd2", [128, 16], F32))
    s_wqb = S(nc.sbuf_tensor("s_wqb", [128, 64], F32))
    s_wkb = S(nc.sbuf_tensor("s_wkb", [128, 64], F32))
    s_den = S(nc.sbuf_tensor("s_den", [128, 4], F32))
    s_rec = S(nc.sbuf_tensor("s_rec", [128, 4], F32))
    s_amax = S(nc.sbuf_tensor("s_amax", [128, 16], F32))
    s_am128 = S(nc.sbuf_tensor("s_am128", [1, 128], F32))
    s_sml = S(nc.sbuf_tensor("s_sml", [128, 16], F32))
    s_eps1 = S(nc.sbuf_tensor("s_eps1", [128, 1], F32))
    s_eps2 = S(nc.sbuf_tensor("s_eps2", [128, 1], F32))
    s_ones = S(nc.sbuf_tensor("s_ones", [1, 128], F32))
    s_ln8 = S(nc.sbuf_tensor("s_ln8", [1, 1], F32))
    s_mh = S(nc.sbuf_tensor("s_mh", [1, 4], F32))
    s_nmh = S(nc.sbuf_tensor("s_nmh", [128, 1], F32))
    s_xs = S(nc.sbuf_tensor("s_xs", [1, 2], F32))
    s_xsb = S(nc.sbuf_tensor("s_xsb", [128, 2], F32))

    p_a = S(nc.psum_tensor("p_a", [128, 1024], F32))
    p_b = S(nc.psum_tensor("p_b", [128, 1024], F32))
    p_c = S(nc.psum_tensor("p_c", [128, 1024], F32))
    p_t1 = S(nc.psum_tensor("p_t1", [128, 1024], BF16))
    p_t2 = S(nc.psum_tensor("p_t2", [128, 1024], BF16))

    sems = {}
    for name in ("lw0", "lw1", "lwx", "ldx", "ldg", "ldg2", "pe", "act",
                 "dve", "pool", "stE", "stO", "scc", "cc", "xld"):
        sems[name] = _Sem(S(nc.semaphore(name)))
    SW0, SW1, SWX, SX, SG, SG2, SP, SA, SV, SL, SSE, SSO, SCC, SC, SXL = (
        sems[n] for n in ("lw0", "lw1", "lwx", "ldx", "ldg", "ldg2", "pe",
                          "act", "dve", "pool", "stE", "stO", "scc", "cc",
                          "xld"))

    Q = {e: [] for e in ENGS}

    def on(eng, fn, waits=(), inc=None, n=1):
        tok = None
        rec_inc = None
        if inc is not None:
            inc.n += n
            tok = (inc, inc.n)
            rec_inc = (inc.h, n)
        Q[eng].append((tuple(w for w in waits if w is not None), fn, rec_inc))
        return tok

    def drain(eng):
        on(eng, lambda e: e.drain())

    def xt_blk(ch, t):
        base = ch * 4096 + t * 256
        return arena[:, base:base + 256].bitcast(BF16)

    def xt_full():
        return arena[:, 0:32768].bitcast(BF16)

    def qt_win(qc, ch):
        off = QT0 + qc * 8192 + ch * 1024
        return arena[:, off:off + 1024].bitcast(BF16)

    def qt_blk(qc, ch, tl):
        off = QT0 + qc * 8192 + ch * 1024 + tl * 256
        return arena[:, off:off + 256].bitcast(BF16)

    def kt_blk(j, ch):
        return s_kt[:, j * 1024 + ch * 128: j * 1024 + (ch + 1) * 128]

    def x2_tile(t):
        if t < 8:
            return arena[:, t * 4096:(t + 1) * 4096].bitcast(F32)
        return arena[:, QT0 + (t - 8) * 4096: QT0 + (t - 7) * 4096].bitcast(F32)

    def tmp_f32():
        return s_tmp[:, :].bitcast(F32)

    sml = lambda i: s_sml[:, i:i + 1]

    on("vector", lambda e: e.memset(s_eps1[:], RMS_EPS))
    on("vector", lambda e: e.memset(s_eps2[:], LN_EPS))
    on("vector", lambda e: e.memset(s_ones[:], 1.0))
    on("vector", lambda e: e.memset(s_ln8[:], LN_0125))
    v3 = s_v[:, :].rearrange("p (g j) -> p g j", j=65)
    tok_vones = on("vector", lambda e: e.memset(v3[:, :, 64:65], 1.0), inc=SV)

    on("sync", lambda e: e.dma_start(out=s_tri[:], in_=d_trimask), inc=SX, n=16)
    on("sync", lambda e: e.dma_start(out=s_id[:], in_=d_ident), inc=SX, n=16)
    stg = s_an[:, :].bitcast(F32)
    stg_v = [stg[:, i * 512:(i + 1) * 512] for i in range(4)]
    for i, dt_ in enumerate((d_cosq, d_sinq, d_cosk, d_sink)):
        src = dt_.rearrange("(n p) j -> p n j", p=128)
        dst = stg_v[i].rearrange("p (n j) -> p n j", j=32)
        on("sync", lambda e, s=src, d=dst: e.dma_start(out=d, in_=s),
           inc=SX, n=16)
    tok_tri = tok_id = (SX, SX.n)
    cs_toks = [(SX, SX.n)] * 4
    on("gpsimd", lambda e: e.dma_start(out=s_wqb[:], in_=d_wqrow.broadcast_to([128, HD])), inc=SG, n=16)
    on("gpsimd", lambda e: e.dma_start(out=s_wkb[:], in_=d_wkrow.broadcast_to([128, HD])), inc=SG, n=16)
    tok_wqb = tok_wkb = (SG, SG.n)
    def wload(dst_sb, src, sem, waits=()):
        src_r = src.rearrange("(c p) n -> p c n", p=128)
        dst_r = dst_sb[:, :].rearrange("p (c n) -> p c n", n=1024)
        return on("gpsimd", lambda e: e.dma_start(out=dst_r, in_=src_r),
                  waits=waits, inc=sem, n=16)
    tok_wq = wload(s_w0, d_wq, SW0)
    tok_wk = wload(s_w1, d_wk, SW1)
    xt_src = d_xt.rearrange("(c p) t -> p c t", p=128)
    xt_dst = xt_full().rearrange("p (c t) -> p c t", t=2048)
    tok_xt = on("gpsimd", lambda e: e.dma_start(out=xt_dst, in_=xt_src), inc=SWX, n=16)

    tbl = [s_tbl[:, i * 1024:(i + 1) * 1024] for i in range(4)]
    tok_tbl = None
    for m, (ci, si_, wb, wtok) in enumerate(((0, 1, s_wqb, tok_wqb), (2, 3, s_wkb, tok_wkb))):
        for tix, fi in ((2 * m, ci), (2 * m + 1, si_)):
            dst3 = tbl[tix].rearrange("p (t j) -> p t j", j=64)
            src3 = stg_v[fi].rearrange("p (t j) -> p t j", j=32)
            for half in range(2):
                wv_ = wb[:, half * 32:(half + 1) * 32].unsqueeze(1).broadcast_to([128, 16, 32])
                dsth = dst3[:, :, half * 32:(half + 1) * 32]
                tok_tbl = on("vector",
                             lambda e, d_=dsth, s_=src3, w_=wv_: e.tensor_tensor(d_, s_, w_, ALU.mult),
                             waits=(cs_toks[fi], wtok), inc=SV)

    tok_sq1 = on("scalar", lambda e: e.activation(s_scr[0:1, 0:64], s_wqb[0:1, :], AF.Square,
                                                  accum_out=s_mh[:, 0:1]),
                 waits=(tok_wqb,), inc=SA)
    tok_sq2 = on("scalar", lambda e: e.activation(s_scr[0:1, 64:128], s_wkb[0:1, :], AF.Square,
                                                  accum_out=s_mh[:, 1:2]),
                 waits=(tok_wkb,), inc=SA)
    tok_t1 = on("vector", lambda e: e.tensor_tensor(s_mh[:, 2:3], s_mh[:, 0:1], s_mh[:, 1:2], ALU.mult),
                waits=(tok_sq1, tok_sq2), inc=SV)
    tok_lnm = on("scalar", lambda e: e.activation(s_mh[:, 3:4], s_mh[:, 2:3], AF.Ln),
                 waits=(tok_t1,), inc=SA)
    drain("scalar")
    tok_mh = on("scalar", lambda e: e.activation(s_mh[:, 2:3], s_mh[:, 3:4], AF.Exp,
                                                 scale=0.5, bias=s_ln8[:]),
                waits=(tok_lnm,), inc=SA)
    drain("scalar")
    tok_nmh0 = on("scalar", lambda e: e.mul(s_mh[:, 3:4], s_mh[:, 2:3], -1.0),
                  waits=(tok_mh,), inc=SA)
    tok_nmhmm = on("tensor", lambda e: e.matmul(p_c[:, 0:1], s_ones[:], s_mh[0:1, 3:4],
                                                start=True, stop=True),
                   waits=(tok_nmh0,), inc=SP)
    tok_nmh = on("scalar", lambda e: e.activation(s_nmh[:], p_c[:, 0:1], AF.Copy),
                 waits=(tok_nmhmm,), inc=SA)

    psum_slots = (p_a, p_b, p_c)
    slot_free = [None, None, tok_nmh]
    tp_banks = (p_t1, p_t2)
    bank_free = [None, None]
    bank_idx = [0]

    def tp_group(pairs, ev_eng, first_waits=(), ev_first_waits=()):
        b = bank_idx[0] % 2
        bank_idx[0] += 1
        pt = tp_banks[b]
        ev_sem = SA if ev_eng == "scalar" else SV
        tps = []
        for i, (src, dst) in enumerate(pairs):
            w = [bank_free[b]] + list(first_waits) if i == 0 else []
            tp = on("tensor",
                    lambda e, o_=pt[:, i * 128:(i + 1) * 128], s_=src:
                    e.transpose(o_, s_, s_id[:]),
                    waits=w, inc=SP if i == len(pairs) - 1 else None)
            tps.append((tp, pt[:, i * 128:(i + 1) * 128], dst))
        last_tp = tps[-1][0]
        last_ev = None
        for i, (_, slot, dst) in enumerate(tps):
            w = [last_tp] + list(ev_first_waits) if i == 0 else []
            if ev_eng == "scalar":
                last_ev = on("scalar",
                             lambda e, d_=dst, s_=slot: e.activation(d_, s_, AF.Copy),
                             waits=w, inc=ev_sem)
            else:
                last_ev = on("vector",
                             lambda e, d_=dst, s_=slot: e.tensor_copy(d_, s_),
                             waits=w, inc=ev_sem)
        bank_free[b] = last_ev
        return last_tp, last_ev

    tok_vevac_last = None
    tok_rq_free = None
    tok_qh_free = None
    tok_rstd_free = None
    tok_scr_free = None
    tok_tmp_free = None
    last_proj_tok = {}
    pending = [None]

    def emit_qk_tp():
        nonlocal tok_rq_free
        if pending[0] is None:
            return
        tok_rq, mat, t = pending[0]
        pending[0] = None
        pairs = []
        for ch in range(NCH):
            if mat == "q":
                dst = qt_blk(t // 4, ch, t % 4)
            else:
                dst = kt_blk(t, ch)
            pairs.append((s_rq[:, ch * 128:(ch + 1) * 128], dst))
        last_tp, _ = tp_group(pairs, "scalar", first_waits=(tok_rq, tok_id))
        tok_rq_free = last_tp

    si = 0
    for m_i, mat in enumerate(("q", "k", "v")):
        wbuf = s_w0 if mat in ("q", "v") else s_w1
        wtok = {"q": tok_wq, "k": tok_wk}.get(mat)
        if mat == "v":
            wtok = wload(s_w0, d_wv, SW0, waits=(last_proj_tok["q"],))
        tbl_c = tbl[0] if mat == "q" else tbl[2]
        tbl_s = tbl[1] if mat == "q" else tbl[3]
        for t in range(NT):
            slot = si % 3
            si += 1
            pp = psum_slots[slot]
            mm_tok = None
            for ch in range(NCH):
                for half in range(2):
                    w_v = wbuf[:, ch * 1024 + half * 512: ch * 1024 + (half + 1) * 512]
                    waits = []
                    if ch == 0 and half == 0:
                        waits = [slot_free[slot], wtok, tok_xt]
                    mm_tok = on("tensor",
                                lambda e, p_=pp[:, half * 512:(half + 1) * 512],
                                       l_=xt_blk(ch, t), r_=w_v, s0=(ch == 0), s1=(ch == NCH - 1):
                                e.matmul(p_, l_, r_, start=s0, stop=s1),
                                waits=waits,
                                inc=SP if (ch == NCH - 1 and half == 1) else None)
            last_proj_tok[mat] = mm_tok
            emit_qk_tp()
            pview = pp[:, :].rearrange("p (h j) -> p h j", j=64)

            if mat == "v":
                dst = s_v[:, t * (H * 65):(t + 1) * (H * 65)].rearrange(
                    "p (h j) -> p h j", j=65)[:, :, 0:64]
                tok_vevac_last = on("vector",
                                    lambda e, d_=dst, s_=pview: e.tensor_copy(d_, s_),
                                    waits=(mm_tok, tok_vones), inc=SV)
                slot_free[slot] = tok_vevac_last
                continue

            sqv = tmp_f32()
            tok_sq = on("scalar", lambda e, s_=pp[:, :], d_=sqv: e.activation(d_, s_, AF.Square),
                        waits=(mm_tok, tok_tmp_free), inc=SA)
            tok_ms = on("vector",
                        lambda e: e.tensor_reduce(
                            s_ms[:, :].rearrange("p (h o) -> p h o", o=1),
                            tmp_f32().rearrange("p (h j) -> p h j", j=64), AX, ALU.add),
                        waits=(tok_sq, tok_rstd_free), inc=SV)
            tok_ln = on("scalar", lambda e: e.activation(s_rstd[:], s_ms[:], AF.Ln,
                                                         scale=1.0 / 64.0, bias=s_eps1[:]),
                        waits=(tok_ms,), inc=SA)
            drain("scalar")
            tok_rstd = on("scalar", lambda e: e.activation(s_rstd2[:], s_rstd[:], AF.Exp, scale=-0.5),
                          waits=(tok_ln,), inc=SA)
            rb = s_rstd2[:, :].rearrange("p (h o) -> p h o", o=1).broadcast_to([128, 16, 64])
            qh3 = s_xqt[:, :].rearrange("p (h j) -> p h j", j=64)
            tok_qh = on("vector",
                        lambda e, o_=qh3, i_=pview, b_=rb: e.tensor_tensor(o_, i_, b_, ALU.mult),
                        waits=(tok_rstd, tok_qh_free), inc=SV)
            tok_rstd_free = tok_qh
            slot_free[slot] = tok_qh
            qq = s_xqt[:, :].rearrange("p (h two j) -> p h two j", two=2, j=32)
            q1v, q2v = qq[:, :, 0, :], qq[:, :, 1, :]
            ct = tbl_c[:, t * 64:(t + 1) * 64]
            st_ = tbl_s[:, t * 64:(t + 1) * 64]
            c1 = ct[:, 0:32].unsqueeze(1).broadcast_to([128, 16, 32])
            c2 = ct[:, 32:64].unsqueeze(1).broadcast_to([128, 16, 32])
            s1v = st_[:, 0:32].unsqueeze(1).broadcast_to([128, 16, 32])
            s2v = st_[:, 32:64].unsqueeze(1).broadcast_to([128, 16, 32])
            tmp3 = s_tmp[:, :].rearrange("p (k j) -> p k j", j=512)
            ta = tmp3[:, 0, :].rearrange("p (h j) -> p h j", j=32)
            tb = tmp3[:, 1, :].rearrange("p (h j) -> p h j", j=32)
            tc = tmp3[:, 2, :].rearrange("p (h j) -> p h j", j=32)
            td = tmp3[:, 3, :].rearrange("p (h j) -> p h j", j=32)
            drain("vector")
            on("vector", lambda e, o_=ta, a_=q1v, b_=c1: e.tensor_tensor(o_, a_, b_, ALU.mult),
               waits=(tok_tbl,))
            on("vector", lambda e, o_=tb, a_=q2v, b_=s2v: e.tensor_tensor(o_, a_, b_, ALU.mult))
            on("vector", lambda e, o_=tc, a_=q2v, b_=c2: e.tensor_tensor(o_, a_, b_, ALU.mult))
            on("vector", lambda e, o_=td, a_=q1v, b_=s1v: e.tensor_tensor(o_, a_, b_, ALU.mult))
            drain("vector")
            rq3 = s_rq[:, :].rearrange("p (h two j) -> p h two j", two=2, j=32)
            on("vector", lambda e, o_=rq3[:, :, 0, :], a_=ta, b_=tb: e.tensor_tensor(o_, a_, b_, ALU.subtract),
               waits=(tok_rq_free,))
            tok_rq = on("vector", lambda e, o_=rq3[:, :, 1, :], a_=tc, b_=td: e.tensor_tensor(o_, a_, b_, ALU.add),
                        inc=SV)
            tok_qh_free = tok_rq
            tok_tmp_free = tok_rq
            pending[0] = (tok_rq, mat, t)

    tok_qk_ready = (SA, SA.n)
    marks = {1: ({e: len(Q[e]) for e in ENGS}, (SA.n, SV.n, SP.n))}

    tok_wo = wload(s_w1, d_wo, SW1, waits=(last_proj_tok["k"],))
    tok_wff = wload(s_w0, d_wfft, SW0, waits=(last_proj_tok["v"],))

    ph_bufs = (s_pt[:, 0:1024], s_pt[:, 1024:2048], s_xqt[:, :], s_rq[:, :])
    pth_free4 = [None] * 4
    pc_free = [slot_free[2], slot_free[2]]
    psum_free = {"a": slot_free[0], "b": slot_free[1]}
    tmp_free_att = tok_tmp_free
    tld_gate = [tok_tmp_free, tok_tmp_free]
    gidx = [0]
    an_free = None
    tok_r_evac = {}
    tok_x2 = {}
    tok_amax = {}
    tok_xdma = None
    tok_xti_free = None
    att_gate = [tok_qk_ready, tok_vevac_last, tok_nmh, tok_tri]
    first_s_done = False
    wo_tok_prev = None
    tok_sqr1_last = None

    for c in range(NQC):
        an_toks = {}
        at_last_tp = [None]
        at_first = [True]

        def emit_at_tp(m, c=c, an_toks=an_toks, at_last_tp=at_last_tp,
                       at_first=at_first):
            pairs = []
            for tl in range(4):
                src = s_an[:, tl * 1024 + m * 128: tl * 1024 + (m + 1) * 128]
                dst = s_at[:, m * 512 + tl * 128: m * 512 + (tl + 1) * 128]
                pairs.append((src, dst))
            evw = ()
            if at_first[0]:
                evw = (wo_tok_prev,)
                at_first[0] = False
            tp, _ = tp_group(pairs, "vector",
                             first_waits=(an_toks[2 * m + 1],),
                             ev_first_waits=evw)
            at_last_tp[0] = tp

        jmaxc = 4 * (c + 1)
        ngrp = jmaxc // 2
        last_g = ngrp - 1
        av_last_of = {}

        def emit_an(h, c=c, an_toks=an_toks):
            pcb = (h % 2) * 512
            pcv = p_c[:, pcb:pcb + 260].rearrange("p (t j) -> p t j", j=65)
            on("vector",
               lambda e, v_=pcv: e.tensor_copy(
                   s_den[:, :].rearrange("p (t o) -> p t o", o=1), v_[:, :, 64:65]),
               waits=(av_last_of[h], an_free), inc=SV)
            drain("vector")
            on("vector", lambda e: e.reciprocal(s_rec[:], s_den[:]), inc=SV)
            drain("vector")
            an4 = s_an[:, :].rearrange("p (t hh j) -> p t hh j", hh=16, j=64)
            recb = s_rec[:, :].rearrange("p (t o) -> p t o", o=1).broadcast_to([128, 4, 64])
            tok_an = on("vector",
                        lambda e, o_=an4[:, :, h, :], i_=pcv, b_=recb:
                        e.tensor_tensor(o_, i_[:, :, 0:64], b_, ALU.mult),
                        inc=SV)
            pc_free[h % 2] = tok_an
            an_toks[h] = tok_an

        def emit_av(h, g, gi4, etok, mtoks, pth, c=c):
            pcb = (h % 2) * 512
            last = None
            for jj in range(2):
                j = 2 * g + jj
                rhs_v = s_v[:, j * (H * 65) + h * 65: j * (H * 65) + (h + 1) * 65]
                for tl in range(max(0, j - 4 * c), 4):
                    diag = (j - 4 * c == tl)
                    if diag:
                        lhs = s_tmp[:, tl * 128:(tl + 1) * 128]
                    else:
                        lhs = pth[:, jj * 512 + tl * 128: jj * 512 + (tl + 1) * 128]
                    waits = [etok]
                    if diag:
                        waits.append(mtoks[j])
                    s0 = (g == 0 and jj == 0 and tl == 0)
                    if s0:
                        waits.append(pc_free[h % 2])
                    last = on("tensor",
                              lambda e, p_=p_c[:, pcb + tl * 65: pcb + (tl + 1) * 65],
                                     l_=lhs, r_=rhs_v, s0=s0, s1=(j == 4 * c + tl):
                              e.matmul(p_, l_, r_, start=s0, stop=s1,
                                       skip_group_check=True),
                              waits=waits,
                              inc=SP if (jj == 1 and tl == 3) else None)
            pth_free4[gi4] = last
            av_last_of[h] = last
            if g >= 2 * c:
                tld_gate[g - 2 * c] = last
            if g == last_g:
                emit_an(h)

        pend_av = []
        for h in range(H):
            if h % 2 == 1 and h >= 3:
                emit_at_tp((h - 3) // 2)
            ch = h // 2
            hb = (h % 2) * 64
            for g in range(ngrp):
                gi = gidx[0]
                gidx[0] += 1
                par = gi % 2
                ps = p_a if par == 0 else p_b
                pth = ph_bufs[gi % 4]
                qs = [max(0, 2 * g + jj - 4 * c) * 128 for jj in range(2)]
                stok = None
                for jj in range(2):
                    j = 2 * g + jj
                    lhs = kt_blk(j, ch)[hb:hb + 64, :]
                    rhs_q = qt_win(c, ch)[hb:hb + 64, qs[jj]:512]
                    waits = []
                    if jj == 0:
                        waits = [psum_free["a" if par == 0 else "b"]]
                        if not first_s_done:
                            waits += att_gate
                            first_s_done = True
                    stok = on("tensor",
                              lambda e, p_=ps[:, jj * 512 + qs[jj]:(jj + 1) * 512],
                                     l_=lhs, r_=rhs_q:
                              e.matmul(p_, l_, r_, start=True, stop=True),
                              waits=waits, inc=SP if jj == 1 else None)
                qs0 = qs[0]
                pv = ps[:, :].rearrange("p (two q) -> p two q", q=512)[:, :, qs0:512]
                ev_ = pth[:, :].rearrange("p (two q) -> p two q", q=512)[:, :, qs0:512]
                etok = on("scalar",
                          lambda e, o_=ev_, i_=pv: e.activation(o_, i_, AF.Exp, bias=s_nmh[:]),
                          waits=(stok, pth_free4[gi % 4]), inc=SA)
                psum_free["a" if par == 0 else "b"] = etok
                if len(pend_av) == 2:
                    emit_av(*pend_av.pop(0))
                mtoks = {}
                for jj in range(2):
                    j = 2 * g + jj
                    if j >= 4 * c:
                        tld = j - 4 * c
                        blk = pth[:, jj * 512 + tld * 128: jj * 512 + (tld + 1) * 128]
                        mdst = s_tmp[:, tld * 128:(tld + 1) * 128]
                        mtoks[j] = on("gpsimd",
                                      lambda e, o_=mdst, a_=blk, b_=s_tri[:]:
                                      e.tensor_tensor(o_, a_, b_, ALU.mult),
                                      waits=(etok, tld_gate[g - 2 * c]), inc=SL)
                pend_av.append((h, g, gi % 4, etok, mtoks, pth))
        while pend_av:
            emit_av(*pend_av.pop(0))
        tmp_free_att = av_last_of[H - 1]

        emit_at_tp(7)
        an_free = at_last_tp[0]
        tok_at_ready = (SV, SV.n)

        for tl in range(4):
            t = c * 4 + tl
            par = "a" if tl % 2 == 0 else "b"
            pp = p_a if par == "a" else p_b
            tok_xdma = on("sync",
                          lambda e, t_=t: e.dma_start(out=s_xti[:], in_=d_xin[t_ * 128:(t_ + 1) * 128, :]),
                          waits=(tok_xti_free,), inc=SXL, n=16)
            wo_tok = None
            for chh in range(NCH):
                for half in range(2):
                    lhs = s_at[:, chh * 512 + tl * 128: chh * 512 + (tl + 1) * 128]
                    rhs = s_w1[:, chh * 1024 + half * 512: chh * 1024 + (half + 1) * 512]
                    waits = []
                    if chh == 0 and half == 0:
                        waits = [tok_at_ready, psum_free[par]]
                        if c == 0 and tl == 0:
                            waits.append(tok_wo)
                    wo_tok = on("tensor",
                                lambda e, p_=pp[:, half * 512:(half + 1) * 512], l_=lhs, r_=rhs,
                                       s0=(chh == 0), s1=(chh == NCH - 1):
                                e.matmul(p_, l_, r_, start=s0, stop=s1),
                                waits=waits,
                                inc=SP if (chh == NCH - 1 and half == 1) else None)
            tok_r = on("vector",
                       lambda e, p_=pp: e.scalar_tensor_tensor(
                           s_scr[:], s_xti[:], 1.0, p_[:, :], ALU.mult, ALU.add,
                           accum_out=sml(0)),
                       waits=(wo_tok, tok_xdma, tok_scr_free), inc=SV)
            psum_free[par] = tok_r
            tok_r_evac[t] = tok_r
            drain("vector")
            tok_sqr = on("scalar",
                         lambda e: e.activation(s_xti[:], s_scr[:], AF.Square, accum_out=sml(1)),
                         waits=(tok_r,), inc=SA)
            tok_xti_free = tok_sqr
            tok_sqr1_last = tok_sqr
            tok_sq1t = on("scalar",
                          lambda e: e.activation(sml(2), sml(0), AF.Square, scale=1.0 / 32.0),
                          waits=(tok_r,), inc=SA)
            tok_t2 = on("vector",
                        lambda e: e.tensor_tensor(sml(3), sml(1), sml(2), ALU.subtract),
                        waits=(tok_sqr, tok_sq1t), inc=SV)
            tok_lnv = on("scalar",
                         lambda e: e.activation(sml(6), sml(3), AF.Ln, scale=1.0 / 1024.0,
                                                bias=s_eps2[:]),
                         waits=(tok_t2,), inc=SA)
            drain("scalar")
            tok_rstd2 = on("scalar",
                           lambda e: e.activation(sml(4), sml(6), AF.Exp, scale=-0.5),
                           waits=(tok_lnv,), inc=SA)
            tok_nmr = on("vector",
                         lambda e: e.scalar_tensor_tensor(sml(5), sml(0), -1.0 / 1024.0, sml(4),
                                                          ALU.mult, ALU.mult),
                         waits=(tok_rstd2,), inc=SV)
            drain("vector")
            tok_x2[t] = on("vector",
                           lambda e, o_=x2_tile(t): e.tensor_scalar(
                               o_, s_scr[:], sml(4), sml(5), ALU.mult, ALU.add),
                           inc=SV)
            tok_scr_free = tok_x2[t]
            tok_amax[t] = on("vector",
                             lambda e, o_=s_amax[:, t:t + 1], i_=x2_tile(t):
                             e.tensor_reduce(o_, i_, AX, ALU.max,
                                             apply_absolute_value=True),
                             waits=(tok_x2[t],), inc=SV)
        wo_tok_prev = wo_tok

    marks[2] = ({e: len(Q[e]) for e in ENGS}, (SA.n, SV.n, SP.n))

    ws_b = s_at[:, 0:2048].bitcast(F32)
    b_b = s_at[:, 2048:4096].bitcast(F32)
    on("gpsimd", lambda e: e.dma_start(out=ws_b, in_=d_wsrow.broadcast_to([128, D])),
       waits=(wo_tok,), inc=SG2, n=16)
    on("gpsimd", lambda e: e.dma_start(out=b_b, in_=d_brow.broadcast_to([128, D])),
       waits=(wo_tok,), inc=SG2, n=16)
    tok_wsb = tok_bb = (SG2, SG2.n)

    tok_amc = on("vector", lambda e: e.tensor_reduce(sml(7), s_amax[:], AX, ALU.max),
                 waits=(tok_amax[NT - 1],), inc=SV)
    tok_gl = on("gpsimd",
                lambda e: e.tensor_reduce(s_xs[0:1, 0:1], sml(7),
                                          mybir.AxisListType.XYZWC, ALU.max),
                waits=(tok_amc,), inc=SL)
    if _SIM_LOCAL_CC:
        tok_cdn = on("gpsimd", lambda e: e.tensor_copy(s_am128[0:1, 0:1], s_xs[0:1, 0:1]),
                     waits=(tok_gl,), inc=SL)
    else:
        tok_cup = on("gpsimd", lambda e: e.dma_start(out=d_ccin.ap(), in_=s_xs[0:1, 0:1]),
                     waits=(tok_gl,), inc=SCC, n=16)
        tok_cc = on("gpsimd", lambda e: e.collective_compute(
            "AllReduce", ALU.max, replica_groups=[list(range(8))],
            ins=[d_ccin.ap()], outs=[d_ccout.ap()]),
            waits=(tok_cup,), inc=SC, n=1)
        tok_cdn = on("gpsimd", lambda e: e.dma_start(out=s_am128[0:1, 0:1], in_=d_ccout.ap()),
                     waits=(tok_cc,), inc=SCC, n=16)
    tok_xsv = on("vector", lambda e: e.tensor_scalar(s_xs[:, 0:1], s_am128[0:1, 0:1],
                                                     float(np.float32(1.0) / np.float32(127.0)),
                                                     QEPS, ALU.mult, ALU.max),
                 waits=(tok_cdn,), inc=SV)
    drain("vector")
    tok_rxs = on("vector", lambda e: e.reciprocal(s_xs[:, 1:2], s_xs[:, 0:1]),
                 waits=(tok_xsv,), inc=SV)
    tok_bc1 = on("tensor", lambda e: e.matmul(p_c[:, 2:3], s_ones[:], s_xs[0:1, 0:1],
                                              start=True, stop=True),
                 waits=(tok_rxs, pc_free[0]), inc=SP)
    tok_bc2 = on("tensor", lambda e: e.matmul(p_c[:, 3:4], s_ones[:], s_xs[0:1, 1:2],
                                              start=True, stop=True),
                 waits=(tok_rxs, pc_free[1]), inc=SP)
    tok_xsb = on("scalar", lambda e: e.activation(s_xsb[:, 0:1], p_c[:, 2:3], AF.Copy),
                 waits=(tok_bc1,), inc=SA)
    tok_rxsb = on("scalar", lambda e: e.activation(s_xsb[:, 1:2], p_c[:, 3:4], AF.Copy),
                  waits=(tok_bc2,), inc=SA)
    ts_b = s_an[:, 0:2048].bitcast(F32)
    tok_tsb = on("vector", lambda e: e.tensor_scalar(ts_b, ws_b, s_xsb[:, 0:1], None, ALU.mult),
                 waits=(tok_wsb, tok_xsb, an_free), inc=SV)

    xq_all = s_kt[:, :]
    vf32 = s_v[:, :].bitcast(F32)
    scrF = (vf32[:, 0:1024], vf32[:, 1024:2048])
    tmpF = (vf32[:, 2048:3072], vf32[:, 3072:4096])
    x2bF = (vf32[:, 4096:5120], vf32[:, 5120:6144])
    xtiF = (vf32[:, 6144:7168], vf32[:, 7168:8192])
    smlF = lambda t, i: s_sml[:, (t % 2) * 8 + i:(t % 2) * 8 + i + 1]

    tok_q2 = {}
    tok_q3 = {}
    tok_r2d = {}
    tok_sqr2 = {}

    def emit_quant(t):
        p = t % 2
        tok_q1 = on("vector",
                    lambda e, i_=x2_tile(t), o_=scrF[p]: e.tensor_scalar(
                        o_, i_, s_xsb[:, 1:2], MAGIC, ALU.mult, ALU.add),
                    waits=(tok_rxsb, tok_x2[t],
                           tmp_free_att if t < 2 else tok_r2d[t - 2]), inc=SV)
        tok_q2[t] = on("scalar",
                       lambda e, i_=scrF[p], o_=tmpF[p]: e.activation(
                           o_, i_, AF.Copy, bias=-MAGIC),
                       waits=(tok_q1, tok_q3.get(t - 2)), inc=SA)
        tok_q3[t] = on("vector",
                       lambda e, i_=tmpF[p], o_=xq_all[:, t * 1024:(t + 1) * 1024]:
                       e.tensor_scalar(o_, i_, 127.0, -128.0, ALU.min, ALU.max),
                       waits=(tok_q2[t],), inc=SV)

    emit_quant(0)
    emit_quant(1)

    marks[3] = ({e: len(Q[e]) for e in ENGS}, (SA.n, SV.n, SP.n))

    xqt_bufs = (s_xqt, s_rq)
    tok_xqt_free = [None, None]
    tok_store = {}
    tok_out_free = [None, None]
    tok_xqt_t = {}

    def emit_xq_tp(t):
        xb = xqt_bufs[t % 2]
        pairs = []
        for chh in range(NCH):
            src = xq_all[:, t * 1024 + chh * 128: t * 1024 + (chh + 1) * 128]
            dst = xb[:, chh * 128:(chh + 1) * 128]
            pairs.append((src, dst))
        tp_group(pairs, "scalar", first_waits=(tok_q3[t],),
                 ev_first_waits=(tok_xqt_free[t % 2],)
                 if t >= 2 else (tmp_free_att,))
        tok_xqt_t[t] = (SA, SA.n)

    emit_xq_tp(0)
    for t in range(NT):
        p = t % 2
        par = "a" if p == 0 else "b"
        pp = p_a if par == "a" else p_b
        xb = xqt_bufs[p]
        tok_xqt = tok_xqt_t[t]
        tok_x2b = on("gpsimd",
                     lambda e, i_=x2_tile(t), o_=x2bF[p]: e.tensor_tensor(
                         o_, i_, b_b, ALU.add),
                     waits=(tok_bb, tok_x2[t],
                            tmp_free_att if t < 2 else tok_sqr2[t - 2]), inc=SL)
        mm_tok = None
        for chh in range(NCH):
            for half in range(2):
                rhs = s_w0[:, chh * 1024 + half * 512: chh * 1024 + (half + 1) * 512]
                waits = []
                if chh == 0 and half == 0:
                    waits = [tok_xqt, psum_free[par]]
                    if t == 0:
                        waits.append(tok_wff)
                mm_tok = on("tensor",
                            lambda e, p_=pp[:, half * 512:(half + 1) * 512],
                                   l_=xb[:, chh * 128:(chh + 1) * 128], r_=rhs,
                                   s0=(chh == 0), s1=(chh == NCH - 1):
                            e.matmul(p_, l_, r_, start=s0, stop=s1),
                            waits=waits,
                            inc=SP if (chh == NCH - 1 and half == 1) else None)
        tok_xqt_free[p] = mm_tok
        if t + 1 < NT:
            emit_xq_tp(t + 1)
        tok_o1 = on("vector",
                    lambda e, p_=pp, o_=scrF[p]: e.tensor_tensor(o_, p_[:, :], ts_b, ALU.mult),
                    waits=(mm_tok, tok_tsb), inc=SV)
        psum_free[par] = tok_o1
        drain("vector")
        tok_r2d[t] = on("vector",
                        lambda e, a_=scrF[p], b_=x2bF[p], o_=xtiF[p]:
                        e.scalar_tensor_tensor(o_, a_, 1.0, b_, ALU.mult, ALU.add,
                                               accum_out=smlF(t, 0)),
                        waits=(tok_x2b, tok_sqr2.get(t - 2)), inc=SV)
        tok_r2 = tok_r2d[t]
        drain("vector")
        tok_sqr2[t] = on("scalar",
                         lambda e, i_=xtiF[p], o_=x2bF[p]: e.activation(
                             o_, i_, AF.Square, accum_out=smlF(t, 1)),
                         waits=(tok_r2,), inc=SA)
        tok_sq1t = on("scalar",
                      lambda e: e.activation(smlF(t, 2), smlF(t, 0), AF.Square, scale=1.0 / 32.0),
                      waits=(tok_r2,), inc=SA)
        tok_t2 = on("vector",
                    lambda e: e.tensor_tensor(smlF(t, 3), smlF(t, 1), smlF(t, 2), ALU.subtract),
                    waits=(tok_sqr2[t], tok_sq1t), inc=SV)
        tok_lnv = on("scalar",
                     lambda e: e.activation(smlF(t, 6), smlF(t, 3), AF.Ln, scale=1.0 / 1024.0,
                                            bias=s_eps2[:]),
                     waits=(tok_t2,), inc=SA)
        drain("scalar")
        tok_rstd2 = on("scalar",
                       lambda e: e.activation(smlF(t, 4), smlF(t, 6), AF.Exp, scale=-0.5),
                       waits=(tok_lnv,), inc=SA)
        tok_nmr = on("vector",
                     lambda e: e.scalar_tensor_tensor(smlF(t, 5), smlF(t, 0), -1.0 / 1024.0,
                                                      smlF(t, 4), ALU.mult, ALU.mult),
                     waits=(tok_rstd2,), inc=SV)
        drain("vector")
        outv = s_tbl[:, p * 2048:p * 2048 + 2048].bitcast(F32)
        tok_out = on("vector",
                     lambda e, o_=outv, i_=xtiF[p]: e.tensor_scalar(
                         o_, i_, smlF(t, 4), smlF(t, 5), ALU.mult, ALU.add),
                     waits=(tok_out_free[p],), inc=SV)
        tok_store[t] = on("gpsimd",
                          lambda e, t_=t, o_=outv: e.dma_start(out=d_y[t_ * 128:(t_ + 1) * 128, :], in_=o_),
                          waits=(tok_out,), inc=SSE if p == 0 else SSO, n=16)
        tok_out_free[p] = tok_store[t]
        if t + 2 < NT:
            emit_quant(t + 2)

    on("gpsimd", lambda e: e.nop(), waits=((SSE, SSE.n), (SSO, SSO.n)))

    if _TRUNC:
        cut, (na, nv, np_) = marks[_TRUNC]
        for e in ENGS:
            Q[e] = Q[e][:cut[e]]
        Q["gpsimd"].append((((SA, na), (SV, nv), (SP, np_)),
                            lambda e: e.nop(), None))

    with nc.Block() as block:
        def run(eng_name):
            waited = {}

            def body(eng):
                for waits, fn, inc in Q[eng_name]:
                    for (sm, v) in waits:
                        if v > waited.get(id(sm), 0):
                            eng.wait_ge(sm.h, v)
                            waited[id(sm)] = v
                    ins = fn(eng)
                    if inc is not None:
                        ins.then_inc(inc[0], inc[1])
            return body

        block.sync(run("sync"))
        block.scalar(run("scalar"))
        block.vector(run("vector"))
        block.tensor(run("tensor"))
        block.gpsimd(run("gpsimd"))

    es.close()
    return nc


_NC_CACHE = None


def _get_nc():
    global _NC_CACHE
    if _NC_CACHE is None:
        _NC_CACHE = _build_program()
    return _NC_CACHE


def _rope_tables():
    t = np.arange(T, dtype=np.float32)
    invf = (1.0 / (10000.0 ** (np.arange(0, HD, 2, dtype=np.float64) / HD))).astype(np.float32)
    freqs = t[:, None] * invf[None, :]
    cos = np.cos(freqs.astype(np.float64)).astype(np.float32)
    sin = np.sin(freqs.astype(np.float64)).astype(np.float32)
    cosq = cos * np.float32(0.125)
    sinq = sin * np.float32(0.125)
    return cosq, sinq, cos, sin


def kernel(L0_attn_x, L0_attn_Wq, L0_attn_Wk, L0_attn_Wv, L0_attn_Wo,
           L0_attn_q_norm_w, L0_attn_k_norm_w,
           L0_ffn_weight_int8, L0_ffn_weight_scale, L0_ffn_bias):
    x = np.ascontiguousarray(np.asarray(L0_attn_x, dtype=np.float32))
    wq = np.ascontiguousarray(np.asarray(L0_attn_Wq, dtype=np.float32))
    wk = np.ascontiguousarray(np.asarray(L0_attn_Wk, dtype=np.float32))
    wv = np.ascontiguousarray(np.asarray(L0_attn_Wv, dtype=np.float32))
    wo = np.ascontiguousarray(np.asarray(L0_attn_Wo, dtype=np.float32))
    wqn = np.asarray(L0_attn_q_norm_w, dtype=np.float32).reshape(1, HD)
    wkn = np.asarray(L0_attn_k_norm_w, dtype=np.float32).reshape(1, HD)
    w8 = np.ascontiguousarray(np.asarray(L0_ffn_weight_int8).T)
    wsr = np.ascontiguousarray(np.asarray(L0_ffn_weight_scale, dtype=np.float32).reshape(1, D))
    br = np.ascontiguousarray(np.asarray(L0_ffn_bias, dtype=np.float32).reshape(1, D))
    cosq, sinq, cosk, sink = _rope_tables()
    tri = np.triu(np.ones((128, 128), dtype=np.float32)).astype(ml_dtypes.bfloat16)
    ident = np.eye(128, dtype=np.float32).astype(ml_dtypes.bfloat16)

    nc = _get_nc()
    in_maps = []
    for c in range(8):
        xc = np.ascontiguousarray(x[c])
        in_maps.append({
            "xin": xc,
            "xtin": np.ascontiguousarray(xc.T),
            "wq": wq, "wk": wk, "wv": wv, "wo": wo,
            "wfft": w8, "wsrow": wsr, "brow": br,
            "wqrow": wqn, "wkrow": wkn,
            "cosq": cosq, "sinq": sinq, "cosk": cosk, "sink": sink,
            "trimask": tri, "identb": ident,
        })
    res = run_bass_kernel_spmd(nc, in_maps, core_ids=list(range(8)))
    out = np.stack([np.asarray(res.results[c]["y"]) for c in range(8)], axis=0)
    return out


# version 18 (final)
# speedup vs baseline: 1.2581x; 1.0711x over previous
import math

import numpy as np
import ml_dtypes
import concourse.bass as bass
import concourse.mybir as mybir
from concourse.bass_utils import run_bass_kernel_spmd

F32 = mybir.dt.float32
BF16 = mybir.dt.bfloat16
I8 = mybir.dt.int8
U8 = mybir.dt.uint8
AF = mybir.ActivationFunctionType
ALU = mybir.AluOpType
AX = mybir.AxisListType.X

T = 2048
D = 1024
H = 16
HD = 64
NT = 16
NCH = 8
NQC = 4
MAGIC = 12582912.0
LN_EPS = 1e-5
RMS_EPS = 1e-6
QEPS = 1e-5
LN_0125 = math.log(0.125)

ENGS = ("sync", "scalar", "vector", "tensor", "gpsimd")

_SIM_LOCAL_CC = False
_TRUNC = 0


class _Sem:
    __slots__ = ("h", "n")

    def __init__(self, h):
        self.h = h
        self.n = 0


def _build_program():
    nc = bass.Bass("TRN2", target_bir_lowering=False)

    d_xin = nc.dram_tensor("xin", [T, D], F32, kind="ExternalInput").ap()
    d_xt = nc.dram_tensor("xtin", [D, T], F32, kind="ExternalInput").ap()
    d_wq = nc.dram_tensor("wq", [D, D], F32, kind="ExternalInput").ap()
    d_wk = nc.dram_tensor("wk", [D, D], F32, kind="ExternalInput").ap()
    d_wv = nc.dram_tensor("wv", [D, D], F32, kind="ExternalInput").ap()
    d_wo = nc.dram_tensor("wo", [D, D], F32, kind="ExternalInput").ap()
    d_wfft = nc.dram_tensor("wfft", [D, D], I8, kind="ExternalInput").ap()
    d_wsrow = nc.dram_tensor("wsrow", [1, D], F32, kind="ExternalInput").ap()
    d_brow = nc.dram_tensor("brow", [1, D], F32, kind="ExternalInput").ap()
    d_wqrow = nc.dram_tensor("wqrow", [1, HD], F32, kind="ExternalInput").ap()
    d_wkrow = nc.dram_tensor("wkrow", [1, HD], F32, kind="ExternalInput").ap()
    d_cosq = nc.dram_tensor("cosq", [T, 32], F32, kind="ExternalInput").ap()
    d_sinq = nc.dram_tensor("sinq", [T, 32], F32, kind="ExternalInput").ap()
    d_cosk = nc.dram_tensor("cosk", [T, 32], F32, kind="ExternalInput").ap()
    d_sink = nc.dram_tensor("sink", [T, 32], F32, kind="ExternalInput").ap()
    d_trimask = nc.dram_tensor("trimask", [128, 128], BF16, kind="ExternalInput").ap()
    d_ident = nc.dram_tensor("identb", [128, 128], BF16, kind="ExternalInput").ap()
    d_y = nc.dram_tensor("y", [T, D], F32, kind="ExternalOutput").ap()

    d_ccin = nc.dram_tensor("ccin", [1, 1], F32)
    d_ccout = nc.dram_tensor("ccout", [1, 1], F32, addr_space="Shared")

    from contextlib import ExitStack
    es = ExitStack()
    S = es.enter_context

    arena = S(nc.sbuf_tensor("arena", [128, 65536], U8))
    QT0 = 32768
    s_kt = S(nc.sbuf_tensor("s_kt", [128, 16 * 1024], BF16))
    s_v = S(nc.sbuf_tensor("s_v", [128, NT * (H * 65)], BF16))
    s_w0 = S(nc.sbuf_tensor("s_w0", [128, NCH * 1024], BF16))
    s_w1 = S(nc.sbuf_tensor("s_w1", [128, NCH * 1024], BF16))
    s_pt = S(nc.sbuf_tensor("s_pt", [128, 2048], BF16))
    s_an = S(nc.sbuf_tensor("s_an", [128, 4096], BF16))
    s_at = S(nc.sbuf_tensor("s_at", [128, 4096], BF16))
    s_xti = S(nc.sbuf_tensor("s_xti", [128, 1024], F32))
    s_xqt = S(nc.sbuf_tensor("s_xqt", [128, 1024], BF16))
    s_rq = S(nc.sbuf_tensor("s_rq", [128, 1024], BF16))
    s_scr = S(nc.sbuf_tensor("s_scr", [128, 1024], F32))
    s_tmp = S(nc.sbuf_tensor("s_tmp", [128, 2048], BF16))
    s_tbl = S(nc.sbuf_tensor("s_tbl", [128, 4096], BF16))
    s_tri = S(nc.sbuf_tensor("s_tri", [128, 128], BF16))
    s_id = S(nc.sbuf_tensor("s_id", [128, 128], BF16))
    s_ms = S(nc.sbuf_tensor("s_ms", [128, 16], F32))
    s_rstd = S(nc.sbuf_tensor("s_rstd", [128, 16], F32))
    s_rstd2 = S(nc.sbuf_tensor("s_rstd2", [128, 16], F32))
    s_wqb = S(nc.sbuf_tensor("s_wqb", [128, 64], F32))
    s_wkb = S(nc.sbuf_tensor("s_wkb", [128, 64], F32))
    s_den = S(nc.sbuf_tensor("s_den", [128, 4], F32))
    s_rec = S(nc.sbuf_tensor("s_rec", [128, 4], F32))
    s_amax = S(nc.sbuf_tensor("s_amax", [128, 16], F32))
    s_am128 = S(nc.sbuf_tensor("s_am128", [1, 128], F32))
    s_sml = S(nc.sbuf_tensor("s_sml", [128, 16], F32))
    s_eps1 = S(nc.sbuf_tensor("s_eps1", [128, 1], F32))
    s_eps2 = S(nc.sbuf_tensor("s_eps2", [128, 1], F32))
    s_ones = S(nc.sbuf_tensor("s_ones", [1, 128], F32))
    s_ln8 = S(nc.sbuf_tensor("s_ln8", [1, 1], F32))
    s_mh = S(nc.sbuf_tensor("s_mh", [1, 4], F32))
    s_nmh = S(nc.sbuf_tensor("s_nmh", [128, 1], F32))
    s_xs = S(nc.sbuf_tensor("s_xs", [1, 2], F32))
    s_xsb = S(nc.sbuf_tensor("s_xsb", [128, 2], F32))

    p_a = S(nc.psum_tensor("p_a", [128, 1024], F32))
    p_b = S(nc.psum_tensor("p_b", [128, 1024], F32))
    p_c = S(nc.psum_tensor("p_c", [128, 1024], F32))
    p_t1 = S(nc.psum_tensor("p_t1", [128, 1024], BF16))
    p_t2 = S(nc.psum_tensor("p_t2", [128, 1024], BF16))

    sems = {}
    for name in ("lw0", "lw1", "lwx", "ldx", "ldg", "ldg2", "pe", "act",
                 "dve", "pool", "stE", "stO", "scc", "cc", "xld"):
        sems[name] = _Sem(S(nc.semaphore(name)))
    SW0, SW1, SWX, SX, SG, SG2, SP, SA, SV, SL, SSE, SSO, SCC, SC, SXL = (
        sems[n] for n in ("lw0", "lw1", "lwx", "ldx", "ldg", "ldg2", "pe",
                          "act", "dve", "pool", "stE", "stO", "scc", "cc",
                          "xld"))

    Q = {e: [] for e in ENGS}

    def on(eng, fn, waits=(), inc=None, n=1):
        tok = None
        rec_inc = None
        if inc is not None:
            inc.n += n
            tok = (inc, inc.n)
            rec_inc = (inc.h, n)
        Q[eng].append((tuple(w for w in waits if w is not None), fn, rec_inc))
        return tok

    def drain(eng):
        on(eng, lambda e: e.drain())

    def xt_blk(ch, t):
        base = ch * 4096 + t * 256
        return arena[:, base:base + 256].bitcast(BF16)

    def xt_full():
        return arena[:, 0:32768].bitcast(BF16)

    def qt_win(qc, ch):
        off = QT0 + qc * 8192 + ch * 1024
        return arena[:, off:off + 1024].bitcast(BF16)

    def qt_blk(qc, ch, tl):
        off = QT0 + qc * 8192 + ch * 1024 + tl * 256
        return arena[:, off:off + 256].bitcast(BF16)

    def kt_blk(j, ch):
        return s_kt[:, j * 1024 + ch * 128: j * 1024 + (ch + 1) * 128]

    def x2_tile(t):
        if t < 8:
            return arena[:, t * 4096:(t + 1) * 4096].bitcast(F32)
        return arena[:, QT0 + (t - 8) * 4096: QT0 + (t - 7) * 4096].bitcast(F32)

    def tmp_f32():
        return s_tmp[:, :].bitcast(F32)

    sml = lambda i: s_sml[:, i:i + 1]

    on("vector", lambda e: e.memset(s_eps1[:], RMS_EPS))
    on("vector", lambda e: e.memset(s_eps2[:], LN_EPS))
    on("vector", lambda e: e.memset(s_ones[:], 1.0))
    on("vector", lambda e: e.memset(s_ln8[:], LN_0125))
    v3 = s_v[:, :].rearrange("p (g j) -> p g j", j=65)
    tok_vones = on("vector", lambda e: e.memset(v3[:, :, 64:65], 1.0), inc=SV)

    on("sync", lambda e: e.dma_start(out=s_tri[:], in_=d_trimask), inc=SX, n=16)
    on("sync", lambda e: e.dma_start(out=s_id[:], in_=d_ident), inc=SX, n=16)
    stg = s_an[:, :].bitcast(F32)
    stg_v = [stg[:, i * 512:(i + 1) * 512] for i in range(4)]
    for i, dt_ in enumerate((d_cosq, d_sinq, d_cosk, d_sink)):
        src = dt_.rearrange("(n p) j -> p n j", p=128)
        dst = stg_v[i].rearrange("p (n j) -> p n j", j=32)
        on("sync", lambda e, s=src, d=dst: e.dma_start(out=d, in_=s),
           inc=SX, n=16)
    tok_tri = tok_id = (SX, SX.n)
    cs_toks = [(SX, SX.n)] * 4
    on("gpsimd", lambda e: e.dma_start(out=s_wqb[:], in_=d_wqrow.broadcast_to([128, HD])), inc=SG, n=16)
    on("gpsimd", lambda e: e.dma_start(out=s_wkb[:], in_=d_wkrow.broadcast_to([128, HD])), inc=SG, n=16)
    tok_wqb = tok_wkb = (SG, SG.n)
    def wload(dst_sb, src, sem, waits=()):
        src_r = src.rearrange("(c p) n -> p c n", p=128)
        dst_r = dst_sb[:, :].rearrange("p (c n) -> p c n", n=1024)
        return on("gpsimd", lambda e: e.dma_start(out=dst_r, in_=src_r),
                  waits=waits, inc=sem, n=16)
    tok_wq = wload(s_w0, d_wq, SW0)
    tok_wk = wload(s_w1, d_wk, SW1)
    xt_src = d_xt.rearrange("(c p) t -> p c t", p=128)
    xt_dst = xt_full().rearrange("p (c t) -> p c t", t=2048)
    tok_xt = on("gpsimd", lambda e: e.dma_start(out=xt_dst, in_=xt_src), inc=SWX, n=16)

    tbl = [s_tbl[:, i * 1024:(i + 1) * 1024] for i in range(4)]
    tok_tbl = None
    for m, (ci, si_, wb, wtok) in enumerate(((0, 1, s_wqb, tok_wqb), (2, 3, s_wkb, tok_wkb))):
        for tix, fi in ((2 * m, ci), (2 * m + 1, si_)):
            dst3 = tbl[tix].rearrange("p (t j) -> p t j", j=64)
            src3 = stg_v[fi].rearrange("p (t j) -> p t j", j=32)
            for half in range(2):
                wv_ = wb[:, half * 32:(half + 1) * 32].unsqueeze(1).broadcast_to([128, 16, 32])
                dsth = dst3[:, :, half * 32:(half + 1) * 32]
                tok_tbl = on("vector",
                             lambda e, d_=dsth, s_=src3, w_=wv_: e.tensor_tensor(d_, s_, w_, ALU.mult),
                             waits=(cs_toks[fi], wtok), inc=SV)

    tok_sq1 = on("scalar", lambda e: e.activation(s_scr[0:1, 0:64], s_wqb[0:1, :], AF.Square,
                                                  accum_out=s_mh[:, 0:1]),
                 waits=(tok_wqb,), inc=SA)
    tok_sq2 = on("scalar", lambda e: e.activation(s_scr[0:1, 64:128], s_wkb[0:1, :], AF.Square,
                                                  accum_out=s_mh[:, 1:2]),
                 waits=(tok_wkb,), inc=SA)
    tok_t1 = on("vector", lambda e: e.tensor_tensor(s_mh[:, 2:3], s_mh[:, 0:1], s_mh[:, 1:2], ALU.mult),
                waits=(tok_sq1, tok_sq2), inc=SV)
    tok_lnm = on("scalar", lambda e: e.activation(s_mh[:, 3:4], s_mh[:, 2:3], AF.Ln),
                 waits=(tok_t1,), inc=SA)
    drain("scalar")
    tok_mh = on("scalar", lambda e: e.activation(s_mh[:, 2:3], s_mh[:, 3:4], AF.Exp,
                                                 scale=0.5, bias=s_ln8[:]),
                waits=(tok_lnm,), inc=SA)
    drain("scalar")
    tok_nmh0 = on("scalar", lambda e: e.mul(s_mh[:, 3:4], s_mh[:, 2:3], -1.0),
                  waits=(tok_mh,), inc=SA)
    tok_nmhmm = on("tensor", lambda e: e.matmul(p_c[:, 0:1], s_ones[:], s_mh[0:1, 3:4],
                                                start=True, stop=True),
                   waits=(tok_nmh0,), inc=SP)
    tok_nmh = on("scalar", lambda e: e.activation(s_nmh[:], p_c[:, 0:1], AF.Copy),
                 waits=(tok_nmhmm,), inc=SA)

    psum_slots = (p_a, p_b, p_c)
    slot_free = [None, None, tok_nmh]
    tp_banks = (p_t1, p_t2)
    bank_free = [None, None]
    bank_idx = [0]

    def tp_group(pairs, ev_eng, first_waits=(), ev_first_waits=()):
        b = bank_idx[0] % 2
        bank_idx[0] += 1
        pt = tp_banks[b]
        ev_sem = SA if ev_eng == "scalar" else SV
        tps = []
        for i, (src, dst) in enumerate(pairs):
            w = [bank_free[b]] + list(first_waits) if i == 0 else []
            tp = on("tensor",
                    lambda e, o_=pt[:, i * 128:(i + 1) * 128], s_=src:
                    e.transpose(o_, s_, s_id[:]),
                    waits=w, inc=SP if i == len(pairs) - 1 else None)
            tps.append((tp, pt[:, i * 128:(i + 1) * 128], dst))
        last_tp = tps[-1][0]
        last_ev = None
        for i, (_, slot, dst) in enumerate(tps):
            w = [last_tp] + list(ev_first_waits) if i == 0 else []
            if ev_eng == "scalar":
                last_ev = on("scalar",
                             lambda e, d_=dst, s_=slot: e.activation(d_, s_, AF.Copy),
                             waits=w, inc=ev_sem)
            else:
                last_ev = on("vector",
                             lambda e, d_=dst, s_=slot: e.tensor_copy(d_, s_),
                             waits=w, inc=ev_sem)
        bank_free[b] = last_ev
        return last_tp, last_ev

    tok_vevac_last = None
    tok_rq_free = None
    tok_qh_free = None
    tok_rstd_free = None
    tok_scr_free = None
    tok_tmp_free = None
    last_proj_tok = {}
    pending = [None]

    pend_ev = [None]

    def emit_qk_tp():
        nonlocal tok_rq_free
        if pending[0] is None:
            return
        tok_rq, mat, t = pending[0]
        pending[0] = None
        b = bank_idx[0] % 2
        bank_idx[0] += 1
        pt = tp_banks[b]
        dsts = []
        last_tp = None
        for ch in range(NCH):
            if mat == "q":
                dst = qt_blk(t // 4, ch, t % 4)
            else:
                dst = kt_blk(t, ch)
            w = [bank_free[b], tok_rq, tok_id] if ch == 0 else []
            last_tp = on("tensor",
                         lambda e, o_=pt[:, ch * 128:(ch + 1) * 128],
                                s_=s_rq[:, ch * 128:(ch + 1) * 128]:
                         e.transpose(o_, s_, s_id[:]),
                         waits=w, inc=SP if ch == NCH - 1 else None)
            dsts.append(dst)
        pend_ev[0] = (b, pt, dsts, last_tp)
        tok_rq_free = last_tp

    def emit_qk_ev():
        if pend_ev[0] is None:
            return
        b, pt, dsts, last_tp = pend_ev[0]
        pend_ev[0] = None
        last_ev = None
        for i, dst in enumerate(dsts):
            w = [last_tp] if i == 0 else []
            last_ev = on("scalar",
                         lambda e, d_=dst, s_=pt[:, i * 128:(i + 1) * 128]:
                         e.activation(d_, s_, AF.Copy),
                         waits=w, inc=SA)
        bank_free[b] = last_ev

    si = 0
    for m_i, mat in enumerate(("q", "k", "v")):
        wbuf = s_w0 if mat in ("q", "v") else s_w1
        wtok = {"q": tok_wq, "k": tok_wk}.get(mat)
        if mat == "v":
            wtok = wload(s_w0, d_wv, SW0, waits=(last_proj_tok["q"],))
        tbl_c = tbl[0] if mat == "q" else tbl[2]
        tbl_s = tbl[1] if mat == "q" else tbl[3]
        for t in range(NT):
            slot = si % 3
            si += 1
            pp = psum_slots[slot]
            mm_tok = None
            for ch in range(NCH):
                for half in range(2):
                    w_v = wbuf[:, ch * 1024 + half * 512: ch * 1024 + (half + 1) * 512]
                    waits = []
                    if ch == 0 and half == 0:
                        waits = [slot_free[slot], wtok, tok_xt]
                    mm_tok = on("tensor",
                                lambda e, p_=pp[:, half * 512:(half + 1) * 512],
                                       l_=xt_blk(ch, t), r_=w_v, s0=(ch == 0), s1=(ch == NCH - 1):
                                e.matmul(p_, l_, r_, start=s0, stop=s1),
                                waits=waits,
                                inc=SP if (ch == NCH - 1 and half == 1) else None)
            last_proj_tok[mat] = mm_tok
            emit_qk_tp()
            pview = pp[:, :].rearrange("p (h j) -> p h j", j=64)

            if mat == "v":
                emit_qk_ev()
                dst = s_v[:, t * (H * 65):(t + 1) * (H * 65)].rearrange(
                    "p (h j) -> p h j", j=65)[:, :, 0:64]
                tok_vevac_last = on("vector",
                                    lambda e, d_=dst, s_=pview: e.tensor_copy(d_, s_),
                                    waits=(mm_tok, tok_vones), inc=SV)
                slot_free[slot] = tok_vevac_last
                continue

            sqv = tmp_f32()
            tok_sq = on("scalar", lambda e, s_=pp[:, :], d_=sqv: e.activation(d_, s_, AF.Square),
                        waits=(mm_tok, tok_tmp_free), inc=SA)
            tok_ms = on("vector",
                        lambda e: e.tensor_reduce(
                            s_ms[:, :].rearrange("p (h o) -> p h o", o=1),
                            tmp_f32().rearrange("p (h j) -> p h j", j=64), AX, ALU.add),
                        waits=(tok_sq, tok_rstd_free), inc=SV)
            tok_ln = on("scalar", lambda e: e.activation(s_rstd[:], s_ms[:], AF.Ln,
                                                         scale=1.0 / 64.0, bias=s_eps1[:]),
                        waits=(tok_ms,), inc=SA)
            drain("scalar")
            tok_rstd = on("scalar", lambda e: e.activation(s_rstd2[:], s_rstd[:], AF.Exp, scale=-0.5),
                          waits=(tok_ln,), inc=SA)
            emit_qk_ev()
            rb = s_rstd2[:, :].rearrange("p (h o) -> p h o", o=1).broadcast_to([128, 16, 64])
            qh3 = s_xqt[:, :].rearrange("p (h j) -> p h j", j=64)
            tok_qh = on("vector",
                        lambda e, o_=qh3, i_=pview, b_=rb: e.tensor_tensor(o_, i_, b_, ALU.mult),
                        waits=(tok_rstd, tok_qh_free), inc=SV)
            tok_rstd_free = tok_qh
            slot_free[slot] = tok_qh
            qq = s_xqt[:, :].rearrange("p (h two j) -> p h two j", two=2, j=32)
            q1v, q2v = qq[:, :, 0, :], qq[:, :, 1, :]
            ct = tbl_c[:, t * 64:(t + 1) * 64]
            st_ = tbl_s[:, t * 64:(t + 1) * 64]
            c1 = ct[:, 0:32].unsqueeze(1).broadcast_to([128, 16, 32])
            c2 = ct[:, 32:64].unsqueeze(1).broadcast_to([128, 16, 32])
            s1v = st_[:, 0:32].unsqueeze(1).broadcast_to([128, 16, 32])
            s2v = st_[:, 32:64].unsqueeze(1).broadcast_to([128, 16, 32])
            tmp3 = s_tmp[:, :].rearrange("p (k j) -> p k j", j=512)
            ta = tmp3[:, 0, :].rearrange("p (h j) -> p h j", j=32)
            tb = tmp3[:, 1, :].rearrange("p (h j) -> p h j", j=32)
            tc = tmp3[:, 2, :].rearrange("p (h j) -> p h j", j=32)
            td = tmp3[:, 3, :].rearrange("p (h j) -> p h j", j=32)
            drain("vector")
            on("vector", lambda e, o_=ta, a_=q1v, b_=c1: e.tensor_tensor(o_, a_, b_, ALU.mult),
               waits=(tok_tbl,))
            on("vector", lambda e, o_=tb, a_=q2v, b_=s2v: e.tensor_tensor(o_, a_, b_, ALU.mult))
            on("vector", lambda e, o_=tc, a_=q2v, b_=c2: e.tensor_tensor(o_, a_, b_, ALU.mult))
            on("vector", lambda e, o_=td, a_=q1v, b_=s1v: e.tensor_tensor(o_, a_, b_, ALU.mult))
            drain("vector")
            rq3 = s_rq[:, :].rearrange("p (h two j) -> p h two j", two=2, j=32)
            on("vector", lambda e, o_=rq3[:, :, 0, :], a_=ta, b_=tb: e.tensor_tensor(o_, a_, b_, ALU.subtract),
               waits=(tok_rq_free,))
            tok_rq = on("vector", lambda e, o_=rq3[:, :, 1, :], a_=tc, b_=td: e.tensor_tensor(o_, a_, b_, ALU.add),
                        inc=SV)
            tok_qh_free = tok_rq
            tok_tmp_free = tok_rq
            pending[0] = (tok_rq, mat, t)

    tok_qk_ready = (SA, SA.n)
    marks = {1: ({e: len(Q[e]) for e in ENGS}, (SA.n, SV.n, SP.n))}

    tok_wo = wload(s_w1, d_wo, SW1, waits=(last_proj_tok["k"],))
    tok_wff = wload(s_w0, d_wfft, SW0, waits=(last_proj_tok["v"],))

    ph_bufs = (s_pt[:, 0:1024], s_pt[:, 1024:2048], s_xqt[:, :], s_rq[:, :])
    pth_free4 = [None] * 4
    pc_free = [slot_free[2], slot_free[2]]
    psum_free = {"a": slot_free[0], "b": slot_free[1]}
    tmp_free_att = tok_tmp_free
    tld_gate = [tok_tmp_free, tok_tmp_free]
    gidx = [0]
    an_free = None
    tok_r_evac = {}
    tok_x2 = {}
    tok_amax = {}
    tok_xdma = None
    tok_xti_free = None
    att_gate = [tok_qk_ready, tok_vevac_last, tok_nmh, tok_tri]
    first_s_done = False
    wo_tok_prev = None
    tok_sqr1_last = None

    for c in range(NQC):
        an_toks = {}
        at_last_tp = [None]
        at_first = [True]

        def emit_at_tp(m, c=c, an_toks=an_toks, at_last_tp=at_last_tp,
                       at_first=at_first):
            pairs = []
            for tl in range(4):
                src = s_an[:, tl * 1024 + m * 128: tl * 1024 + (m + 1) * 128]
                dst = s_at[:, m * 512 + tl * 128: m * 512 + (tl + 1) * 128]
                pairs.append((src, dst))
            evw = ()
            if at_first[0]:
                evw = (wo_tok_prev,)
                at_first[0] = False
            tp, _ = tp_group(pairs, "vector",
                             first_waits=(an_toks[2 * m + 1],),
                             ev_first_waits=evw)
            at_last_tp[0] = tp

        jmaxc = 4 * (c + 1)
        ngrp = jmaxc // 2
        last_g = ngrp - 1
        av_last_of = {}

        def emit_an(h, c=c, an_toks=an_toks):
            pcb = (h % 2) * 512
            pcv = p_c[:, pcb:pcb + 260].rearrange("p (t j) -> p t j", j=65)
            on("vector",
               lambda e, v_=pcv: e.tensor_copy(
                   s_den[:, :].rearrange("p (t o) -> p t o", o=1), v_[:, :, 64:65]),
               waits=(av_last_of[h], an_free), inc=SV)
            drain("vector")
            on("vector", lambda e: e.reciprocal(s_rec[:], s_den[:]), inc=SV)
            drain("vector")
            an4 = s_an[:, :].rearrange("p (t hh j) -> p t hh j", hh=16, j=64)
            recb = s_rec[:, :].rearrange("p (t o) -> p t o", o=1).broadcast_to([128, 4, 64])
            tok_an = on("vector",
                        lambda e, o_=an4[:, :, h, :], i_=pcv, b_=recb:
                        e.tensor_tensor(o_, i_[:, :, 0:64], b_, ALU.mult),
                        inc=SV)
            pc_free[h % 2] = tok_an
            an_toks[h] = tok_an

        def emit_av(h, g, gi4, etok, mtoks, pth, c=c):
            pcb = (h % 2) * 512
            last = None
            for jj in range(2):
                j = 2 * g + jj
                rhs_v = s_v[:, j * (H * 65) + h * 65: j * (H * 65) + (h + 1) * 65]
                for tl in range(max(0, j - 4 * c), 4):
                    diag = (j - 4 * c == tl)
                    if diag:
                        lhs = s_tmp[:, tl * 128:(tl + 1) * 128]
                    else:
                        lhs = pth[:, jj * 512 + tl * 128: jj * 512 + (tl + 1) * 128]
                    waits = [etok]
                    if diag:
                        waits.append(mtoks[j])
                    s0 = (g == 0 and jj == 0 and tl == 0)
                    if s0:
                        waits.append(pc_free[h % 2])
                    last = on("tensor",
                              lambda e, p_=p_c[:, pcb + tl * 65: pcb + (tl + 1) * 65],
                                     l_=lhs, r_=rhs_v, s0=s0, s1=(j == 4 * c + tl):
                              e.matmul(p_, l_, r_, start=s0, stop=s1,
                                       skip_group_check=True),
                              waits=waits,
                              inc=SP if (jj == 1 and tl == 3) else None)
            pth_free4[gi4] = last
            av_last_of[h] = last
            if g >= 2 * c:
                tld_gate[g - 2 * c] = last
            if g == last_g:
                emit_an(h)

        pend_av = []
        for h in range(H):
            if h % 2 == 1 and h >= 3:
                emit_at_tp((h - 3) // 2)
            ch = h // 2
            hb = (h % 2) * 64
            for g in range(ngrp):
                gi = gidx[0]
                gidx[0] += 1
                par = gi % 2
                ps = p_a if par == 0 else p_b
                pth = ph_bufs[gi % 4]
                qs = [max(0, 2 * g + jj - 4 * c) * 128 for jj in range(2)]
                stok = None
                for jj in range(2):
                    j = 2 * g + jj
                    lhs = kt_blk(j, ch)[hb:hb + 64, :]
                    rhs_q = qt_win(c, ch)[hb:hb + 64, qs[jj]:512]
                    waits = []
                    if jj == 0:
                        waits = [psum_free["a" if par == 0 else "b"]]
                        if not first_s_done:
                            waits += att_gate
                            first_s_done = True
                    stok = on("tensor",
                              lambda e, p_=ps[:, jj * 512 + qs[jj]:(jj + 1) * 512],
                                     l_=lhs, r_=rhs_q:
                              e.matmul(p_, l_, r_, start=True, stop=True),
                              waits=waits, inc=SP if jj == 1 else None)
                qs0 = qs[0]
                pv = ps[:, :].rearrange("p (two q) -> p two q", q=512)[:, :, qs0:512]
                ev_ = pth[:, :].rearrange("p (two q) -> p two q", q=512)[:, :, qs0:512]
                etok = on("scalar",
                          lambda e, o_=ev_, i_=pv: e.activation(o_, i_, AF.Exp, bias=s_nmh[:]),
                          waits=(stok, pth_free4[gi % 4]), inc=SA)
                psum_free["a" if par == 0 else "b"] = etok
                if len(pend_av) == 2:
                    emit_av(*pend_av.pop(0))
                mtoks = {}
                for jj in range(2):
                    j = 2 * g + jj
                    if j >= 4 * c:
                        tld = j - 4 * c
                        blk = pth[:, jj * 512 + tld * 128: jj * 512 + (tld + 1) * 128]
                        mdst = s_tmp[:, tld * 128:(tld + 1) * 128]
                        mtoks[j] = on("gpsimd",
                                      lambda e, o_=mdst, a_=blk, b_=s_tri[:]:
                                      e.tensor_tensor(o_, a_, b_, ALU.mult),
                                      waits=(etok, tld_gate[g - 2 * c]), inc=SL)
                pend_av.append((h, g, gi % 4, etok, mtoks, pth))
        while pend_av:
            emit_av(*pend_av.pop(0))
        tmp_free_att = av_last_of[H - 1]

        emit_at_tp(7)
        an_free = at_last_tp[0]
        tok_at_ready = (SV, SV.n)

        for tl in range(4):
            t = c * 4 + tl
            par = "a" if tl % 2 == 0 else "b"
            pp = p_a if par == "a" else p_b
            tok_xdma = on("sync",
                          lambda e, t_=t: e.dma_start(out=s_xti[:], in_=d_xin[t_ * 128:(t_ + 1) * 128, :]),
                          waits=(tok_xti_free,), inc=SXL, n=16)
            wo_tok = None
            for chh in range(NCH):
                for half in range(2):
                    lhs = s_at[:, chh * 512 + tl * 128: chh * 512 + (tl + 1) * 128]
                    rhs = s_w1[:, chh * 1024 + half * 512: chh * 1024 + (half + 1) * 512]
                    waits = []
                    if chh == 0 and half == 0:
                        waits = [tok_at_ready, psum_free[par]]
                        if c == 0 and tl == 0:
                            waits.append(tok_wo)
                    wo_tok = on("tensor",
                                lambda e, p_=pp[:, half * 512:(half + 1) * 512], l_=lhs, r_=rhs,
                                       s0=(chh == 0), s1=(chh == NCH - 1):
                                e.matmul(p_, l_, r_, start=s0, stop=s1),
                                waits=waits,
                                inc=SP if (chh == NCH - 1 and half == 1) else None)
            tok_r = on("vector",
                       lambda e, p_=pp: e.scalar_tensor_tensor(
                           s_scr[:], s_xti[:], 1.0, p_[:, :], ALU.mult, ALU.add,
                           accum_out=sml(0)),
                       waits=(wo_tok, tok_xdma, tok_scr_free), inc=SV)
            psum_free[par] = tok_r
            tok_r_evac[t] = tok_r
            drain("vector")
            tok_sqr = on("scalar",
                         lambda e: e.activation(s_xti[:], s_scr[:], AF.Square, accum_out=sml(1)),
                         waits=(tok_r,), inc=SA)
            tok_xti_free = tok_sqr
            tok_sqr1_last = tok_sqr
            tok_sq1t = on("scalar",
                          lambda e: e.activation(sml(2), sml(0), AF.Square, scale=1.0 / 32.0),
                          waits=(tok_r,), inc=SA)
            tok_t2 = on("vector",
                        lambda e: e.tensor_tensor(sml(3), sml(1), sml(2), ALU.subtract),
                        waits=(tok_sqr, tok_sq1t), inc=SV)
            tok_lnv = on("scalar",
                         lambda e: e.activation(sml(6), sml(3), AF.Ln, scale=1.0 / 1024.0,
                                                bias=s_eps2[:]),
                         waits=(tok_t2,), inc=SA)
            drain("scalar")
            tok_rstd2 = on("scalar",
                           lambda e: e.activation(sml(4), sml(6), AF.Exp, scale=-0.5),
                           waits=(tok_lnv,), inc=SA)
            tok_nmr = on("vector",
                         lambda e: e.scalar_tensor_tensor(sml(5), sml(0), -1.0 / 1024.0, sml(4),
                                                          ALU.mult, ALU.mult),
                         waits=(tok_rstd2,), inc=SV)
            drain("vector")
            tok_x2[t] = on("vector",
                           lambda e, o_=x2_tile(t): e.tensor_scalar(
                               o_, s_scr[:], sml(4), sml(5), ALU.mult, ALU.add),
                           inc=SV)
            tok_scr_free = tok_x2[t]
            tok_amax[t] = on("vector",
                             lambda e, o_=s_amax[:, t:t + 1], i_=x2_tile(t):
                             e.tensor_reduce(o_, i_, AX, ALU.max,
                                             apply_absolute_value=True),
                             waits=(tok_x2[t],), inc=SV)
        wo_tok_prev = wo_tok

    marks[2] = ({e: len(Q[e]) for e in ENGS}, (SA.n, SV.n, SP.n))

    ws_b = s_at[:, 0:2048].bitcast(F32)
    b_b = s_at[:, 2048:4096].bitcast(F32)
    on("gpsimd", lambda e: e.dma_start(out=ws_b, in_=d_wsrow.broadcast_to([128, D])),
       waits=(wo_tok,), inc=SG2, n=16)
    on("gpsimd", lambda e: e.dma_start(out=b_b, in_=d_brow.broadcast_to([128, D])),
       waits=(wo_tok,), inc=SG2, n=16)
    tok_wsb = tok_bb = (SG2, SG2.n)

    tok_amc = on("vector", lambda e: e.tensor_reduce(sml(7), s_amax[:], AX, ALU.max),
                 waits=(tok_amax[NT - 1],), inc=SV)
    tok_gl = on("gpsimd",
                lambda e: e.tensor_reduce(s_xs[0:1, 0:1], sml(7),
                                          mybir.AxisListType.XYZWC, ALU.max),
                waits=(tok_amc,), inc=SL)
    if _SIM_LOCAL_CC:
        tok_cdn = on("gpsimd", lambda e: e.tensor_copy(s_am128[0:1, 0:1], s_xs[0:1, 0:1]),
                     waits=(tok_gl,), inc=SL)
    else:
        tok_cup = on("gpsimd", lambda e: e.dma_start(out=d_ccin.ap(), in_=s_xs[0:1, 0:1]),
                     waits=(tok_gl,), inc=SCC, n=16)
        tok_cc = on("gpsimd", lambda e: e.collective_compute(
            "AllReduce", ALU.max, replica_groups=[list(range(8))],
            ins=[d_ccin.ap()], outs=[d_ccout.ap()]),
            waits=(tok_cup,), inc=SC, n=1)
        tok_cdn = on("gpsimd", lambda e: e.dma_start(out=s_am128[0:1, 0:1], in_=d_ccout.ap()),
                     waits=(tok_cc,), inc=SCC, n=16)
    tok_xsv = on("vector", lambda e: e.tensor_scalar(s_xs[:, 0:1], s_am128[0:1, 0:1],
                                                     float(np.float32(1.0) / np.float32(127.0)),
                                                     QEPS, ALU.mult, ALU.max),
                 waits=(tok_cdn,), inc=SV)
    drain("vector")
    tok_rxs = on("vector", lambda e: e.reciprocal(s_xs[:, 1:2], s_xs[:, 0:1]),
                 waits=(tok_xsv,), inc=SV)
    tok_bc1 = on("tensor", lambda e: e.matmul(p_c[:, 2:3], s_ones[:], s_xs[0:1, 0:1],
                                              start=True, stop=True),
                 waits=(tok_rxs, pc_free[0]), inc=SP)
    tok_bc2 = on("tensor", lambda e: e.matmul(p_c[:, 3:4], s_ones[:], s_xs[0:1, 1:2],
                                              start=True, stop=True),
                 waits=(tok_rxs, pc_free[1]), inc=SP)
    tok_xsb = on("scalar", lambda e: e.activation(s_xsb[:, 0:1], p_c[:, 2:3], AF.Copy),
                 waits=(tok_bc1,), inc=SA)
    tok_rxsb = on("scalar", lambda e: e.activation(s_xsb[:, 1:2], p_c[:, 3:4], AF.Copy),
                  waits=(tok_bc2,), inc=SA)
    ts_b = s_an[:, 0:2048].bitcast(F32)
    tok_tsb = on("vector", lambda e: e.tensor_scalar(ts_b, ws_b, s_xsb[:, 0:1], None, ALU.mult),
                 waits=(tok_wsb, tok_xsb, an_free), inc=SV)

    xq_all = s_kt[:, :]
    vf32 = s_v[:, :].bitcast(F32)
    scrF = (vf32[:, 0:1024], vf32[:, 1024:2048])
    tmpF = (vf32[:, 2048:3072], vf32[:, 3072:4096])
    x2bF = (vf32[:, 4096:5120], vf32[:, 5120:6144])
    xtiF = (vf32[:, 6144:7168], vf32[:, 7168:8192])
    smlF = lambda t, i: s_sml[:, (t % 2) * 8 + i:(t % 2) * 8 + i + 1]

    tok_q2 = {}
    tok_q3 = {}
    tok_r2d = {}
    tok_sqr2 = {}

    def emit_quant(t):
        p = t % 2
        tok_q1 = on("vector",
                    lambda e, i_=x2_tile(t), o_=scrF[p]: e.tensor_scalar(
                        o_, i_, s_xsb[:, 1:2], MAGIC, ALU.mult, ALU.add),
                    waits=(tok_rxsb, tok_x2[t],
                           tmp_free_att if t < 2 else tok_r2d[t - 2]), inc=SV)
        tok_q2[t] = on("scalar",
                       lambda e, i_=scrF[p], o_=tmpF[p]: e.activation(
                           o_, i_, AF.Copy, bias=-MAGIC),
                       waits=(tok_q1, tok_q3.get(t - 2)), inc=SA)
        tok_q3[t] = on("vector",
                       lambda e, i_=tmpF[p], o_=xq_all[:, t * 1024:(t + 1) * 1024]:
                       e.tensor_scalar(o_, i_, 127.0, -128.0, ALU.min, ALU.max),
                       waits=(tok_q2[t],), inc=SV)

    emit_quant(0)
    emit_quant(1)

    marks[3] = ({e: len(Q[e]) for e in ENGS}, (SA.n, SV.n, SP.n))

    xqt_bufs = (s_xqt, s_rq)
    tok_xqt_free = [None, None]
    tok_store = {}
    tok_out_free = [None, None]
    tok_xqt_t = {}

    def emit_xq_tp(t):
        xb = xqt_bufs[t % 2]
        pairs = []
        for chh in range(NCH):
            src = xq_all[:, t * 1024 + chh * 128: t * 1024 + (chh + 1) * 128]
            dst = xb[:, chh * 128:(chh + 1) * 128]
            pairs.append((src, dst))
        tp_group(pairs, "scalar", first_waits=(tok_q3[t],),
                 ev_first_waits=(tok_xqt_free[t % 2],)
                 if t >= 2 else (tmp_free_att,))
        tok_xqt_t[t] = (SA, SA.n)

    emit_xq_tp(0)
    for t in range(NT):
        p = t % 2
        par = "a" if p == 0 else "b"
        pp = p_a if par == "a" else p_b
        xb = xqt_bufs[p]
        tok_xqt = tok_xqt_t[t]
        tok_x2b = on("gpsimd",
                     lambda e, i_=x2_tile(t), o_=x2bF[p]: e.tensor_tensor(
                         o_, i_, b_b, ALU.add),
                     waits=(tok_bb, tok_x2[t],
                            tmp_free_att if t < 2 else tok_sqr2[t - 2]), inc=SL)
        mm_tok = None
        for chh in range(NCH):
            for half in range(2):
                rhs = s_w0[:, chh * 1024 + half * 512: chh * 1024 + (half + 1) * 512]
                waits = []
                if chh == 0 and half == 0:
                    waits = [tok_xqt, psum_free[par]]
                    if t == 0:
                        waits.append(tok_wff)
                mm_tok = on("tensor",
                            lambda e, p_=pp[:, half * 512:(half + 1) * 512],
                                   l_=xb[:, chh * 128:(chh + 1) * 128], r_=rhs,
                                   s0=(chh == 0), s1=(chh == NCH - 1):
                            e.matmul(p_, l_, r_, start=s0, stop=s1),
                            waits=waits,
                            inc=SP if (chh == NCH - 1 and half == 1) else None)
        tok_xqt_free[p] = mm_tok
        if t + 1 < NT:
            emit_xq_tp(t + 1)
        tok_o1 = on("vector",
                    lambda e, p_=pp, o_=scrF[p]: e.tensor_tensor(o_, p_[:, :], ts_b, ALU.mult),
                    waits=(mm_tok, tok_tsb), inc=SV)
        psum_free[par] = tok_o1
        drain("vector")
        tok_r2d[t] = on("vector",
                        lambda e, a_=scrF[p], b_=x2bF[p], o_=xtiF[p]:
                        e.scalar_tensor_tensor(o_, a_, 1.0, b_, ALU.mult, ALU.add,
                                               accum_out=smlF(t, 0)),
                        waits=(tok_x2b, tok_sqr2.get(t - 2)), inc=SV)
        tok_r2 = tok_r2d[t]
        drain("vector")
        tok_sqr2[t] = on("scalar",
                         lambda e, i_=xtiF[p], o_=x2bF[p]: e.activation(
                             o_, i_, AF.Square, accum_out=smlF(t, 1)),
                         waits=(tok_r2,), inc=SA)
        tok_sq1t = on("scalar",
                      lambda e: e.activation(smlF(t, 2), smlF(t, 0), AF.Square, scale=1.0 / 32.0),
                      waits=(tok_r2,), inc=SA)
        tok_t2 = on("vector",
                    lambda e: e.tensor_tensor(smlF(t, 3), smlF(t, 1), smlF(t, 2), ALU.subtract),
                    waits=(tok_sqr2[t], tok_sq1t), inc=SV)
        tok_lnv = on("scalar",
                     lambda e: e.activation(smlF(t, 6), smlF(t, 3), AF.Ln, scale=1.0 / 1024.0,
                                            bias=s_eps2[:]),
                     waits=(tok_t2,), inc=SA)
        drain("scalar")
        tok_rstd2 = on("scalar",
                       lambda e: e.activation(smlF(t, 4), smlF(t, 6), AF.Exp, scale=-0.5),
                       waits=(tok_lnv,), inc=SA)
        tok_nmr = on("vector",
                     lambda e: e.scalar_tensor_tensor(smlF(t, 5), smlF(t, 0), -1.0 / 1024.0,
                                                      smlF(t, 4), ALU.mult, ALU.mult),
                     waits=(tok_rstd2,), inc=SV)
        drain("vector")
        outv = s_tbl[:, p * 2048:p * 2048 + 2048].bitcast(F32)
        tok_out = on("vector",
                     lambda e, o_=outv, i_=xtiF[p]: e.tensor_scalar(
                         o_, i_, smlF(t, 4), smlF(t, 5), ALU.mult, ALU.add),
                     waits=(tok_out_free[p],), inc=SV)
        tok_store[t] = on("gpsimd",
                          lambda e, t_=t, o_=outv: e.dma_start(out=d_y[t_ * 128:(t_ + 1) * 128, :], in_=o_),
                          waits=(tok_out,), inc=SSE if p == 0 else SSO, n=16)
        tok_out_free[p] = tok_store[t]
        if t + 2 < NT:
            emit_quant(t + 2)

    on("gpsimd", lambda e: e.nop(), waits=((SSE, SSE.n), (SSO, SSO.n)))

    if _TRUNC:
        cut, (na, nv, np_) = marks[_TRUNC]
        for e in ENGS:
            Q[e] = Q[e][:cut[e]]
        Q["gpsimd"].append((((SA, na), (SV, nv), (SP, np_)),
                            lambda e: e.nop(), None))

    with nc.Block() as block:
        def run(eng_name):
            waited = {}

            def body(eng):
                for waits, fn, inc in Q[eng_name]:
                    for (sm, v) in waits:
                        if v > waited.get(id(sm), 0):
                            eng.wait_ge(sm.h, v)
                            waited[id(sm)] = v
                    ins = fn(eng)
                    if inc is not None:
                        ins.then_inc(inc[0], inc[1])
            return body

        block.sync(run("sync"))
        block.scalar(run("scalar"))
        block.vector(run("vector"))
        block.tensor(run("tensor"))
        block.gpsimd(run("gpsimd"))

    es.close()
    return nc


_NC_CACHE = None


def _get_nc():
    global _NC_CACHE
    if _NC_CACHE is None:
        _NC_CACHE = _build_program()
    return _NC_CACHE


def _rope_tables():
    t = np.arange(T, dtype=np.float32)
    invf = (1.0 / (10000.0 ** (np.arange(0, HD, 2, dtype=np.float64) / HD))).astype(np.float32)
    freqs = t[:, None] * invf[None, :]
    cos = np.cos(freqs.astype(np.float64)).astype(np.float32)
    sin = np.sin(freqs.astype(np.float64)).astype(np.float32)
    cosq = cos * np.float32(0.125)
    sinq = sin * np.float32(0.125)
    return cosq, sinq, cos, sin


def kernel(L0_attn_x, L0_attn_Wq, L0_attn_Wk, L0_attn_Wv, L0_attn_Wo,
           L0_attn_q_norm_w, L0_attn_k_norm_w,
           L0_ffn_weight_int8, L0_ffn_weight_scale, L0_ffn_bias):
    x = np.ascontiguousarray(np.asarray(L0_attn_x, dtype=np.float32))
    wq = np.ascontiguousarray(np.asarray(L0_attn_Wq, dtype=np.float32))
    wk = np.ascontiguousarray(np.asarray(L0_attn_Wk, dtype=np.float32))
    wv = np.ascontiguousarray(np.asarray(L0_attn_Wv, dtype=np.float32))
    wo = np.ascontiguousarray(np.asarray(L0_attn_Wo, dtype=np.float32))
    wqn = np.asarray(L0_attn_q_norm_w, dtype=np.float32).reshape(1, HD)
    wkn = np.asarray(L0_attn_k_norm_w, dtype=np.float32).reshape(1, HD)
    w8 = np.ascontiguousarray(np.asarray(L0_ffn_weight_int8).T)
    wsr = np.ascontiguousarray(np.asarray(L0_ffn_weight_scale, dtype=np.float32).reshape(1, D))
    br = np.ascontiguousarray(np.asarray(L0_ffn_bias, dtype=np.float32).reshape(1, D))
    cosq, sinq, cosk, sink = _rope_tables()
    tri = np.triu(np.ones((128, 128), dtype=np.float32)).astype(ml_dtypes.bfloat16)
    ident = np.eye(128, dtype=np.float32).astype(ml_dtypes.bfloat16)

    nc = _get_nc()
    in_maps = []
    for c in range(8):
        xc = np.ascontiguousarray(x[c])
        in_maps.append({
            "xin": xc,
            "xtin": np.ascontiguousarray(xc.T),
            "wq": wq, "wk": wk, "wv": wv, "wo": wo,
            "wfft": w8, "wsrow": wsr, "brow": br,
            "wqrow": wqn, "wkrow": wkn,
            "cosq": cosq, "sinq": sinq, "cosk": cosk, "sink": sink,
            "trimask": tri, "identb": ident,
        })
    res = run_bass_kernel_spmd(nc, in_maps, core_ids=list(range(8)))
    out = np.stack([np.asarray(res.results[c]["y"]) for c in range(8)], axis=0)
    return out
